# version 19
# baseline (speedup 1.0000x reference)
import numpy as np
import ml_dtypes
import concourse.bass as bass
import concourse.mybir as mybir
from concourse.bass import IndirectOffsetOnAxis
from concourse.bass_utils import run_bass_kernel_spmd

F32 = mybir.dt.float32
BF16 = mybir.dt.bfloat16
FP8 = mybir.dt.float8e4
U32 = mybir.dt.uint32
I32 = mybir.dt.int32
ALU = mybir.AluOpType
AF = mybir.ActivationFunctionType
AX = mybir.AxisListType

D = 2048
S = 4096
B = 2
T = 1024
NC = 8
SC = float(128.0 ** -0.25)
EPS = 1e-5
NEG = -30.0
C24 = float(np.exp(24.0))
GRPS = [list(range(8))]
SH = 6144


class Prog:

    def __init__(self, nc):
        self.nc = nc
        self.ops = {"sync": [], "gpsimd": [], "vector": [], "scalar": [],
                    "tensor": []}
        self.sem = {}
        self.cnt = {}

    def mksem(self, name):
        self.sem[name] = self.nc.alloc_semaphore(name)
        self.cnt[name] = 0

    def emit(self, eng, fn):
        self.ops[eng].append(fn)

    def inc(self, eng, sem, n, fn):
        self.cnt[sem] += n
        h = self.sem[sem]
        self.ops[eng].append(lambda e, fn=fn, h=h, n=n: fn(e).then_inc(h, n))

    def dma(self, eng, sem, fn):
        self.inc(eng, sem, 16, fn)

    def wait(self, eng, sem, upto=None):
        v = self.cnt[sem] if upto is None else upto
        if v <= 0:
            return
        h = self.sem[sem]
        self.ops[eng].append(lambda e, h=h, v=v: e.wait_ge(h, v))

    def val(self, sem):
        return self.cnt[sem]

    def fence(self, name, fn):
        self.mksem(name)
        for _ in range(8):
            self.dma("gpsimd", name, fn)


def build(nc):
    xin = nc.dram_tensor("xin", [T, D], F32, kind="ExternalInput").ap()
    wqT = nc.dram_tensor("wqT", [D, D], F32, kind="ExternalInput").ap()
    wkT = nc.dram_tensor("wkT", [D, D], F32, kind="ExternalInput").ap()
    wvT = nc.dram_tensor("wvT", [D, D], F32, kind="ExternalInput").ap()
    woT = nc.dram_tensor("woT", [D, D], F32, kind="ExternalInput").ap()
    wfT = nc.dram_tensor("wfT", [D, D], FP8, kind="ExternalInput").ap()
    bqi = nc.dram_tensor("bqi", [128, 16], F32, kind="ExternalInput").ap()
    bki = nc.dram_tensor("bki", [128, 16], F32, kind="ExternalInput").ap()
    bvi = nc.dram_tensor("bvi", [1, D], F32, kind="ExternalInput").ap()
    boi = nc.dram_tensor("boi", [1, D], F32, kind="ExternalInput").ap()
    wsc = nc.dram_tensor("wsc", [1, 1], F32, kind="ExternalInput").ap()
    isc = nc.dram_tensor("isc", [1, 1], BF16, kind="ExternalInput").ap()
    id_in = nc.dram_tensor("id_in", [128, 128], BF16, kind="ExternalInput").ap()
    onc_in = nc.dram_tensor("onc_in", [128, 1], BF16, kind="ExternalInput").ap()
    onr_in = nc.dram_tensor("onr_in", [1, 128], BF16, kind="ExternalInput").ap()
    onrf_in = nc.dram_tensor("onrf_in", [1, 128], F32, kind="ExternalInput").ap()
    mask_in = nc.dram_tensor("mask_in", [128, 128], F32, kind="ExternalInput").ap()
    idxt = nc.dram_tensor("idxt", [128, 64], I32, kind="ExternalInput").ap()
    yout = nc.dram_tensor("yout", [T, D], F32, kind="ExternalOutput").ap()

    cc1i = nc.dram_tensor("cc1i", [SH, 1024], BF16, kind="Internal").ap()
    cc1oq = nc.dram_tensor("cc1oq", [16384, 1024], BF16, kind="Internal",
                           addr_space="Shared").ap()
    cc1ok = nc.dram_tensor("cc1ok", [16384, 1024], BF16, kind="Internal",
                           addr_space="Shared").ap()
    cc1ov = nc.dram_tensor("cc1ov", [16384, 1024], BF16, kind="Internal",
                           addr_space="Shared").ap()
    cc2i = nc.dram_tensor("cc2i", [512, 4096], BF16, kind="Internal").ap()
    cc2o = nc.dram_tensor("cc2o", [4096, 4096], BF16, kind="Internal",
                          addr_space="Shared").ap()

    A = nc.alloc_sbuf_tensor
    idxt_sb = A("idxt_sb", [128, 64], I32)
    id_sb = A("id_sb", [128, 128], BF16)
    onc = A("onc", [128, 1], BF16)
    onr = A("onr", [1, 128], BF16)
    onrf = A("onrf", [1, 128], F32)
    maskd = A("maskd", [128, 128], F32)
    bqs = A("bqs", [128, 16], F32)
    bks = A("bks", [128, 16], F32)
    bvr = A("bvr", [1, D], BF16)
    bor = A("bor", [1, D], BF16)
    scl = A("scl", [1, 8], F32)
    iscb = A("iscb", [1, 1], BF16)
    rinv_bc = A("rinv_bc", [128, 1], F32)
    wssc_bc = A("wssc_bc", [128, 1], F32)
    nb_sb = A("nb_sb", [128, 1], F32)
    mv = A("mv", [128, 2], F32)
    stats = A("stats", [128, 24], F32)
    r1 = A("r1", [128, 1], F32)
    b2 = A("b2", [128, 1], F32)
    smax = A("smax", [1, 8], F32)
    cmx = A("cmx", [1, 16], F32)
    epsv = A("epsv", [128, 1], F32)
    oncf = A("oncf", [128, 1], F32)
    lsb = A("lsb", [1, 1024], F32)
    lnc = A("lnc", [1, 1], F32)
    lsb2 = A("lsb2", [1, 1024], F32)
    r1a = A("r1a", [128, 1], F32)
    glt = A("glt", [128, 16], BF16)

    xts = [A(f"xts{i}", [128, D], F32) for i in range(2)]
    xh_ts = [A(f"xh{i}", [128, D], BF16) for i in range(2)]
    stg = [A(f"stg{i}", [128, 512], BF16) for i in range(4)]

    xhat_fm = A("xhat_fm", [128, 16 * 1024], BF16)
    a_xf = nc.lookup_mloc(xhat_fm).addr
    oall = nc.alloc_sbuf_tensor_at("oall", [128, 16 * 1024], BF16, offset=a_xf)

    wb = A("wb", [128, 16 * D], BF16)
    a_wb = nc.lookup_mloc(wb).addr
    wf_sb = nc.alloc_sbuf_tensor_at("wf_sb", [128, 16 * D], FP8, offset=a_wb)
    xq_fm = nc.alloc_sbuf_tensor_at("xq_fm", [128, 16 * 1024], FP8,
                                    offset=a_wb + 32768)

    x1 = A("x1", [128, 8 * D], F32)
    a_x1 = nc.lookup_mloc(x1).addr
    qkv = nc.alloc_sbuf_tensor_at("qkv", [128, 3 * 4096], BF16, offset=a_x1)
    qbh = qkv[:, 0:4096]
    kbh = qkv[:, 4096:8192]
    vbh = qkv[:, 8192:12288]
    pt_sb = [nc.alloc_sbuf_tensor_at(f"pt{i}", [128, 1024], BF16,
                                     offset=a_x1 + 24576 + i * 2048)
             for i in range(2)]
    linv_bc = nc.alloc_sbuf_tensor_at("linv_bc", [128, 1024], F32,
                                      offset=a_x1 + 28672)
    osb = nc.alloc_sbuf_tensor_at("osb", [128, 1024], BF16,
                                  offset=a_x1 + 32768)
    qsq = nc.alloc_sbuf_tensor_at("qsq", [128, 512], BF16,
                                  offset=a_x1 + 34816)
    a_xts0 = nc.lookup_mloc(xts[0]).addr
    a_xts1 = nc.lookup_mloc(xts[1]).addr
    tq = nc.alloc_sbuf_tensor_at("tq", [128, D], F32, offset=a_xts0)
    xh2 = nc.alloc_sbuf_tensor_at("xh2", [128, D], F32, offset=a_xts1)
    a_xh0 = nc.lookup_mloc(xh_ts[0]).addr
    q8 = nc.alloc_sbuf_tensor_at("q8", [128, D], FP8, offset=a_xh0)
    a_stg0 = nc.lookup_mloc(stg[0]).addr
    assert nc.lookup_mloc(stg[3]).addr == a_stg0 + 3 * 1024
    sqts = nc.alloc_sbuf_tensor_at("sqts", [128, D], BF16, offset=a_stg0)
    a_xh1 = nc.lookup_mloc(xh_ts[1]).addr
    sqtsd = nc.alloc_sbuf_tensor_at("sqtsd", [128, D], BF16, offset=a_xh1)
    wb2 = nc.alloc_sbuf_tensor_at("wb2", [128, 16 * D], BF16, offset=a_x1)
    qkv2 = nc.alloc_sbuf_tensor_at("qkv2", [128, 3 * 4096], BF16,
                                   offset=a_x1 + 36864)
    qkvs = [qkv, qkv2]
    ptsum = nc.alloc_sbuf_tensor_at("ptsum", [128, 1024], F32,
                                    offset=a_x1 + 61440)
    ptsum2 = nc.alloc_sbuf_tensor_at("ptsum2", [128, 1024], F32,
                                     offset=a_xh1)
    ysb = nc.alloc_sbuf_tensor_at("ysb", [128, D], F32, offset=a_xf)
    ptsums = [ptsum, ptsum2]
    praw = nc.alloc_sbuf_tensor_at("praw", [128, 1024], BF16, offset=a_stg0)

    P0 = nc.alloc_psum_tensor("P0", [128, 1024], F32)
    P1 = nc.alloc_psum_tensor("P1", [128, 1024], F32)
    P2 = nc.alloc_psum_tensor("P2", [128, 1024], F32)
    L0 = nc.alloc_psum_tensor("L0", [128, 512], F32)
    L1 = nc.alloc_psum_tensor("L1", [128, 512], F32)
    banks4 = [P0[:, 0:512], P0[:, 512:1024], P1[:, 0:512], P1[:, 512:1024]]

    p = Prog(nc)
    sems = ["ld", "st", "wt", "pe", "ve", "se", "cc", "gp", "gl", "junk",
            "ob0", "ob1", "ob2", "ob3", "xr0", "xr1", "wfl"]
    for w_ in ("q", "k", "v", "y"):
        for k_ in range(4):
            sems.append(f"sg{w_}{k_}")
    for s_ in sems:
        p.mksem(s_)
    vals = {}

    for out_, in_ in ((idxt_sb, idxt), (id_sb, id_in), (onc, onc_in),
                      (onr, onr_in), (onrf, onrf_in), (maskd, mask_in), (bqs, bqi),
                      (bks, bki)):
        p.dma("sync", "ld", lambda e, o=out_, i=in_: e.dma_start(
            out=o[:, :], in_=i[:, :]))
    p.dma("sync", "ld", lambda e: e.dma_start(out=scl[0:1, 0:1], in_=wsc[:, :]))
    start_ld = p.val("ld")
    p.dma("gpsimd", "junk", lambda e: e.dma_start(out=bvr[:, :], in_=bvi[:, :]))
    p.dma("gpsimd", "junk", lambda e: e.dma_start(out=bor[:, :], in_=boi[:, :]))
    p.dma("gpsimd", "junk", lambda e: e.dma_start(out=iscb[:, :], in_=isc[:, :]))
    p.fence("f_sm", lambda e: e.dma_start(out=glt[:, :], in_=xin[0:128, 0:16]))
    for j in range(16):
        p.dma("gpsimd", "junk", lambda e, j=j: e.dma_start(
            out=wb[:, j * D:(j + 1) * D], in_=wqT[j * 128:(j + 1) * 128, :]))
    p.fence("f_wq", lambda e: e.dma_start(out=glt[:, :], in_=xin[0:128, 0:16]))

    p.wait("vector", "ld", start_ld)
    p.inc("vector", "ve", 1, lambda e: e.memset(epsv[:, :], EPS))
    p.inc("vector", "ve", 1, lambda e: e.memset(lnc[:, :], 24.0))
    p.inc("vector", "ve", 1, lambda e: e.memset(oncf[:, :], 1.0))
    p.inc("vector", "ve", 1, lambda e: e.tensor_scalar(
        bqs[:, :], bqs[:, :], SC, None, ALU.mult))
    p.inc("vector", "ve", 1, lambda e: e.tensor_scalar(
        bks[:, :], bks[:, :], SC, None, ALU.mult))
    p.wait("vector", "f_sm", 128)
    p.inc("vector", "ve", 1, lambda e: e.tensor_copy(scl[0:1, 3:4], iscb[:, :]))
    ve_scl = p.val("ve")
    p.wait("scalar", "ve", ve_scl)
    p.inc("scalar", "se", 1, lambda e: e.activation(
        scl[0:1, 4:5], scl[0:1, 3:4], AF.Ln))
    p.inc("scalar", "se", 1, lambda e: e.activation(
        scl[0:1, 1:2], scl[0:1, 4:5], AF.Exp, scale=-1.0))
    se_scl = p.val("se")
    p.wait("tensor", "ve", ve_scl)
    p.wait("tensor", "se", se_scl)
    p.inc("tensor", "pe", 1, lambda e: e.matmul(
        P2[:, 0:1], onrf[0:1, :], scl[0:1, 1:2], start=True, stop=True))
    p.inc("tensor", "pe", 1, lambda e: e.matmul(
        P2[:, 1:2], onrf[0:1, :], scl[0:1, 0:1], start=True, stop=True))
    p.inc("tensor", "pe", 1, lambda e: e.matmul(
        P2[:, 2:3], onrf[0:1, :], scl[0:1, 3:4], start=True, stop=True))
    pe_bc0 = p.val("pe")
    p.wait("scalar", "pe", pe_bc0)
    p.inc("scalar", "se", 1, lambda e: e.copy(rinv_bc[:, :], P2[:, 0:1]))
    p.inc("scalar", "se", 1, lambda e: e.copy(nb_sb[:, :], P2[:, 1:2]))
    se_bcast = p.val("se")
    p.wait("vector", "pe", pe_bc0)
    p.wait("vector", "se", se_bcast)
    p.inc("vector", "ve", 1, lambda e: e.tensor_tensor(
        wssc_bc[:, :], nb_sb[:, :], P2[:, 2:3], ALU.mult))
    ve_bcast = p.val("ve")

    def ln_tiles(src_getter, dst_q8=False):
        pass

    prev_se_r1 = 0
    prev_lnapply = {}
    prev_trev = {-1: 0}
    for t in range(8):
        xb, hb = xts[t % 2], xh_ts[t % 2]
        if t >= 2:
            p.wait("sync", "se", prev_lnapply[t - 2])
        p.dma("sync", "ld", lambda e, t=t, xb=xb: e.dma_start(
            out=xb[:, :], in_=xin[t * 128:(t + 1) * 128, :]))
        ld_here = p.val("ld")
        p.wait("scalar", "ld", ld_here)
        if t >= 1:
            p.wait("scalar", "ve", vals.get("ln1ve", 0))
        p.inc("scalar", "se", 1, lambda e, xb=xb: e.activation(
            sqts[:, :], xb[:, :], AF.Square))
        se_sq = p.val("se")
        p.wait("vector", "ld", ld_here)
        p.wait("vector", "se", se_sq)
        if t >= 1:
            p.wait("vector", "se", prev_se_r1)
        p.inc("vector", "ve", 1, lambda e, xb=xb: e.reduce_sum(
            mv[:, 0:1], xb[:, :], axis=AX.X))
        p.inc("vector", "ve", 1, lambda e: e.reduce_sum(
            mv[:, 1:2], sqts[:, :], axis=AX.X))
        p.inc("vector", "ve", 1, lambda e: e.tensor_scalar(
            stats[:, 0:1], mv[:, 0:1], 1.0 / D, None, ALU.mult))
        p.inc("vector", "ve", 1, lambda e: e.scalar_tensor_tensor(
            stats[:, 1:2], mv[:, 0:1], -1.0 / (D * D), mv[:, 0:1],
            ALU.mult, ALU.mult))
        ve_stats = p.val("ve")
        vals["ln1ve"] = ve_stats
        p.wait("scalar", "ve", ve_stats)
        p.inc("scalar", "se", 1, lambda e: e.activation(
            stats[:, 3:4], mv[:, 1:2], AF.Identity, bias=stats[:, 1:2],
            scale=1.0 / D))
        p.inc("scalar", "se", 1, lambda e: e.activation(
            r1a[:, :], stats[:, 3:4], AF.Ln, bias=epsv[:, :], scale=1.0))
        p.inc("scalar", "se", 1, lambda e: e.activation(
            r1[:, :], r1a[:, :], AF.Exp, bias=0.0, scale=-0.5))
        prev_se_r1 = p.val("se")
        p.wait("vector", "se", prev_se_r1)
        p.inc("vector", "ve", 1, lambda e: e.scalar_tensor_tensor(
            b2[:, :], stats[:, 0:1], -1.0, r1[:, :], ALU.mult, ALU.mult))
        ve_b2 = p.val("ve")
        p.wait("scalar", "ve", ve_b2)
        if t >= 2:
            p.wait("scalar", "pe", vals[f"tr{t-2}"])
        p.inc("scalar", "se", 1, lambda e, xb=xb, hb=hb: e.activation(
            hb[:, :], xb[:, :], AF.Identity, bias=b2[:, :], scale=r1[:, :]))
        prev_lnapply[t] = p.val("se")
        for rnd in range(2):
            ps = P2 if rnd == 0 else P1
            if t == 0 and rnd == 0:
                p.wait("tensor", "se", se_bcast)
                p.wait("tensor", "ve", ve_bcast)
            p.wait("tensor", "se", prev_lnapply[t])
            p.wait("tensor", "ve", prev_trev.get(t - 1, prev_trev[-1])
                   if rnd == 0 else vals.get(f"trevP1_{t-1}", 0))
            for jj in range(8):
                j = rnd * 8 + jj
                p.inc("tensor", "pe", 1,
                      lambda e, hb=hb, j=j, jj=jj, ps=ps: e.matmul(
                          ps[:, jj * 128:(jj + 1) * 128],
                          hb[:, j * 128:(j + 1) * 128], id_sb[:, :],
                          start=True, stop=True))
            pe_tr = p.val("pe")
            p.wait("vector", "pe", pe_tr)
            p.inc("vector", "ve", 1,
                  lambda e, t=t, rnd=rnd, ps=ps: e.tensor_copy(
                      xhat_fm[:, :].rearrange("p (j c) -> p j c", j=16)
                      [:, rnd * 8:(rnd + 1) * 8, t * 128:(t + 1) * 128],
                      ps[:, :].rearrange("p (j c) -> p j c", j=8)))
            if rnd == 0:
                prev_trev[t] = p.val("ve")
            else:
                vals[f"trevP1_{t}"] = p.val("ve")
        vals[f"tr{t}"] = p.val("pe")
    xhat_ve = p.val("ve")

    def proj_qk(which, wsem, bias_sb, base, wbuf, prevsec=None):
        for i, (oc, tb) in enumerate([(o, t2) for o in range(16)
                                      for t2 in range(2)]):
            bank, sb = banks4[i % 4], stg[i % 4]
            sem = f"sg{which}{i % 4}"
            p.wait("tensor", wsem, 128)
            if i == 0:
                p.wait("tensor", "ve", xhat_ve)
            if i >= 4:
                p.wait("tensor", sem, vals[f"{which}dma{i-4}"])
            elif prevsec:
                p.wait("tensor", f"sg{prevsec}{i % 4}",
                       vals[f"{prevsec}sgfull{i % 4}"])
            for j in range(16):
                p.inc("tensor", "pe", 1,
                      lambda e, j=j, oc=oc, tb=tb, bank=bank, wbuf=wbuf:
                      e.matmul(
                          bank, wbuf[:, j * D + oc * 128: j * D + (oc + 1) * 128],
                          xhat_fm[:, j * 1024 + tb * 512:
                                  j * 1024 + (tb + 1) * 512],
                          start=(j == 0), stop=(j == 15)))
            pe_here = p.val("pe")
            p.wait("scalar", "pe", pe_here)
            p.inc("scalar", "se", 1,
                  lambda e, bank=bank, sb=sb, oc=oc, bias_sb=bias_sb:
                  e.activation(sb[:, :], bank, AF.Identity,
                               bias=bias_sb[:, oc:oc + 1], scale=SC))
            p.dma("scalar", sem,
                  lambda e, sb=sb, oc=oc, tb=tb, base=base: e.dma_start(
                      out=cc1i[oc * 128 + base: oc * 128 + base + 128,
                               tb * 512:(tb + 1) * 512], in_=sb[:, :]))
            vals[f"{which}dma{i}"] = p.val(sem)
        for k_ in range(4):
            vals[f"{which}sgfull{k_}"] = p.val(f"sg{which}{k_}")

    for j in range(16):
        p.dma("gpsimd", "junk", lambda e, j=j: e.dma_start(
            out=wb2[:, j * D:(j + 1) * D], in_=wkT[j * 128:(j + 1) * 128, :]))
    p.fence("f_wk", lambda e: e.dma_start(out=glt[:, :], in_=xin[0:128, 0:16]))
    proj_qk("q", "f_wq", bqs, 0, wb)
    q_mms = p.val("pe")
    for k_ in range(4):
        p.wait("gpsimd", f"sgq{k_}", vals[f"qsgfull{k_}"])
    p.inc("gpsimd", "cc", 1, lambda e: e.collective_compute(
        "AllGather", ALU.bypass, ins=[cc1i[0:2048, :]], outs=[cc1oq[:, :]],
        replica_groups=GRPS))
    p.wait("gpsimd", "pe", q_mms)
    for j in range(16):
        p.dma("gpsimd", "junk", lambda e, j=j: e.dma_start(
            out=wb[:, j * D:(j + 1) * D], in_=wvT[j * 128:(j + 1) * 128, :]))
    p.fence("f_wv", lambda e: e.dma_start(out=glt[:, :], in_=xin[0:128, 0:16]))
    proj_qk("k", "f_wk", bks, 2048, wb2, prevsec="q")
    for k_ in range(4):
        p.wait("gpsimd", f"sgk{k_}", vals[f"ksgfull{k_}"])
    p.inc("gpsimd", "cc", 1, lambda e: e.collective_compute(
        "AllGather", ALU.bypass, ins=[cc1i[2048:4096, :]], outs=[cc1ok[:, :]],
        replica_groups=GRPS))

    for i, (tt, oc4) in enumerate([(t2, o) for t2 in range(8)
                                   for o in range(4)]):
        bank, sb = banks4[i % 4], stg[i % 4]
        sem = f"sgv{i % 4}"
        p.wait("tensor", "f_wv", 128)
        if i >= 4:
            p.wait("tensor", sem, vals[f"vdma{i-4}"])
        else:
            p.wait("tensor", f"sgk{i % 4}", vals[f"ksgfull{i % 4}"])
        for j in range(16):
            p.inc("tensor", "pe", 1,
                  lambda e, j=j, tt=tt, oc4=oc4, bank=bank: e.matmul(
                      bank,
                      xhat_fm[:, j * 1024 + tt * 128: j * 1024 + (tt + 1) * 128],
                      wb[:, j * D + oc4 * 512: j * D + (oc4 + 1) * 512],
                      start=(j == 0), stop=False))
        p.inc("tensor", "pe", 1, lambda e, oc4=oc4, bank=bank: e.matmul(
            bank, onr[:, :], bvr[0:1, oc4 * 512:(oc4 + 1) * 512],
            start=False, stop=True))
        pe_here = p.val("pe")
        p.wait("scalar", "pe", pe_here)
        p.inc("scalar", "se", 1, lambda e, bank=bank, sb=sb: e.activation(
            sb[:, :], bank, AF.Identity))
        p.dma("scalar", sem, lambda e, sb=sb, tt=tt, oc4=oc4: e.dma_start(
            out=cc1i[4096:6144, tt * 128:(tt + 1) * 128]
            .rearrange("(i q) c -> q i c", q=128)
            [:, 4 * oc4:4 * oc4 + 4, :],
            in_=sb[:, :].rearrange("p (i c) -> p i c", i=4)))
        vals[f"vdma{i}"] = p.val(sem)

    for k_ in range(4):
        p.wait("gpsimd", f"sgv{k_}", p.val(f"sgv{k_}"))
    p.inc("gpsimd", "cc", 1, lambda e: e.collective_compute(
        "AllGather", ALU.bypass, ins=[cc1i[4096:6144, :]], outs=[cc1ov[:, :]],
        replica_groups=GRPS))

    p.wait("gpsimd", "ld", start_ld)
    gstep = 0
    exp_hist = {}
    ov_hist = {}
    pts_hist = {}
    secs = (cc1oq, cc1ok, cc1ov)

    def emit_fin_l(fin):
        p.wait("tensor", "ve", pts_hist[fin["gend"] - 1])
        p.wait("tensor", "se", vals.get("se_lbank_free", 0))
        pts = fin["pts"]
        p.inc("tensor", "pe", 1, lambda e, pts=pts: e.matmul(
            L0[0:1, :], oncf[:, :], pts[:, 0:512], start=True, stop=True))
        p.inc("tensor", "pe", 1, lambda e, pts=pts: e.matmul(
            L1[0:1, :], oncf[:, :], pts[:, 512:1024], start=True, stop=True))
        vals[f"ptsum_free{fin['qc'] % 2}"] = p.val("pe")
        pe_qc = p.val("pe")
        p.wait("scalar", "pe", pe_qc)
        p.inc("scalar", "se", 1, lambda e: e.activation(
            lsb[0:1, 0:512], L0[0:1, :], AF.Ln, scale=C24))
        p.inc("scalar", "se", 1, lambda e: e.activation(
            lsb[0:1, 512:1024], L1[0:1, :], AF.Ln, scale=C24))
        p.inc("scalar", "se", 1, lambda e: e.activation(
            lsb2[0:1, :], lsb[0:1, :], AF.Exp, bias=lnc[0:1, :], scale=-1.0))
        vals["se_l_last"] = p.val("se")

    def emit_fin_osb(fin):
        u, qc = fin["u"], fin["qc"]
        p.wait("tensor", "se", vals["se_l_last"])
        p.inc("tensor", "pe", 1, lambda e: e.matmul(
            L0[:, :], onrf[0:1, :], lsb2[0:1, 0:512], start=True, stop=True))
        p.inc("tensor", "pe", 1, lambda e: e.matmul(
            L1[:, :], onrf[0:1, :], lsb2[0:1, 512:1024], start=True, stop=True))
        pe_l = p.val("pe")
        p.wait("scalar", "pe", pe_l)
        p.inc("scalar", "se", 1, lambda e: e.copy(linv_bc[:, 0:512], L0[:, :]))
        p.inc("scalar", "se", 1, lambda e: e.copy(
            linv_bc[:, 512:1024], L1[:, :]))
        se_linv = p.val("se")
        vals["se_lbank_free"] = se_linv
        p.wait("vector", "se", se_linv)
        p.wait("vector", f"ob{u}", p.val(f"ob{u}"))
        p.inc("vector", "ve", 1, lambda e: e.tensor_tensor(
            osb[:, :], praw[:, :], linv_bc[:, :], ALU.mult))
        ve_o = p.val("ve")
        p.wait("scalar", "ve", ve_o)
        p.dma("scalar", f"ob{u}", lambda e, u=u, qc=qc: e.dma_start(
            out=cc2i[u * 128:(u + 1) * 128, qc * 1024:(qc + 1) * 1024],
            in_=osb[:, :]))

    def emit_fin_praw(fin):
        p.wait("vector", "pe", ov_hist[fin["gend"] - 1])
        p.inc("vector", "ve", 1, lambda e: e.tensor_copy(
            praw[:, :], P2[:, :]))
        fin["ve_praw"] = p.val("ve")

    def emit_ag2(u):
        p.wait("gpsimd", f"ob{u}", 64)
        p.inc("gpsimd", "cc", 1, lambda e, u=u: e.collective_compute(
            "AllGather", ALU.bypass, ins=[cc2i[u * 128:(u + 1) * 128, :]],
            outs=[cc2o[u * 1024:(u + 1) * 1024, :]],
            replica_groups=GRPS))

    def emit_opiece(pu):
        p.wait("gpsimd", "cc", 4 + pu)
        for h in range(pu, 16, 4):
            p.dma("gpsimd", "junk", lambda e, h=h: e.indirect_dma_start(
                oall[:, h * 1024:(h + 1) * 1024], None,
                cc2o[:, :].rearrange("a (b c) -> (a b) c", c=1024),
                IndirectOffsetOnAxis(idxt_sb[:, 48 + h:49 + h], 0)))

    for u in range(4):
        qkvu = qkvs[u % 2]
        qb = qkvu[:, 0:4096]
        kb = qkvu[:, 4096:8192]
        vb = qkvu[:, 8192:12288]
        if u == 1:
            for j in range(16):
                p.dma("gpsimd", "junk", lambda e, j=j: e.dma_start(
                    out=wb[:, j * D:(j + 1) * D],
                    in_=woT[j * 128:(j + 1) * 128, :]))
            p.fence("f_wo", lambda e: e.dma_start(
                out=glt[:, :], in_=xin[0:128, 0:16]))
        if u >= 2:
            emit_ag2(u - 2)
            emit_opiece(u - 2)
            p.wait("gpsimd", "pe", vals[f"unitpe{u-2}"])
            p.wait("gpsimd", "se", vals[f"unitse{u-2}"])
        if u == 0:
            p.wait("gpsimd", "cc", 2)
        for s_ in range(4):
            for r_ in range(2):
                p.dma("gpsimd", "junk", lambda e, qkvu=qkvu, u=u, s_=s_, r_=r_:
                      e.indirect_dma_start(
                    qkvu[:, r_ * 4096 + s_ * 1024: r_ * 4096 + (s_ + 1) * 1024],
                    None, secs[r_][:, :],
                    IndirectOffsetOnAxis(
                        idxt_sb[:, u * 12 + s_ * 3 + r_:
                                u * 12 + s_ * 3 + r_ + 1], 0)))
        p.fence(f"f_uqk{u}", lambda e: e.dma_start(
            out=glt[:, :], in_=cc1oq[0:128, 0:16]))
        if u == 0:
            p.wait("gpsimd", "cc", 3)
        for s_ in range(4):
            p.dma("gpsimd", "junk", lambda e, qkvu=qkvu, u=u, s_=s_:
                  e.indirect_dma_start(
                qkvu[:, 8192 + s_ * 1024: 8192 + (s_ + 1) * 1024],
                None, secs[2][:, :],
                IndirectOffsetOnAxis(
                    idxt_sb[:, u * 12 + s_ * 3 + 2:
                            u * 12 + s_ * 3 + 3], 0)))
        p.fence(f"f_uv{u}", lambda e: e.dma_start(
            out=glt[:, :], in_=cc1oq[0:128, 0:16]))

        p.wait("scalar", f"f_uqk{u}", 128)
        if u > 0:
            p.wait("vector", "se", vals[f"unitse{u-1}"])
        for w, (src, slot) in enumerate(((qb, 0), (kb, 1))):
            for c8 in range(8):
                if c8 > 0 or w > 0:
                    p.wait("scalar", "pe", p.val("pe"))
                p.inc("scalar", "se", 1, lambda e, src=src, c8=c8: e.activation(
                    qsq[:, :], src[:, c8 * 512:(c8 + 1) * 512], AF.Square))
                se_h = p.val("se")
                p.wait("tensor", "se", se_h)
                p.wait("tensor", "ve", p.val("ve"))
                p.inc("tensor", "pe", 1, lambda e: e.matmul(
                    L0[0:1, :], onc[:, :], qsq[:, :], start=True, stop=True))
                pe_h = p.val("pe")
                p.wait("vector", "pe", pe_h)
                p.inc("vector", "ve", 1,
                      lambda e, w=w, c8=c8: e.reduce_max(
                          cmx[0:1, w * 8 + c8:w * 8 + c8 + 1],
                          L0[0:1, :], axis=AX.X))
        p.wait("scalar", "ve", p.val("ve"))
        p.inc("scalar", "se", 1, lambda e: e.copy(
            smax[0:1, 7:8], scl[0:1, 0:1]))
        p.wait("vector", "se", p.val("se"))
        p.inc("vector", "ve", 1, lambda e: e.reduce_max(
            smax[0:1, 0:1], cmx[0:1, 0:8], axis=AX.X))
        p.inc("vector", "ve", 1, lambda e: e.reduce_max(
            smax[0:1, 1:2], cmx[0:1, 8:16], axis=AX.X))
        p.wait("scalar", "ve", p.val("ve"))
        p.inc("scalar", "se", 1, lambda e: e.activation(
            smax[0:1, 2:3], smax[0:1, 0:1], AF.Ln))
        p.inc("scalar", "se", 1, lambda e: e.activation(
            smax[0:1, 5:6], smax[0:1, 1:2], AF.Ln))
        p.wait("vector", "se", p.val("se"))
        p.inc("vector", "ve", 1, lambda e: e.tensor_tensor(
            smax[0:1, 3:4], smax[0:1, 2:3], smax[0:1, 5:6], ALU.add))
        p.wait("scalar", "ve", p.val("ve"))
        p.inc("scalar", "se", 1, lambda e: e.activation(
            smax[0:1, 6:7], smax[0:1, 3:4], AF.Exp, scale=0.5))
        se_bnd = p.val("se")
        p.wait("vector", "se", se_bnd)
        p.inc("vector", "ve", 1, lambda e: e.tensor_scalar(
            smax[0:1, 4:5], smax[0:1, 6:7], -1.0, None, ALU.mult))
        ve_nb = p.val("ve")
        p.wait("tensor", "ve", ve_nb)
        p.inc("tensor", "pe", 1, lambda e: e.matmul(
            P2[:, 0:1], onrf[0:1, :], smax[0:1, 4:5], start=True, stop=True))
        pe_nb = p.val("pe")
        p.wait("scalar", "pe", pe_nb)
        p.inc("scalar", "se", 1, lambda e: e.copy(nb_sb[:, :], P2[:, 0:1]))
        se_nb = p.val("se")
        p.wait("tensor", f"f_uqk{u}", 128)
        p.wait("tensor", f"f_uv{u}", 128)

        fin = None
        for qc in range(4):
            qcb = qc * 1024
            nkt = 8 * qc + 8
            pend = None
            for kt in range(nkt):
                w0 = max(kt * 128 - qcb, 0)
                si = gstep % 2
                stp, ptb = (P0, pt_sb[0]) if si == 0 else (P1, pt_sb[1])
                if fin is not None:
                    if kt == 0:
                        emit_fin_praw(fin)
                    elif kt == 1:
                        emit_fin_l(fin)
                        p.wait("tensor", "ve", fin["ve_praw"])
                    elif kt == 2:
                        emit_fin_osb(fin)
                        fin = None
                if gstep - 2 in exp_hist:
                    p.wait("tensor", "se", exp_hist[gstep - 2])
                for a in ((w0, 512), (512, 1024)):
                    a0, b0 = max(a[0], w0), a[1]
                    if b0 <= w0 or b0 <= a0:
                        continue
                    p.inc("tensor", "pe", 1,
                          lambda e, kt=kt, a0=a0, b0=b0, qcb=qcb, stp=stp,
                          kb=kb, qb=qb:
                          e.matmul(stp[:, a0:b0],
                                   kb[:, kt * 128:(kt + 1) * 128],
                                   qb[:, qcb + a0:qcb + b0],
                                   start=True, stop=True))
                qk_here = p.val("pe")
                masked = kt * 128 >= qcb
                if masked:
                    p.wait("vector", "pe", qk_here)
                    p.inc("vector", "ve", 1,
                          lambda e, stp=stp, w0=w0: e.tensor_tensor(
                              stp[:, w0:w0 + 128], stp[:, w0:w0 + 128],
                              maskd[:, :], ALU.add))
                    mask_here = p.val("ve")
                p.wait("scalar", "pe", qk_here)
                if masked:
                    p.wait("scalar", "ve", mask_here)
                if gstep - 2 in ov_hist:
                    p.wait("scalar", "pe", ov_hist[gstep - 2])
                if gstep - 2 in pts_hist:
                    p.wait("scalar", "ve", pts_hist[gstep - 2])
                p.inc("scalar", "se", 1,
                      lambda e, stp=stp, ptb=ptb, w0=w0: e.activation(
                          ptb[:, w0:1024], stp[:, w0:1024], AF.Exp,
                          bias=nb_sb[:, :], scale=1.0))
                exp_hist[gstep] = p.val("se")
                p.wait("vector", "se", exp_hist[gstep])
                pts = ptsums[qc % 2]
                if kt == 0:
                    p.wait("vector", "pe", vals.get(f"ptsum_free{qc % 2}", 0))
                    p.inc("vector", "ve", 1,
                          lambda e, ptb=ptb, pts=pts: e.tensor_copy(
                              pts[:, :], ptb[:, :]))
                else:
                    p.inc("vector", "ve", 1,
                          lambda e, ptb=ptb, w0=w0, pts=pts: e.tensor_tensor(
                              pts[:, w0:1024], pts[:, w0:1024],
                              ptb[:, w0:1024], ALU.add))
                pts_hist[gstep] = p.val("ve")
                if pend is not None:
                    emit_ov(p, pend, P2, L0, L1, vb, onc, pt_sb,
                            exp_hist, ov_hist)
                pend = (kt, w0, si, gstep, kt == 0, False)
                gstep += 1
            pend = pend[:5] + (True,)
            emit_ov(p, pend, P2, L0, L1, vb, onc, pt_sb, exp_hist, ov_hist)
            pend = None
            fin = {"gend": gstep, "pts": ptsums[qc % 2], "u": u, "qc": qc}
        emit_fin_praw(fin)
        emit_fin_l(fin)
        p.wait("tensor", "ve", fin["ve_praw"])
        emit_fin_osb(fin)
        fin = None
        vals[f"unitpe{u}"] = p.val("pe")
        vals[f"unitse{u}"] = p.val("se")
        vals[f"odma{u}"] = p.val(f"ob{u}")

    emit_ag2(2)
    emit_opiece(2)
    emit_ag2(3)
    emit_opiece(3)
    p.fence("f_og", lambda e: e.dma_start(out=glt[:, :], in_=cc2o[0:128, 0:16]))

    p.wait("sync", "cc", 7)
    for i, (tt, oc4) in enumerate([(t2, o) for t2 in range(8)
                                   for o in range(4)]):
        bank = banks4[i % 4]
        if oc4 == 0:
            xb = xts[tt % 2]
            if tt >= 2:
                p.wait("sync", "ve", vals[f"resid{tt-2}_3"])
            p.dma("sync", f"xr{tt % 2}", lambda e, tt=tt, xb=xb: e.dma_start(
                out=xb[:, :], in_=xin[tt * 128:(tt + 1) * 128, :]))
            vals[f"xl{tt}"] = p.val(f"xr{tt % 2}")
        p.wait("tensor", "f_wo", 128)
        if i == 0:
            p.wait("tensor", "f_og", 128)
        if i >= 4:
            p.wait("tensor", "ve", vals[f"resid{(i-4)//4}_{(i-4)%4}"])
        for j in range(16):
            p.inc("tensor", "pe", 1,
                  lambda e, j=j, tt=tt, oc4=oc4, bank=bank: e.matmul(
                      bank,
                      oall[:, j * 1024 + tt * 128: j * 1024 + (tt + 1) * 128],
                      wb[:, j * D + oc4 * 512: j * D + (oc4 + 1) * 512],
                      start=(j == 0), stop=False))
        p.inc("tensor", "pe", 1, lambda e, oc4=oc4, bank=bank: e.matmul(
            bank, onr[:, :], bor[0:1, oc4 * 512:(oc4 + 1) * 512],
            start=False, stop=True))
        pe_here = p.val("pe")
        p.wait("vector", "pe", pe_here)
        p.wait("vector", f"xr{tt % 2}", vals[f"xl{tt}"])
        p.inc("vector", "ve", 1,
              lambda e, tt=tt, oc4=oc4, bank=bank, xb=xts[tt % 2]:
              e.tensor_tensor(
                  x1[:, tt * D + oc4 * 512: tt * D + (oc4 + 1) * 512],
                  bank, xb[:, oc4 * 512:(oc4 + 1) * 512], ALU.add))
        vals[f"resid{tt}_{oc4}"] = p.val("ve")
    op_mms = p.val("pe")

    p.wait("sync", "pe", op_mms)
    for j in range(16):
        p.dma("sync", "wfl", lambda e, j=j: e.dma_start(
            out=wf_sb[:, j * D:(j + 1) * D], in_=wfT[j * 128:(j + 1) * 128, :]))
    wf_done = p.val("wfl")

    def emit_ffn_tile(tt):
        for oc4 in range(4):
            i = tt * 4 + oc4
            bank = banks4[i % 4]
            p.wait("tensor", "se", vals[f"q8done{tt}"])
            if i >= 4:
                p.wait("tensor", f"sgy{(i - 4) % 4}", vals[f"ydma{i-4}"])
            for j in range(16):
                p.inc("tensor", "pe", 1,
                      lambda e, j=j, tt=tt, oc4=oc4, bank=bank: e.matmul(
                          bank,
                          xq_fm[:, j * 1024 + tt * 128: j * 1024 + (tt + 1) * 128],
                          wf_sb[:, j * D + oc4 * 512: j * D + (oc4 + 1) * 512],
                          start=(j == 0), stop=(j == 15)))
            pe_here = p.val("pe")
            p.wait("vector", "pe", pe_here)
            p.inc("vector", "ve", 1,
                  lambda e, tt=tt, oc4=oc4, bank=bank: e.scalar_tensor_tensor(
                      ysb[:, oc4 * 512:(oc4 + 1) * 512], bank, wssc_bc[:, 0:1],
                      x1[:, tt * D + oc4 * 512: tt * D + (oc4 + 1) * 512],
                      ALU.mult, ALU.add))
            ve_y = p.val("ve")
            p.wait("scalar", "ve", ve_y)
            p.dma("scalar", f"sgy{i % 4}", lambda e, tt=tt, oc4=oc4: e.dma_start(
                out=yout[tt * 128:(tt + 1) * 128, oc4 * 512:(oc4 + 1) * 512],
                in_=ysb[:, oc4 * 512:(oc4 + 1) * 512]))
            vals[f"ydma{i}"] = p.val(f"sgy{i % 4}")

    prev_se_r1d = 0
    vals["trD-1"] = 0
    for t in range(8):
        p.wait("vector", "ve", vals[f"resid{t}_3"])
        if t >= 1:
            p.wait("vector", "se", prev_se_r1d)
        p.wait("vector", "pe", vals[f"trD{t-1}"])
        p.wait("scalar", "ve", p.val("ve"))
        p.inc("scalar", "se", 1, lambda e, t=t: e.activation(
            sqtsd[:, :], x1[:, t * D:(t + 1) * D], AF.Square))
        se_sq = p.val("se")
        p.wait("vector", "se", se_sq)
        p.inc("vector", "ve", 1, lambda e, t=t: e.reduce_sum(
            mv[:, 0:1], x1[:, t * D:(t + 1) * D], axis=AX.X))
        p.inc("vector", "ve", 1, lambda e: e.reduce_sum(
            mv[:, 1:2], sqtsd[:, :], axis=AX.X))
        p.inc("vector", "ve", 1, lambda e: e.tensor_scalar(
            stats[:, 0:1], mv[:, 0:1], 1.0 / D, None, ALU.mult))
        p.inc("vector", "ve", 1, lambda e: e.scalar_tensor_tensor(
            stats[:, 1:2], mv[:, 0:1], -1.0 / (D * D), mv[:, 0:1],
            ALU.mult, ALU.mult))
        ve_stats = p.val("ve")
        p.wait("scalar", "ve", ve_stats)
        p.inc("scalar", "se", 1, lambda e: e.activation(
            stats[:, 3:4], mv[:, 1:2], AF.Identity, bias=stats[:, 1:2],
            scale=1.0 / D))
        p.inc("scalar", "se", 1, lambda e: e.activation(
            r1a[:, :], stats[:, 3:4], AF.Ln, bias=epsv[:, :], scale=1.0))
        p.inc("scalar", "se", 1, lambda e: e.activation(
            r1[:, :], r1a[:, :], AF.Exp, bias=0.0, scale=-0.5))
        prev_se_r1d = p.val("se")
        p.wait("vector", "se", prev_se_r1d)
        p.inc("vector", "ve", 1, lambda e: e.scalar_tensor_tensor(
            b2[:, :], stats[:, 0:1], -1.0, r1[:, :], ALU.mult, ALU.mult))
        ve_b2 = p.val("ve")
        p.wait("scalar", "ve", ve_b2)
        p.inc("scalar", "se", 1, lambda e, t=t: e.activation(
            xh2[:, :], x1[:, t * D:(t + 1) * D], AF.Identity,
            bias=b2[:, :], scale=r1[:, :]))
        se_xh2 = p.val("se")
        p.wait("vector", "se", se_xh2)
        if t == 0:
            p.wait("vector", "se", se_bcast)
        p.inc("vector", "ve", 1, lambda e: e.tensor_scalar(
            tq[:, :], xh2[:, :], rinv_bc[:, 0:1], None, ALU.mult))
        p.inc("vector", "ve", 1, lambda e: e.tensor_scalar(
            q8[:, :], tq[:, :], 448.0, -448.0, ALU.min, ALU.max))
        ve_q8 = p.val("ve")
        p.wait("tensor", "ve", ve_q8)
        if t == 0:
            p.wait("tensor", "wfl", wf_done)
        for rnd in range(2):
            p.wait("tensor", "se", p.val("se"))
            for jj in range(8):
                j = rnd * 8 + jj
                p.inc("tensor", "pe", 1,
                      lambda e, j=j, jj=jj: e.matmul(
                          P2[:, jj * 128:(jj + 1) * 128],
                          q8[:, j * 128:(j + 1) * 128], id_sb[:, :],
                          start=True, stop=True))
            pe_tr = p.val("pe")
            p.wait("scalar", "pe", pe_tr)
            p.inc("scalar", "se", 1,
                  lambda e, t=t, rnd=rnd: e.copy(
                      xq_fm[:, :].rearrange("p (j c) -> p j c", j=16)
                      [:, rnd * 8:(rnd + 1) * 8, t * 128:(t + 1) * 128],
                      P2[:, :].rearrange("p (j c) -> p j c", j=8)))
        vals[f"q8done{t}"] = p.val("se")
        vals[f"trD{t}"] = p.val("pe")
        if t >= 1:
            emit_ffn_tile(t - 1)
    emit_ffn_tile(7)

    for k_ in range(4):
        p.wait("scalar", f"sgy{k_}")
    p.wait("scalar", "st")

    with nc.Block() as block:
        @block.sync
        def _(e):
            for f in p.ops["sync"]:
                f(e)

        @block.gpsimd
        def _(e):
            for f in p.ops["gpsimd"]:
                f(e)

        @block.vector
        def _(e):
            for f in p.ops["vector"]:
                f(e)

        @block.scalar
        def _(e):
            for f in p.ops["scalar"]:
                f(e)

        @block.tensor
        def _(e):
            for f in p.ops["tensor"]:
                f(e)

    return nc


def emit_ov(p, pend, P2, L0, L1, vbh, onc, pt_sb, exp_hist, ov_hist):
    kt, w0, si, g, is_start, is_stop = pend
    ptb = pt_sb[si]
    p.wait("tensor", "se", exp_hist[g])
    for a in ((w0, 512), (512, 1024)):
        a0, b0 = max(a[0], w0), a[1]
        if b0 <= w0 or b0 <= a0:
            continue
        p.inc("tensor", "pe", 1,
              lambda e, kt=kt, a0=a0, b0=b0, ptb=ptb, st_=is_start,
              sp_=is_stop: e.matmul(
                  P2[:, a0:b0], vbh[:, kt * 128:(kt + 1) * 128],
                  ptb[:, a0:b0], start=st_, stop=sp_))
    ov_hist[g] = p.val("pe")


_NC_CACHE = []


def kernel(L0_attn_X, L0_attn_Wq, L0_attn_bq, L0_attn_Wk, L0_attn_bk,
           L0_attn_Wv, L0_attn_bv, L0_attn_Wo, L0_attn_bo, L0_ffn_weight,
           L0_ffn_weight_scale, L0_ffn_input_scale):
    X = np.ascontiguousarray(np.asarray(L0_attn_X, dtype=np.float32)
                             .reshape(B * S, D))
    wqTh = np.ascontiguousarray(np.asarray(L0_attn_Wq, dtype=np.float32).T)
    wkTh = np.ascontiguousarray(np.asarray(L0_attn_Wk, dtype=np.float32).T)
    wvTh = np.ascontiguousarray(np.asarray(L0_attn_Wv, dtype=np.float32).T)
    woTh = np.ascontiguousarray(np.asarray(L0_attn_Wo, dtype=np.float32).T)
    wf = np.asarray(L0_ffn_weight)
    wfTh = np.ascontiguousarray(wf.view(ml_dtypes.float8_e4m3).T)
    bqh = np.ascontiguousarray(
        np.asarray(L0_attn_bq, np.float32).reshape(16, 128).T)
    bkh = np.ascontiguousarray(
        np.asarray(L0_attn_bk, np.float32).reshape(16, 128).T)
    bvh = np.ascontiguousarray(np.asarray(L0_attn_bv, np.float32).reshape(1, D))
    boh = np.ascontiguousarray(np.asarray(L0_attn_bo, np.float32).reshape(1, D))
    wsch = np.asarray(L0_ffn_weight_scale, np.float32).reshape(1, 1)
    isch = np.asarray(L0_ffn_input_scale).reshape(1, 1)
    if isch.dtype != ml_dtypes.bfloat16:
        isch = isch.view(ml_dtypes.bfloat16) if isch.dtype.itemsize == 2 \
            else isch.astype(ml_dtypes.bfloat16)

    ident = np.eye(128, dtype=ml_dtypes.bfloat16)
    onc_h = np.ones((128, 1), dtype=ml_dtypes.bfloat16)
    onr_h = np.ones((1, 128), dtype=ml_dtypes.bfloat16)
    onrf_h = np.ones((1, 128), dtype=np.float32)
    kk, qq = np.meshgrid(np.arange(128), np.arange(128), indexing="ij")
    maskd_h = np.where(kk <= qq, 0.0, NEG).astype(np.float32)

    in_maps = []
    for c in range(NC):
        idxt_h = np.zeros((128, 64), dtype=np.int32)
        pp = np.arange(128, dtype=np.int32)
        for u in range(4):
            g = 4 * c + u
            h = g % 16
            for s_ in range(4):
                for r_ in range(3):
                    idxt_h[:, u * 12 + s_ * 3 + r_] = \
                        (4 * (c // 4) + s_) * 2048 + h * 128 + pp
        for h in range(16):
            orow = 1024 * (h % 4) + (4 * (c // 4) + h // 4) * 128 + pp
            idxt_h[:, 48 + h] = orow * 4 + (c % 4)
        in_maps.append({
            "xin": X[c * T:(c + 1) * T], "wqT": wqTh, "wkT": wkTh,
            "wvT": wvTh, "woT": woTh, "wfT": wfTh, "bqi": bqh, "bki": bkh,
            "bvi": bvh, "boi": boh, "wsc": wsch, "isc": isch,
            "id_in": ident, "onc_in": onc_h, "onr_in": onr_h, "onrf_in": onrf_h,
            "mask_in": maskd_h, "idxt": idxt_h,
        })

    if not _NC_CACHE:
        nc = bass.Bass("TRN2", target_bir_lowering=False)
        build(nc)
        _NC_CACHE.append(nc)
    nc = _NC_CACHE[0]
    res = run_bass_kernel_spmd(nc, in_maps, core_ids=list(range(NC)))
    out = np.concatenate([res.results[c]["yout"] for c in range(NC)], axis=0)
    return out.reshape(B, S, D)


# version 21
# speedup vs baseline: 1.0183x; 1.0183x over previous
import numpy as np
import ml_dtypes
import concourse.bass as bass
import concourse.mybir as mybir
from concourse.bass import IndirectOffsetOnAxis
from concourse.bass_utils import run_bass_kernel_spmd

F32 = mybir.dt.float32
BF16 = mybir.dt.bfloat16
FP8 = mybir.dt.float8e4
U32 = mybir.dt.uint32
I32 = mybir.dt.int32
ALU = mybir.AluOpType
AF = mybir.ActivationFunctionType
AX = mybir.AxisListType

D = 2048
S = 4096
B = 2
T = 1024
NC = 8
SC = float(128.0 ** -0.25)
EPS = 1e-5
NEG = -30.0
C24 = float(np.exp(24.0))
GRPS = [list(range(8))]
SH = 6144


class Prog:

    def __init__(self, nc):
        self.nc = nc
        self.ops = {"sync": [], "gpsimd": [], "vector": [], "scalar": [],
                    "tensor": []}
        self.sem = {}
        self.cnt = {}

    def mksem(self, name):
        self.sem[name] = self.nc.alloc_semaphore(name)
        self.cnt[name] = 0

    def emit(self, eng, fn):
        self.ops[eng].append(fn)

    def inc(self, eng, sem, n, fn):
        self.cnt[sem] += n
        h = self.sem[sem]
        self.ops[eng].append(lambda e, fn=fn, h=h, n=n: fn(e).then_inc(h, n))

    def dma(self, eng, sem, fn):
        self.inc(eng, sem, 16, fn)

    def wait(self, eng, sem, upto=None):
        v = self.cnt[sem] if upto is None else upto
        if v <= 0:
            return
        h = self.sem[sem]
        self.ops[eng].append(lambda e, h=h, v=v: e.wait_ge(h, v))

    def val(self, sem):
        return self.cnt[sem]

    def fence(self, name, fn):
        self.mksem(name)
        for _ in range(8):
            self.dma("gpsimd", name, fn)


def build(nc):
    xin = nc.dram_tensor("xin", [T, D], F32, kind="ExternalInput").ap()
    wqT = nc.dram_tensor("wqT", [D, D], F32, kind="ExternalInput").ap()
    wkT = nc.dram_tensor("wkT", [D, D], F32, kind="ExternalInput").ap()
    wvT = nc.dram_tensor("wvT", [D, D], F32, kind="ExternalInput").ap()
    woT = nc.dram_tensor("woT", [D, D], F32, kind="ExternalInput").ap()
    wfT = nc.dram_tensor("wfT", [D, D], FP8, kind="ExternalInput").ap()
    bqi = nc.dram_tensor("bqi", [128, 16], F32, kind="ExternalInput").ap()
    bki = nc.dram_tensor("bki", [128, 16], F32, kind="ExternalInput").ap()
    bvi = nc.dram_tensor("bvi", [1, D], F32, kind="ExternalInput").ap()
    boi = nc.dram_tensor("boi", [1, D], F32, kind="ExternalInput").ap()
    wsc = nc.dram_tensor("wsc", [1, 1], F32, kind="ExternalInput").ap()
    isc = nc.dram_tensor("isc", [1, 1], BF16, kind="ExternalInput").ap()
    id_in = nc.dram_tensor("id_in", [128, 128], BF16, kind="ExternalInput").ap()
    onc_in = nc.dram_tensor("onc_in", [128, 1], BF16, kind="ExternalInput").ap()
    onr_in = nc.dram_tensor("onr_in", [1, 128], BF16, kind="ExternalInput").ap()
    onrf_in = nc.dram_tensor("onrf_in", [1, 128], F32, kind="ExternalInput").ap()
    mask_in = nc.dram_tensor("mask_in", [128, 128], F32, kind="ExternalInput").ap()
    idxt = nc.dram_tensor("idxt", [128, 64], I32, kind="ExternalInput").ap()
    yout = nc.dram_tensor("yout", [T, D], F32, kind="ExternalOutput").ap()

    cc1i = nc.dram_tensor("cc1i", [SH, 1024], BF16, kind="Internal").ap()
    cc1oq = nc.dram_tensor("cc1oq", [16384, 1024], BF16, kind="Internal",
                           addr_space="Shared").ap()
    cc1ok = nc.dram_tensor("cc1ok", [16384, 1024], BF16, kind="Internal",
                           addr_space="Shared").ap()
    cc1ov = nc.dram_tensor("cc1ov", [16384, 1024], BF16, kind="Internal",
                           addr_space="Shared").ap()
    cc2i = nc.dram_tensor("cc2i", [512, 4096], BF16, kind="Internal").ap()
    cc2o = nc.dram_tensor("cc2o", [4096, 4096], BF16, kind="Internal",
                          addr_space="Shared").ap()

    A = nc.alloc_sbuf_tensor
    idxt_sb = A("idxt_sb", [128, 64], I32)
    id_sb = A("id_sb", [128, 128], BF16)
    onc = A("onc", [128, 1], BF16)
    onr = A("onr", [1, 128], BF16)
    onrf = A("onrf", [1, 128], F32)
    maskd = A("maskd", [128, 128], F32)
    bqs = A("bqs", [128, 16], F32)
    bks = A("bks", [128, 16], F32)
    bvr = A("bvr", [1, D], BF16)
    bor = A("bor", [1, D], BF16)
    scl = A("scl", [1, 8], F32)
    iscb = A("iscb", [1, 1], BF16)
    rinv_bc = A("rinv_bc", [128, 1], F32)
    wssc_bc = A("wssc_bc", [128, 1], F32)
    nb_sb = A("nb_sb", [128, 1], F32)
    mv = A("mv", [128, 2], F32)
    stats = A("stats", [128, 24], F32)
    r1 = A("r1", [128, 1], F32)
    b2 = A("b2", [128, 1], F32)
    smax = A("smax", [1, 8], F32)
    cmx = A("cmx", [1, 16], F32)
    epsv = A("epsv", [128, 1], F32)
    oncf = A("oncf", [128, 1], F32)
    lsb = A("lsb", [1, 1024], F32)
    lnc = A("lnc", [1, 1], F32)
    lsb2 = A("lsb2", [1, 1024], F32)
    r1a = A("r1a", [128, 1], F32)
    glt = A("glt", [128, 16], BF16)

    xts = [A(f"xts{i}", [128, D], F32) for i in range(2)]
    xh_ts = [A(f"xh{i}", [128, D], BF16) for i in range(2)]
    stg = [A(f"stg{i}", [128, 512], BF16) for i in range(4)]

    xhat_fm = A("xhat_fm", [128, 16 * 1024], BF16)
    a_xf = nc.lookup_mloc(xhat_fm).addr
    oall = nc.alloc_sbuf_tensor_at("oall", [128, 16 * 1024], BF16, offset=a_xf)

    wb = A("wb", [128, 16 * D], BF16)
    a_wb = nc.lookup_mloc(wb).addr
    wf_sb = nc.alloc_sbuf_tensor_at("wf_sb", [128, 16 * D], FP8, offset=a_wb)
    xq_fm = nc.alloc_sbuf_tensor_at("xq_fm", [128, 16 * 1024], FP8,
                                    offset=a_wb + 32768)

    x1 = A("x1", [128, 8 * D], F32)
    a_x1 = nc.lookup_mloc(x1).addr
    qkv = nc.alloc_sbuf_tensor_at("qkv", [128, 3 * 4096], BF16, offset=a_x1)
    qbh = qkv[:, 0:4096]
    kbh = qkv[:, 4096:8192]
    vbh = qkv[:, 8192:12288]
    pt_sb = [nc.alloc_sbuf_tensor_at(f"pt{i}", [128, 1024], BF16,
                                     offset=a_x1 + 24576 + i * 2048)
             for i in range(2)]
    linv_bc = nc.alloc_sbuf_tensor_at("linv_bc", [128, 1024], F32,
                                      offset=a_x1 + 28672)
    osb = nc.alloc_sbuf_tensor_at("osb", [128, 1024], BF16,
                                  offset=a_x1 + 32768)
    qsq = nc.alloc_sbuf_tensor_at("qsq", [128, 512], BF16,
                                  offset=a_x1 + 34816)
    a_xts0 = nc.lookup_mloc(xts[0]).addr
    a_xts1 = nc.lookup_mloc(xts[1]).addr
    tq = nc.alloc_sbuf_tensor_at("tq", [128, D], F32, offset=a_xts0)
    xh2 = nc.alloc_sbuf_tensor_at("xh2", [128, D], F32, offset=a_xts1)
    a_xh0 = nc.lookup_mloc(xh_ts[0]).addr
    q8 = nc.alloc_sbuf_tensor_at("q8", [128, D], FP8, offset=a_xh0)
    a_stg0 = nc.lookup_mloc(stg[0]).addr
    assert nc.lookup_mloc(stg[3]).addr == a_stg0 + 3 * 1024
    sqts = nc.alloc_sbuf_tensor_at("sqts", [128, D], BF16, offset=a_stg0)
    a_xh1 = nc.lookup_mloc(xh_ts[1]).addr
    sqtsd = nc.alloc_sbuf_tensor_at("sqtsd", [128, D], BF16, offset=a_xh1)
    wb2 = nc.alloc_sbuf_tensor_at("wb2", [128, 16 * D], BF16, offset=a_x1)
    qkv2 = nc.alloc_sbuf_tensor_at("qkv2", [128, 3 * 4096], BF16,
                                   offset=a_x1 + 36864)
    qkvs = [qkv, qkv2]
    ptsum = nc.alloc_sbuf_tensor_at("ptsum", [128, 1024], F32,
                                    offset=a_x1 + 61440)
    ptsum2 = nc.alloc_sbuf_tensor_at("ptsum2", [128, 1024], F32,
                                     offset=a_xh1)
    ysb = nc.alloc_sbuf_tensor_at("ysb", [128, D], F32, offset=a_xf)
    ptsums = [ptsum, ptsum2]
    praw = nc.alloc_sbuf_tensor_at("praw", [128, 1024], BF16, offset=a_stg0)

    P0 = nc.alloc_psum_tensor("P0", [128, 1024], F32)
    P1 = nc.alloc_psum_tensor("P1", [128, 1024], F32)
    P2 = nc.alloc_psum_tensor("P2", [128, 1024], F32)
    L0 = nc.alloc_psum_tensor("L0", [128, 512], F32)
    L1 = nc.alloc_psum_tensor("L1", [128, 512], F32)
    banks4 = [P0[:, 0:512], P0[:, 512:1024], P1[:, 0:512], P1[:, 512:1024]]

    p = Prog(nc)
    sems = ["ld", "st", "wt", "pe", "ve", "se", "cc", "gp", "gl", "junk",
            "ob0", "ob1", "ob2", "ob3", "xr0", "xr1", "wfl"]
    for w_ in ("q", "k", "v", "y"):
        for k_ in range(4):
            sems.append(f"sg{w_}{k_}")
    for s_ in sems:
        p.mksem(s_)
    vals = {}

    for out_, in_ in ((idxt_sb, idxt), (id_sb, id_in), (onc, onc_in),
                      (onr, onr_in), (onrf, onrf_in), (maskd, mask_in), (bqs, bqi),
                      (bks, bki)):
        p.dma("sync", "ld", lambda e, o=out_, i=in_: e.dma_start(
            out=o[:, :], in_=i[:, :]))
    p.dma("sync", "ld", lambda e: e.dma_start(out=scl[0:1, 0:1], in_=wsc[:, :]))
    start_ld = p.val("ld")
    p.dma("gpsimd", "junk", lambda e: e.dma_start(out=bvr[:, :], in_=bvi[:, :]))
    p.dma("gpsimd", "junk", lambda e: e.dma_start(out=bor[:, :], in_=boi[:, :]))
    p.dma("gpsimd", "junk", lambda e: e.dma_start(out=iscb[:, :], in_=isc[:, :]))
    p.fence("f_sm", lambda e: e.dma_start(out=glt[:, :], in_=xin[0:128, 0:16]))
    for j in range(16):
        p.dma("gpsimd", "junk", lambda e, j=j: e.dma_start(
            out=wb[:, j * D:(j + 1) * D], in_=wqT[j * 128:(j + 1) * 128, :]))
    p.fence("f_wq", lambda e: e.dma_start(out=glt[:, :], in_=xin[0:128, 0:16]))

    p.wait("vector", "ld", start_ld)
    p.inc("vector", "ve", 1, lambda e: e.memset(epsv[:, :], EPS))
    p.inc("vector", "ve", 1, lambda e: e.memset(lnc[:, :], 24.0))
    p.inc("vector", "ve", 1, lambda e: e.memset(oncf[:, :], 1.0))
    p.inc("vector", "ve", 1, lambda e: e.tensor_scalar(
        bqs[:, :], bqs[:, :], SC, None, ALU.mult))
    p.inc("vector", "ve", 1, lambda e: e.tensor_scalar(
        bks[:, :], bks[:, :], SC, None, ALU.mult))
    p.wait("vector", "f_sm", 128)
    p.inc("vector", "ve", 1, lambda e: e.tensor_copy(scl[0:1, 3:4], iscb[:, :]))
    ve_scl = p.val("ve")
    p.wait("scalar", "ve", ve_scl)
    p.inc("scalar", "se", 1, lambda e: e.activation(
        scl[0:1, 4:5], scl[0:1, 3:4], AF.Ln))
    p.inc("scalar", "se", 1, lambda e: e.activation(
        scl[0:1, 1:2], scl[0:1, 4:5], AF.Exp, scale=-1.0))
    se_scl = p.val("se")
    p.wait("tensor", "ve", ve_scl)
    p.wait("tensor", "se", se_scl)
    p.inc("tensor", "pe", 1, lambda e: e.matmul(
        P2[:, 0:1], onrf[0:1, :], scl[0:1, 1:2], start=True, stop=True))
    p.inc("tensor", "pe", 1, lambda e: e.matmul(
        P2[:, 1:2], onrf[0:1, :], scl[0:1, 0:1], start=True, stop=True))
    p.inc("tensor", "pe", 1, lambda e: e.matmul(
        P2[:, 2:3], onrf[0:1, :], scl[0:1, 3:4], start=True, stop=True))
    pe_bc0 = p.val("pe")
    p.wait("scalar", "pe", pe_bc0)
    p.inc("scalar", "se", 1, lambda e: e.copy(rinv_bc[:, :], P2[:, 0:1]))
    p.inc("scalar", "se", 1, lambda e: e.copy(nb_sb[:, :], P2[:, 1:2]))
    se_bcast = p.val("se")
    p.wait("vector", "pe", pe_bc0)
    p.wait("vector", "se", se_bcast)
    p.inc("vector", "ve", 1, lambda e: e.tensor_tensor(
        wssc_bc[:, :], nb_sb[:, :], P2[:, 2:3], ALU.mult))
    ve_bcast = p.val("ve")

    def ln_tiles(src_getter, dst_q8=False):
        pass

    prev_se_r1 = 0
    prev_lnapply = {}
    prev_trev = {-1: 0}
    for t in range(8):
        xb, hb = xts[t % 2], xh_ts[t % 2]
        if t >= 2:
            p.wait("sync", "se", prev_lnapply[t - 2])
        p.dma("sync", "ld", lambda e, t=t, xb=xb: e.dma_start(
            out=xb[:, :], in_=xin[t * 128:(t + 1) * 128, :]))
        ld_here = p.val("ld")
        p.wait("scalar", "ld", ld_here)
        if t >= 1:
            p.wait("scalar", "ve", vals.get("ln1ve", 0))
        p.inc("scalar", "se", 1, lambda e, xb=xb: e.activation(
            sqts[:, :], xb[:, :], AF.Square))
        se_sq = p.val("se")
        p.wait("vector", "ld", ld_here)
        p.wait("vector", "se", se_sq)
        if t >= 1:
            p.wait("vector", "se", prev_se_r1)
        p.inc("vector", "ve", 1, lambda e, xb=xb: e.reduce_sum(
            mv[:, 0:1], xb[:, :], axis=AX.X))
        p.inc("vector", "ve", 1, lambda e: e.reduce_sum(
            mv[:, 1:2], sqts[:, :], axis=AX.X))
        p.inc("vector", "ve", 1, lambda e: e.tensor_scalar(
            stats[:, 0:1], mv[:, 0:1], 1.0 / D, None, ALU.mult))
        p.inc("vector", "ve", 1, lambda e: e.scalar_tensor_tensor(
            stats[:, 1:2], mv[:, 0:1], -1.0 / (D * D), mv[:, 0:1],
            ALU.mult, ALU.mult))
        ve_stats = p.val("ve")
        vals["ln1ve"] = ve_stats
        p.wait("scalar", "ve", ve_stats)
        p.inc("scalar", "se", 1, lambda e: e.activation(
            stats[:, 3:4], mv[:, 1:2], AF.Identity, bias=stats[:, 1:2],
            scale=1.0 / D))
        p.inc("scalar", "se", 1, lambda e: e.activation(
            r1a[:, :], stats[:, 3:4], AF.Ln, bias=epsv[:, :], scale=1.0))
        p.inc("scalar", "se", 1, lambda e: e.activation(
            r1[:, :], r1a[:, :], AF.Exp, bias=0.0, scale=-0.5))
        prev_se_r1 = p.val("se")
        p.wait("vector", "se", prev_se_r1)
        p.inc("vector", "ve", 1, lambda e: e.scalar_tensor_tensor(
            b2[:, :], stats[:, 0:1], -1.0, r1[:, :], ALU.mult, ALU.mult))
        ve_b2 = p.val("ve")
        p.wait("scalar", "ve", ve_b2)
        if t >= 2:
            p.wait("scalar", "pe", vals[f"tr{t-2}"])
        p.inc("scalar", "se", 1, lambda e, xb=xb, hb=hb: e.activation(
            hb[:, :], xb[:, :], AF.Identity, bias=b2[:, :], scale=r1[:, :]))
        prev_lnapply[t] = p.val("se")
        for rnd in range(2):
            ps = P2 if rnd == 0 else P1
            if t == 0 and rnd == 0:
                p.wait("tensor", "se", se_bcast)
                p.wait("tensor", "ve", ve_bcast)
            p.wait("tensor", "se", prev_lnapply[t])
            p.wait("tensor", "ve", prev_trev.get(t - 1, prev_trev[-1])
                   if rnd == 0 else vals.get(f"trevP1_{t-1}", 0))
            for jj in range(8):
                j = rnd * 8 + jj
                p.inc("tensor", "pe", 1,
                      lambda e, hb=hb, j=j, jj=jj, ps=ps: e.matmul(
                          ps[:, jj * 128:(jj + 1) * 128],
                          hb[:, j * 128:(j + 1) * 128], id_sb[:, :],
                          start=True, stop=True))
            pe_tr = p.val("pe")
            p.wait("vector", "pe", pe_tr)
            p.inc("vector", "ve", 1,
                  lambda e, t=t, rnd=rnd, ps=ps: e.tensor_copy(
                      xhat_fm[:, :].rearrange("p (j c) -> p j c", j=16)
                      [:, rnd * 8:(rnd + 1) * 8, t * 128:(t + 1) * 128],
                      ps[:, :].rearrange("p (j c) -> p j c", j=8)))
            if rnd == 0:
                prev_trev[t] = p.val("ve")
            else:
                vals[f"trevP1_{t}"] = p.val("ve")
        vals[f"tr{t}"] = p.val("pe")
    xhat_ve = p.val("ve")

    def proj_qk(which, wsem, bias_sb, base, wbuf, prevsec=None):
        for i, (oc, tb) in enumerate([(o, t2) for o in range(16)
                                      for t2 in range(2)]):
            bank, sb = banks4[i % 4], stg[i % 4]
            sem = f"sg{which}{i % 4}"
            p.wait("tensor", wsem, 128)
            if i == 0:
                p.wait("tensor", "ve", xhat_ve)
            if i >= 4:
                p.wait("tensor", sem, vals[f"{which}dma{i-4}"])
            elif prevsec:
                p.wait("tensor", f"sg{prevsec}{i % 4}",
                       vals[f"{prevsec}sgfull{i % 4}"])
            for j in range(16):
                p.inc("tensor", "pe", 1,
                      lambda e, j=j, oc=oc, tb=tb, bank=bank, wbuf=wbuf:
                      e.matmul(
                          bank, wbuf[:, j * D + oc * 128: j * D + (oc + 1) * 128],
                          xhat_fm[:, j * 1024 + tb * 512:
                                  j * 1024 + (tb + 1) * 512],
                          start=(j == 0), stop=(j == 15)))
            pe_here = p.val("pe")
            p.wait("scalar", "pe", pe_here)
            p.inc("scalar", "se", 1,
                  lambda e, bank=bank, sb=sb, oc=oc, bias_sb=bias_sb:
                  e.activation(sb[:, :], bank, AF.Identity,
                               bias=bias_sb[:, oc:oc + 1], scale=SC))
            p.dma("scalar", sem,
                  lambda e, sb=sb, oc=oc, tb=tb, base=base: e.dma_start(
                      out=cc1i[oc * 128 + base: oc * 128 + base + 128,
                               tb * 512:(tb + 1) * 512], in_=sb[:, :]))
            vals[f"{which}dma{i}"] = p.val(sem)
        for k_ in range(4):
            vals[f"{which}sgfull{k_}"] = p.val(f"sg{which}{k_}")

    for j in range(16):
        p.dma("gpsimd", "junk", lambda e, j=j: e.dma_start(
            out=wb2[:, j * D:(j + 1) * D], in_=wkT[j * 128:(j + 1) * 128, :]))
    p.fence("f_wk", lambda e: e.dma_start(out=glt[:, :], in_=xin[0:128, 0:16]))
    proj_qk("q", "f_wq", bqs, 0, wb)
    q_mms = p.val("pe")
    for k_ in range(4):
        p.wait("gpsimd", f"sgq{k_}", vals[f"qsgfull{k_}"])
    p.inc("gpsimd", "cc", 1, lambda e: e.collective_compute(
        "AllGather", ALU.bypass, ins=[cc1i[0:2048, :]], outs=[cc1oq[:, :]],
        replica_groups=GRPS))
    p.wait("gpsimd", "pe", q_mms)
    for j in range(16):
        p.dma("gpsimd", "junk", lambda e, j=j: e.dma_start(
            out=wb[:, j * D:(j + 1) * D], in_=wvT[j * 128:(j + 1) * 128, :]))
    p.fence("f_wv", lambda e: e.dma_start(out=glt[:, :], in_=xin[0:128, 0:16]))
    proj_qk("k", "f_wk", bks, 2048, wb2, prevsec="q")
    for k_ in range(4):
        p.wait("gpsimd", f"sgk{k_}", vals[f"ksgfull{k_}"])
    p.inc("gpsimd", "cc", 1, lambda e: e.collective_compute(
        "AllGather", ALU.bypass, ins=[cc1i[2048:4096, :]], outs=[cc1ok[:, :]],
        replica_groups=GRPS))

    for i, (tt, oc4) in enumerate([(t2, o) for t2 in range(8)
                                   for o in range(4)]):
        bank, sb = banks4[i % 4], stg[i % 4]
        sem = f"sgv{i % 4}"
        p.wait("tensor", "f_wv", 128)
        if i >= 4:
            p.wait("tensor", sem, vals[f"vdma{i-4}"])
        else:
            p.wait("tensor", f"sgk{i % 4}", vals[f"ksgfull{i % 4}"])
        for j in range(16):
            p.inc("tensor", "pe", 1,
                  lambda e, j=j, tt=tt, oc4=oc4, bank=bank: e.matmul(
                      bank,
                      xhat_fm[:, j * 1024 + tt * 128: j * 1024 + (tt + 1) * 128],
                      wb[:, j * D + oc4 * 512: j * D + (oc4 + 1) * 512],
                      start=(j == 0), stop=False))
        p.inc("tensor", "pe", 1, lambda e, oc4=oc4, bank=bank: e.matmul(
            bank, onr[:, :], bvr[0:1, oc4 * 512:(oc4 + 1) * 512],
            start=False, stop=True))
        pe_here = p.val("pe")
        p.wait("scalar", "pe", pe_here)
        p.inc("scalar", "se", 1, lambda e, bank=bank, sb=sb: e.activation(
            sb[:, :], bank, AF.Identity))
        p.dma("scalar", sem, lambda e, sb=sb, tt=tt, oc4=oc4: e.dma_start(
            out=cc1i[4096:6144, tt * 128:(tt + 1) * 128]
            .rearrange("(i q) c -> q i c", q=128)
            [:, 4 * oc4:4 * oc4 + 4, :],
            in_=sb[:, :].rearrange("p (i c) -> p i c", i=4)))
        vals[f"vdma{i}"] = p.val(sem)

    for k_ in range(4):
        p.wait("gpsimd", f"sgv{k_}", p.val(f"sgv{k_}"))
    p.inc("gpsimd", "cc", 1, lambda e: e.collective_compute(
        "AllGather", ALU.bypass, ins=[cc1i[4096:6144, :]], outs=[cc1ov[:, :]],
        replica_groups=GRPS))

    p.wait("gpsimd", "ld", start_ld)
    gstep = 0
    exp_hist = {}
    ov_hist = {}
    pts_hist = {}
    secs = (cc1oq, cc1ok, cc1ov)

    def emit_fin_l(fin):
        p.wait("tensor", "ve", pts_hist[fin["gend"] - 1])
        p.wait("tensor", "se", vals.get("se_lbank_free", 0))
        pts = fin["pts"]
        p.inc("tensor", "pe", 1, lambda e, pts=pts: e.matmul(
            L0[0:1, :], oncf[:, :], pts[:, 0:512], start=True, stop=True))
        p.inc("tensor", "pe", 1, lambda e, pts=pts: e.matmul(
            L1[0:1, :], oncf[:, :], pts[:, 512:1024], start=True, stop=True))
        vals[f"ptsum_free{fin['qc'] % 2}"] = p.val("pe")
        pe_qc = p.val("pe")
        p.wait("scalar", "pe", pe_qc)
        p.inc("scalar", "se", 1, lambda e: e.activation(
            lsb[0:1, 0:512], L0[0:1, :], AF.Ln, scale=C24))
        p.inc("scalar", "se", 1, lambda e: e.activation(
            lsb[0:1, 512:1024], L1[0:1, :], AF.Ln, scale=C24))
        p.inc("scalar", "se", 1, lambda e: e.activation(
            lsb2[0:1, :], lsb[0:1, :], AF.Exp, bias=lnc[0:1, :], scale=-1.0))
        vals["se_l_last"] = p.val("se")

    def emit_fin_osb(fin):
        u, qc = fin["u"], fin["qc"]
        p.wait("tensor", "se", vals["se_l_last"])
        p.inc("tensor", "pe", 1, lambda e: e.matmul(
            L0[:, :], onrf[0:1, :], lsb2[0:1, 0:512], start=True, stop=True))
        p.inc("tensor", "pe", 1, lambda e: e.matmul(
            L1[:, :], onrf[0:1, :], lsb2[0:1, 512:1024], start=True, stop=True))
        pe_l = p.val("pe")
        p.wait("scalar", "pe", pe_l)
        p.inc("scalar", "se", 1, lambda e: e.copy(linv_bc[:, 0:512], L0[:, :]))
        p.inc("scalar", "se", 1, lambda e: e.copy(
            linv_bc[:, 512:1024], L1[:, :]))
        se_linv = p.val("se")
        vals["se_lbank_free"] = se_linv
        p.wait("vector", "se", se_linv)
        p.wait("vector", f"ob{u}", p.val(f"ob{u}"))
        p.inc("vector", "ve", 1, lambda e: e.tensor_tensor(
            osb[:, :], praw[:, :], linv_bc[:, :], ALU.mult))
        ve_o = p.val("ve")
        p.wait("scalar", "ve", ve_o)
        p.dma("scalar", f"ob{u}", lambda e, u=u, qc=qc: e.dma_start(
            out=cc2i[u * 128:(u + 1) * 128, qc * 1024:(qc + 1) * 1024],
            in_=osb[:, :]))

    def emit_fin_praw(fin):
        p.wait("vector", "pe", ov_hist[fin["gend"] - 1])
        p.inc("vector", "ve", 1, lambda e: e.tensor_copy(
            praw[:, :], P2[:, :]))
        fin["ve_praw"] = p.val("ve")

    def emit_ag2(u):
        p.wait("gpsimd", f"ob{u}", 64)
        p.inc("gpsimd", "cc", 1, lambda e, u=u: e.collective_compute(
            "AllGather", ALU.bypass, ins=[cc2i[u * 128:(u + 1) * 128, :]],
            outs=[cc2o[u * 1024:(u + 1) * 1024, :]],
            replica_groups=GRPS))

    def emit_opiece(pu):
        p.wait("gpsimd", "cc", 4 + pu)
        for h in range(pu, 16, 4):
            p.dma("gpsimd", "junk", lambda e, h=h: e.indirect_dma_start(
                oall[:, h * 1024:(h + 1) * 1024], None,
                cc2o[:, :].rearrange("a (b c) -> (a b) c", c=1024),
                IndirectOffsetOnAxis(idxt_sb[:, 48 + h:49 + h], 0)))

    for u in range(4):
        qkvu = qkvs[u % 2]
        qb = qkvu[:, 0:4096]
        kb = qkvu[:, 4096:8192]
        vb = qkvu[:, 8192:12288]
        if u == 1:
            for j in range(16):
                p.dma("gpsimd", "junk", lambda e, j=j: e.dma_start(
                    out=wb[:, j * D:(j + 1) * D],
                    in_=woT[j * 128:(j + 1) * 128, :]))
            p.fence("f_wo", lambda e: e.dma_start(
                out=glt[:, :], in_=xin[0:128, 0:16]))
        if u >= 2:
            emit_ag2(u - 2)
            emit_opiece(u - 2)
            p.wait("gpsimd", "pe", vals[f"unitpe{u-2}"])
            p.wait("gpsimd", "se", vals[f"unitse{u-2}"])
        if u == 0:
            p.wait("gpsimd", "cc", 2)
        for s_ in range(4):
            for r_ in range(2):
                p.dma("gpsimd", "junk", lambda e, qkvu=qkvu, u=u, s_=s_, r_=r_:
                      e.indirect_dma_start(
                    qkvu[:, r_ * 4096 + s_ * 1024: r_ * 4096 + (s_ + 1) * 1024],
                    None, secs[r_][:, :],
                    IndirectOffsetOnAxis(
                        idxt_sb[:, u * 12 + s_ * 3 + r_:
                                u * 12 + s_ * 3 + r_ + 1], 0)))
        p.fence(f"f_uqk{u}", lambda e: e.dma_start(
            out=glt[:, :], in_=cc1oq[0:128, 0:16]))
        if u == 0:
            p.wait("gpsimd", "cc", 3)
        for s_ in range(4):
            p.dma("gpsimd", "junk", lambda e, qkvu=qkvu, u=u, s_=s_:
                  e.indirect_dma_start(
                qkvu[:, 8192 + s_ * 1024: 8192 + (s_ + 1) * 1024],
                None, secs[2][:, :],
                IndirectOffsetOnAxis(
                    idxt_sb[:, u * 12 + s_ * 3 + 2:
                            u * 12 + s_ * 3 + 3], 0)))
        p.fence(f"f_uv{u}", lambda e: e.dma_start(
            out=glt[:, :], in_=cc1oq[0:128, 0:16]))

        p.wait("scalar", f"f_uqk{u}", 128)
        if u > 0:
            p.wait("vector", "se", vals[f"unitse{u-1}"])
        for w, (src, slot) in enumerate(((qb, 0), (kb, 1))):
            for c8 in range(8):
                if c8 > 0 or w > 0:
                    p.wait("scalar", "pe", p.val("pe"))
                p.inc("scalar", "se", 1, lambda e, src=src, c8=c8: e.activation(
                    qsq[:, :], src[:, c8 * 512:(c8 + 1) * 512], AF.Square))
                se_h = p.val("se")
                p.wait("tensor", "se", se_h)
                p.wait("tensor", "ve", p.val("ve"))
                p.inc("tensor", "pe", 1, lambda e: e.matmul(
                    L0[0:1, :], onc[:, :], qsq[:, :], start=True, stop=True))
                pe_h = p.val("pe")
                p.wait("vector", "pe", pe_h)
                p.inc("vector", "ve", 1,
                      lambda e, w=w, c8=c8: e.reduce_max(
                          cmx[0:1, w * 8 + c8:w * 8 + c8 + 1],
                          L0[0:1, :], axis=AX.X))
        p.wait("scalar", "ve", p.val("ve"))
        p.inc("scalar", "se", 1, lambda e: e.copy(
            smax[0:1, 7:8], scl[0:1, 0:1]))
        p.wait("vector", "se", p.val("se"))
        p.inc("vector", "ve", 1, lambda e: e.reduce_max(
            smax[0:1, 0:1], cmx[0:1, 0:8], axis=AX.X))
        p.inc("vector", "ve", 1, lambda e: e.reduce_max(
            smax[0:1, 1:2], cmx[0:1, 8:16], axis=AX.X))
        p.wait("scalar", "ve", p.val("ve"))
        p.inc("scalar", "se", 1, lambda e: e.activation(
            smax[0:1, 2:3], smax[0:1, 0:1], AF.Ln))
        p.inc("scalar", "se", 1, lambda e: e.activation(
            smax[0:1, 5:6], smax[0:1, 1:2], AF.Ln))
        p.wait("vector", "se", p.val("se"))
        p.inc("vector", "ve", 1, lambda e: e.tensor_tensor(
            smax[0:1, 3:4], smax[0:1, 2:3], smax[0:1, 5:6], ALU.add))
        p.wait("scalar", "ve", p.val("ve"))
        p.inc("scalar", "se", 1, lambda e: e.activation(
            smax[0:1, 6:7], smax[0:1, 3:4], AF.Exp, scale=0.5))
        se_bnd = p.val("se")
        p.wait("vector", "se", se_bnd)
        p.inc("vector", "ve", 1, lambda e: e.tensor_scalar(
            smax[0:1, 4:5], smax[0:1, 6:7], -1.0, None, ALU.mult))
        ve_nb = p.val("ve")
        p.wait("tensor", "ve", ve_nb)
        p.inc("tensor", "pe", 1, lambda e: e.matmul(
            P2[:, 0:1], onrf[0:1, :], smax[0:1, 4:5], start=True, stop=True))
        pe_nb = p.val("pe")
        p.wait("scalar", "pe", pe_nb)
        p.inc("scalar", "se", 1, lambda e: e.copy(nb_sb[:, :], P2[:, 0:1]))
        se_nb = p.val("se")
        p.wait("tensor", f"f_uqk{u}", 128)
        p.wait("tensor", f"f_uv{u}", 128)

        fin = None
        for qc in range(4):
            qcb = qc * 1024
            nkt = 8 * qc + 8
            pend = None
            for kt in range(nkt):
                w0 = max(kt * 128 - qcb, 0)
                si = gstep % 2
                stp, ptb = (P0, pt_sb[0]) if si == 0 else (P1, pt_sb[1])
                if fin is not None:
                    if kt == 0:
                        emit_fin_praw(fin)
                    elif kt == 1:
                        emit_fin_l(fin)
                        p.wait("tensor", "ve", fin["ve_praw"])
                    elif kt == 2:
                        emit_fin_osb(fin)
                        fin = None
                if gstep - 2 in exp_hist:
                    p.wait("tensor", "se", exp_hist[gstep - 2])
                for a in ((w0, 512), (512, 1024)):
                    a0, b0 = max(a[0], w0), a[1]
                    if b0 <= w0 or b0 <= a0:
                        continue
                    p.inc("tensor", "pe", 1,
                          lambda e, kt=kt, a0=a0, b0=b0, qcb=qcb, stp=stp,
                          kb=kb, qb=qb:
                          e.matmul(stp[:, a0:b0],
                                   kb[:, kt * 128:(kt + 1) * 128],
                                   qb[:, qcb + a0:qcb + b0],
                                   start=True, stop=True))
                qk_here = p.val("pe")
                masked = kt * 128 >= qcb
                if masked:
                    p.wait("vector", "pe", qk_here)
                    p.inc("vector", "ve", 1,
                          lambda e, stp=stp, w0=w0: e.tensor_tensor(
                              stp[:, w0:w0 + 128], stp[:, w0:w0 + 128],
                              maskd[:, :], ALU.add))
                    mask_here = p.val("ve")
                p.wait("scalar", "pe", qk_here)
                if masked:
                    p.wait("scalar", "ve", mask_here)
                if gstep - 2 in ov_hist:
                    p.wait("scalar", "pe", ov_hist[gstep - 2])
                if gstep - 2 in pts_hist:
                    p.wait("scalar", "ve", pts_hist[gstep - 2])
                p.inc("scalar", "se", 1,
                      lambda e, stp=stp, ptb=ptb, w0=w0: e.activation(
                          ptb[:, w0:1024], stp[:, w0:1024], AF.Exp,
                          bias=nb_sb[:, :], scale=1.0))
                exp_hist[gstep] = p.val("se")
                p.wait("vector", "se", exp_hist[gstep])
                pts = ptsums[qc % 2]
                if kt == 0:
                    p.wait("vector", "pe", vals.get(f"ptsum_free{qc % 2}", 0))
                    p.inc("vector", "ve", 1,
                          lambda e, ptb=ptb, pts=pts: e.tensor_copy(
                              pts[:, :], ptb[:, :]))
                else:
                    p.inc("vector", "ve", 1,
                          lambda e, ptb=ptb, w0=w0, pts=pts: e.tensor_tensor(
                              pts[:, w0:1024], pts[:, w0:1024],
                              ptb[:, w0:1024], ALU.add))
                pts_hist[gstep] = p.val("ve")
                if pend is not None:
                    emit_ov(p, pend, P2, L0, L1, vb, onc, pt_sb,
                            exp_hist, ov_hist)
                pend = (kt, w0, si, gstep, kt == 0, False)
                gstep += 1
            pend = pend[:5] + (True,)
            emit_ov(p, pend, P2, L0, L1, vb, onc, pt_sb, exp_hist, ov_hist)
            pend = None
            fin = {"gend": gstep, "pts": ptsums[qc % 2], "u": u, "qc": qc}
        emit_fin_praw(fin)
        emit_fin_l(fin)
        p.wait("tensor", "ve", fin["ve_praw"])
        emit_fin_osb(fin)
        fin = None
        vals[f"unitpe{u}"] = p.val("pe")
        vals[f"unitse{u}"] = p.val("se")
        vals[f"odma{u}"] = p.val(f"ob{u}")

    emit_ag2(2)
    emit_opiece(2)
    emit_ag2(3)
    emit_opiece(3)
    p.fence("f_og", lambda e: e.dma_start(out=glt[:, :], in_=cc2o[0:128, 0:16]))

    p.wait("sync", "cc", 7)
    for i, (tt, oc4) in enumerate([(t2, o) for t2 in range(8)
                                   for o in range(4)]):
        bank = banks4[i % 4]
        if oc4 == 0:
            xb = xts[tt % 2]
            if tt >= 2:
                p.wait("sync", "ve", vals[f"resid{tt-2}_3"])
            p.dma("sync", f"xr{tt % 2}", lambda e, tt=tt, xb=xb: e.dma_start(
                out=xb[:, :], in_=xin[tt * 128:(tt + 1) * 128, :]))
            vals[f"xl{tt}"] = p.val(f"xr{tt % 2}")
        p.wait("tensor", "f_wo", 128)
        if i == 0:
            p.wait("tensor", "f_og", 128)
        if i >= 4:
            p.wait("tensor", "ve", vals[f"resid{(i-4)//4}_{(i-4)%4}"])
        for j in range(16):
            p.inc("tensor", "pe", 1,
                  lambda e, j=j, tt=tt, oc4=oc4, bank=bank: e.matmul(
                      bank,
                      oall[:, j * 1024 + tt * 128: j * 1024 + (tt + 1) * 128],
                      wb[:, j * D + oc4 * 512: j * D + (oc4 + 1) * 512],
                      start=(j == 0), stop=False))
        p.inc("tensor", "pe", 1, lambda e, oc4=oc4, bank=bank: e.matmul(
            bank, onr[:, :], bor[0:1, oc4 * 512:(oc4 + 1) * 512],
            start=False, stop=True))
        pe_here = p.val("pe")
        p.wait("vector", "pe", pe_here)
        p.wait("vector", f"xr{tt % 2}", vals[f"xl{tt}"])
        p.inc("vector", "ve", 1,
              lambda e, tt=tt, oc4=oc4, bank=bank, xb=xts[tt % 2]:
              e.tensor_tensor(
                  x1[:, tt * D + oc4 * 512: tt * D + (oc4 + 1) * 512],
                  bank, xb[:, oc4 * 512:(oc4 + 1) * 512], ALU.add))
        vals[f"resid{tt}_{oc4}"] = p.val("ve")
    op_mms = p.val("pe")

    p.wait("sync", "pe", op_mms)
    for j in range(16):
        p.dma("sync", "wfl", lambda e, j=j: e.dma_start(
            out=wf_sb[:, j * D:(j + 1) * D], in_=wfT[j * 128:(j + 1) * 128, :]))
    wf_done = p.val("wfl")

    def emit_ffn_tile(tt):
        for oc4 in range(4):
            i = tt * 4 + oc4
            bank = banks4[i % 4]
            p.wait("tensor", "se", vals[f"q8done{tt}"])
            if i >= 4:
                p.wait("tensor", f"sgy{(i - 4) % 4}", vals[f"ydma{i-4}"])
            for j in range(16):
                p.inc("tensor", "pe", 1,
                      lambda e, j=j, tt=tt, oc4=oc4, bank=bank: e.matmul(
                          bank,
                          xq_fm[:, j * 1024 + tt * 128: j * 1024 + (tt + 1) * 128],
                          wf_sb[:, j * D + oc4 * 512: j * D + (oc4 + 1) * 512],
                          start=(j == 0), stop=(j == 15)))
            pe_here = p.val("pe")
            p.wait("vector", "pe", pe_here)
            p.inc("vector", "ve", 1,
                  lambda e, tt=tt, oc4=oc4, bank=bank: e.scalar_tensor_tensor(
                      ysb[:, oc4 * 512:(oc4 + 1) * 512], bank, wssc_bc[:, 0:1],
                      x1[:, tt * D + oc4 * 512: tt * D + (oc4 + 1) * 512],
                      ALU.mult, ALU.add))
            ve_y = p.val("ve")
            p.wait("scalar", "ve", ve_y)
            p.dma("scalar", f"sgy{i % 4}", lambda e, tt=tt, oc4=oc4: e.dma_start(
                out=yout[tt * 128:(tt + 1) * 128, oc4 * 512:(oc4 + 1) * 512],
                in_=ysb[:, oc4 * 512:(oc4 + 1) * 512]))
            vals[f"ydma{i}"] = p.val(f"sgy{i % 4}")

    prev_se_r1d = 0
    vals["trD-1"] = 0
    for t in range(8):
        p.wait("vector", "ve", vals[f"resid{t}_3"])
        if t >= 1:
            p.wait("vector", "se", prev_se_r1d)
        p.wait("vector", "pe", vals[f"trD{t-1}"])
        p.wait("scalar", "ve", vals[f"resid{t}_3"])
        if t >= 1:
            p.wait("scalar", "ve", vals[f"ln2ve{t-1}"])
        p.inc("scalar", "se", 1, lambda e, t=t: e.activation(
            sqtsd[:, :], x1[:, t * D:(t + 1) * D], AF.Square))
        se_sq = p.val("se")
        p.wait("vector", "se", se_sq)
        p.inc("vector", "ve", 1, lambda e, t=t: e.reduce_sum(
            mv[:, 0:1], x1[:, t * D:(t + 1) * D], axis=AX.X))
        p.inc("vector", "ve", 1, lambda e: e.reduce_sum(
            mv[:, 1:2], sqtsd[:, :], axis=AX.X))
        vals[f"ln2ve{t}"] = p.val("ve")
        p.inc("vector", "ve", 1, lambda e: e.tensor_scalar(
            stats[:, 0:1], mv[:, 0:1], 1.0 / D, None, ALU.mult))
        p.inc("vector", "ve", 1, lambda e: e.scalar_tensor_tensor(
            stats[:, 1:2], mv[:, 0:1], -1.0 / (D * D), mv[:, 0:1],
            ALU.mult, ALU.mult))
        ve_stats = p.val("ve")
        p.wait("scalar", "ve", ve_stats)
        p.inc("scalar", "se", 1, lambda e: e.activation(
            stats[:, 3:4], mv[:, 1:2], AF.Identity, bias=stats[:, 1:2],
            scale=1.0 / D))
        p.inc("scalar", "se", 1, lambda e: e.activation(
            r1a[:, :], stats[:, 3:4], AF.Ln, bias=epsv[:, :], scale=1.0))
        p.inc("scalar", "se", 1, lambda e: e.activation(
            r1[:, :], r1a[:, :], AF.Exp, bias=0.0, scale=-0.5))
        prev_se_r1d = p.val("se")
        p.wait("vector", "se", prev_se_r1d)
        p.inc("vector", "ve", 1, lambda e: e.scalar_tensor_tensor(
            b2[:, :], stats[:, 0:1], -1.0, r1[:, :], ALU.mult, ALU.mult))
        ve_b2 = p.val("ve")
        p.wait("scalar", "ve", ve_b2)
        p.inc("scalar", "se", 1, lambda e, t=t: e.activation(
            xh2[:, :], x1[:, t * D:(t + 1) * D], AF.Identity,
            bias=b2[:, :], scale=r1[:, :]))
        se_xh2 = p.val("se")
        p.wait("vector", "se", se_xh2)
        if t == 0:
            p.wait("vector", "se", se_bcast)
        p.inc("vector", "ve", 1, lambda e: e.tensor_scalar(
            tq[:, :], xh2[:, :], rinv_bc[:, 0:1], None, ALU.mult))
        p.inc("vector", "ve", 1, lambda e: e.tensor_scalar(
            q8[:, :], tq[:, :], 448.0, -448.0, ALU.min, ALU.max))
        ve_q8 = p.val("ve")
        p.wait("tensor", "ve", ve_q8)
        if t == 0:
            p.wait("tensor", "wfl", wf_done)
        p.wait("tensor", "se", vals.get(f"evP2_{t-1}", 0))
        for jj in range(8):
            p.inc("tensor", "pe", 1,
                  lambda e, jj=jj: e.matmul(
                      P2[:, jj * 128:(jj + 1) * 128],
                      q8[:, jj * 128:(jj + 1) * 128], id_sb[:, :],
                      start=True, stop=True))
        pe_tr0 = p.val("pe")
        p.wait("tensor", "se", vals.get(f"evL_{t-1}", 0))
        for jj in range(8):
            j = 8 + jj
            lt = L0 if jj < 4 else L1
            p.inc("tensor", "pe", 1,
                  lambda e, j=j, jj=jj, lt=lt: e.matmul(
                      lt[:, (jj % 4) * 128:(jj % 4 + 1) * 128],
                      q8[:, j * 128:(j + 1) * 128], id_sb[:, :],
                      start=True, stop=True))
        pe_tr1 = p.val("pe")
        p.wait("scalar", "pe", pe_tr0)
        p.inc("scalar", "se", 1,
              lambda e, t=t: e.copy(
                  xq_fm[:, :].rearrange("p (j c) -> p j c", j=16)
                  [:, 0:8, t * 128:(t + 1) * 128],
                  P2[:, :].rearrange("p (j c) -> p j c", j=8)))
        vals[f"evP2_{t}"] = p.val("se")
        p.wait("scalar", "pe", pe_tr1)
        p.inc("scalar", "se", 1,
              lambda e, t=t: e.copy(
                  xq_fm[:, :].rearrange("p (j c) -> p j c", j=16)
                  [:, 8:12, t * 128:(t + 1) * 128],
                  L0[:, :].rearrange("p (j c) -> p j c", j=4)))
        p.inc("scalar", "se", 1,
              lambda e, t=t: e.copy(
                  xq_fm[:, :].rearrange("p (j c) -> p j c", j=16)
                  [:, 12:16, t * 128:(t + 1) * 128],
                  L1[:, :].rearrange("p (j c) -> p j c", j=4)))
        vals[f"evL_{t}"] = p.val("se")
        vals[f"q8done{t}"] = p.val("se")
        vals[f"trD{t}"] = p.val("pe")
        if t >= 1:
            emit_ffn_tile(t - 1)
    emit_ffn_tile(7)

    for k_ in range(4):
        p.wait("scalar", f"sgy{k_}")
    p.wait("scalar", "st")

    with nc.Block() as block:
        @block.sync
        def _(e):
            for f in p.ops["sync"]:
                f(e)

        @block.gpsimd
        def _(e):
            for f in p.ops["gpsimd"]:
                f(e)

        @block.vector
        def _(e):
            for f in p.ops["vector"]:
                f(e)

        @block.scalar
        def _(e):
            for f in p.ops["scalar"]:
                f(e)

        @block.tensor
        def _(e):
            for f in p.ops["tensor"]:
                f(e)

    return nc


def emit_ov(p, pend, P2, L0, L1, vbh, onc, pt_sb, exp_hist, ov_hist):
    kt, w0, si, g, is_start, is_stop = pend
    ptb = pt_sb[si]
    p.wait("tensor", "se", exp_hist[g])
    for a in ((w0, 512), (512, 1024)):
        a0, b0 = max(a[0], w0), a[1]
        if b0 <= w0 or b0 <= a0:
            continue
        p.inc("tensor", "pe", 1,
              lambda e, kt=kt, a0=a0, b0=b0, ptb=ptb, st_=is_start,
              sp_=is_stop: e.matmul(
                  P2[:, a0:b0], vbh[:, kt * 128:(kt + 1) * 128],
                  ptb[:, a0:b0], start=st_, stop=sp_))
    ov_hist[g] = p.val("pe")


_NC_CACHE = []


def kernel(L0_attn_X, L0_attn_Wq, L0_attn_bq, L0_attn_Wk, L0_attn_bk,
           L0_attn_Wv, L0_attn_bv, L0_attn_Wo, L0_attn_bo, L0_ffn_weight,
           L0_ffn_weight_scale, L0_ffn_input_scale):
    X = np.ascontiguousarray(np.asarray(L0_attn_X, dtype=np.float32)
                             .reshape(B * S, D))
    wqTh = np.ascontiguousarray(np.asarray(L0_attn_Wq, dtype=np.float32).T)
    wkTh = np.ascontiguousarray(np.asarray(L0_attn_Wk, dtype=np.float32).T)
    wvTh = np.ascontiguousarray(np.asarray(L0_attn_Wv, dtype=np.float32).T)
    woTh = np.ascontiguousarray(np.asarray(L0_attn_Wo, dtype=np.float32).T)
    wf = np.asarray(L0_ffn_weight)
    wfTh = np.ascontiguousarray(wf.view(ml_dtypes.float8_e4m3).T)
    bqh = np.ascontiguousarray(
        np.asarray(L0_attn_bq, np.float32).reshape(16, 128).T)
    bkh = np.ascontiguousarray(
        np.asarray(L0_attn_bk, np.float32).reshape(16, 128).T)
    bvh = np.ascontiguousarray(np.asarray(L0_attn_bv, np.float32).reshape(1, D))
    boh = np.ascontiguousarray(np.asarray(L0_attn_bo, np.float32).reshape(1, D))
    wsch = np.asarray(L0_ffn_weight_scale, np.float32).reshape(1, 1)
    isch = np.asarray(L0_ffn_input_scale).reshape(1, 1)
    if isch.dtype != ml_dtypes.bfloat16:
        isch = isch.view(ml_dtypes.bfloat16) if isch.dtype.itemsize == 2 \
            else isch.astype(ml_dtypes.bfloat16)

    ident = np.eye(128, dtype=ml_dtypes.bfloat16)
    onc_h = np.ones((128, 1), dtype=ml_dtypes.bfloat16)
    onr_h = np.ones((1, 128), dtype=ml_dtypes.bfloat16)
    onrf_h = np.ones((1, 128), dtype=np.float32)
    kk, qq = np.meshgrid(np.arange(128), np.arange(128), indexing="ij")
    maskd_h = np.where(kk <= qq, 0.0, NEG).astype(np.float32)

    in_maps = []
    for c in range(NC):
        idxt_h = np.zeros((128, 64), dtype=np.int32)
        pp = np.arange(128, dtype=np.int32)
        for u in range(4):
            g = 4 * c + u
            h = g % 16
            for s_ in range(4):
                for r_ in range(3):
                    idxt_h[:, u * 12 + s_ * 3 + r_] = \
                        (4 * (c // 4) + s_) * 2048 + h * 128 + pp
        for h in range(16):
            orow = 1024 * (h % 4) + (4 * (c // 4) + h // 4) * 128 + pp
            idxt_h[:, 48 + h] = orow * 4 + (c % 4)
        in_maps.append({
            "xin": X[c * T:(c + 1) * T], "wqT": wqTh, "wkT": wkTh,
            "wvT": wvTh, "woT": woTh, "wfT": wfTh, "bqi": bqh, "bki": bkh,
            "bvi": bvh, "boi": boh, "wsc": wsch, "isc": isch,
            "id_in": ident, "onc_in": onc_h, "onr_in": onr_h, "onrf_in": onrf_h,
            "mask_in": maskd_h, "idxt": idxt_h,
        })

    if not _NC_CACHE:
        nc = bass.Bass("TRN2", target_bir_lowering=False)
        build(nc)
        _NC_CACHE.append(nc)
    nc = _NC_CACHE[0]
    res = run_bass_kernel_spmd(nc, in_maps, core_ids=list(range(NC)))
    out = np.concatenate([res.results[c]["yout"] for c in range(NC)], axis=0)
    return out.reshape(B, S, D)


# version 23
# speedup vs baseline: 1.0596x; 1.0405x over previous
import numpy as np
import ml_dtypes
import concourse.bass as bass
import concourse.mybir as mybir
from concourse.bass import IndirectOffsetOnAxis
from concourse.bass_utils import run_bass_kernel_spmd

F32 = mybir.dt.float32
BF16 = mybir.dt.bfloat16
FP8 = mybir.dt.float8e4
U32 = mybir.dt.uint32
I32 = mybir.dt.int32
ALU = mybir.AluOpType
AF = mybir.ActivationFunctionType
AX = mybir.AxisListType

D = 2048
S = 4096
B = 2
T = 1024
NC = 8
SC = float(128.0 ** -0.25)
EPS = 1e-5
NEG = -30.0
C24 = float(np.exp(24.0))
GRPS = [list(range(8))]
SH = 6144


class Prog:

    def __init__(self, nc):
        self.nc = nc
        self.ops = {"sync": [], "gpsimd": [], "vector": [], "scalar": [],
                    "tensor": []}
        self.sem = {}
        self.cnt = {}

    def mksem(self, name):
        self.sem[name] = self.nc.alloc_semaphore(name)
        self.cnt[name] = 0

    def emit(self, eng, fn):
        self.ops[eng].append(fn)

    def inc(self, eng, sem, n, fn):
        self.cnt[sem] += n
        h = self.sem[sem]
        self.ops[eng].append(lambda e, fn=fn, h=h, n=n: fn(e).then_inc(h, n))

    def dma(self, eng, sem, fn):
        self.inc(eng, sem, 16, fn)

    def wait(self, eng, sem, upto=None):
        v = self.cnt[sem] if upto is None else upto
        if v <= 0:
            return
        h = self.sem[sem]
        self.ops[eng].append(lambda e, h=h, v=v: e.wait_ge(h, v))

    def val(self, sem):
        return self.cnt[sem]

    def fence(self, name, fn):
        self.mksem(name)
        for _ in range(8):
            self.dma("gpsimd", name, fn)


def build(nc):
    xin = nc.dram_tensor("xin", [T, D], F32, kind="ExternalInput").ap()
    wqT = nc.dram_tensor("wqT", [D, D], F32, kind="ExternalInput").ap()
    wkT = nc.dram_tensor("wkT", [D, D], F32, kind="ExternalInput").ap()
    wvT = nc.dram_tensor("wvT", [D, D], F32, kind="ExternalInput").ap()
    woT = nc.dram_tensor("woT", [D, D], F32, kind="ExternalInput").ap()
    wfT = nc.dram_tensor("wfT", [D, D], FP8, kind="ExternalInput").ap()
    bqi = nc.dram_tensor("bqi", [128, 16], F32, kind="ExternalInput").ap()
    bki = nc.dram_tensor("bki", [128, 16], F32, kind="ExternalInput").ap()
    bvi = nc.dram_tensor("bvi", [1, D], F32, kind="ExternalInput").ap()
    boi = nc.dram_tensor("boi", [1, D], F32, kind="ExternalInput").ap()
    wsc = nc.dram_tensor("wsc", [1, 1], F32, kind="ExternalInput").ap()
    isc = nc.dram_tensor("isc", [1, 1], BF16, kind="ExternalInput").ap()
    id_in = nc.dram_tensor("id_in", [128, 128], BF16, kind="ExternalInput").ap()
    onc_in = nc.dram_tensor("onc_in", [128, 1], BF16, kind="ExternalInput").ap()
    onr_in = nc.dram_tensor("onr_in", [1, 128], BF16, kind="ExternalInput").ap()
    onrf_in = nc.dram_tensor("onrf_in", [1, 128], F32, kind="ExternalInput").ap()
    mask_in = nc.dram_tensor("mask_in", [128, 128], F32, kind="ExternalInput").ap()
    idxt = nc.dram_tensor("idxt", [128, 64], I32, kind="ExternalInput").ap()
    yout = nc.dram_tensor("yout", [T, D], F32, kind="ExternalOutput").ap()

    cc1i = nc.dram_tensor("cc1i", [SH, 1024], BF16, kind="Internal").ap()
    cc1oq = nc.dram_tensor("cc1oq", [16384, 1024], BF16, kind="Internal",
                           addr_space="Shared").ap()
    cc1ok = nc.dram_tensor("cc1ok", [16384, 1024], BF16, kind="Internal",
                           addr_space="Shared").ap()
    cc1ov = nc.dram_tensor("cc1ov", [16384, 1024], BF16, kind="Internal",
                           addr_space="Shared").ap()
    cc2i = nc.dram_tensor("cc2i", [512, 4096], BF16, kind="Internal").ap()
    cc2o = nc.dram_tensor("cc2o", [4096, 4096], BF16, kind="Internal",
                          addr_space="Shared").ap()

    A = nc.alloc_sbuf_tensor
    idxt_sb = A("idxt_sb", [128, 64], I32)
    id_sb = A("id_sb", [128, 128], BF16)
    onc = A("onc", [128, 1], BF16)
    onr = A("onr", [1, 128], BF16)
    onrf = A("onrf", [1, 128], F32)
    maskd = A("maskd", [128, 128], F32)
    bqs = A("bqs", [128, 16], F32)
    bks = A("bks", [128, 16], F32)
    bvr = A("bvr", [1, D], BF16)
    bor = A("bor", [1, D], BF16)
    scl = A("scl", [1, 8], F32)
    iscb = A("iscb", [1, 1], BF16)
    rinv_bc = A("rinv_bc", [128, 1], F32)
    wssc_bc = A("wssc_bc", [128, 1], F32)
    nb_sb = A("nb_sb", [128, 1], F32)
    mv = A("mv", [128, 2], F32)
    stats = A("stats", [128, 24], F32)
    r1 = A("r1", [128, 1], F32)
    b2 = A("b2", [128, 1], F32)
    smax = A("smax", [1, 8], F32)
    cmx = A("cmx", [1, 16], F32)
    epsv = A("epsv", [128, 1], F32)
    oncf = A("oncf", [128, 1], F32)
    lsb = A("lsb", [1, 1024], F32)
    lnc = A("lnc", [1, 1], F32)
    lsb2 = A("lsb2", [1, 1024], F32)
    r1a = A("r1a", [128, 1], F32)
    glt = A("glt", [128, 16], BF16)

    xts = [A(f"xts{i}", [128, D], F32) for i in range(2)]
    xh_ts = [A(f"xh{i}", [128, D], BF16) for i in range(2)]
    stg = [A(f"stg{i}", [128, 512], BF16) for i in range(4)]

    xhat_fm = A("xhat_fm", [128, 16 * 1024], BF16)
    a_xf = nc.lookup_mloc(xhat_fm).addr
    oall = nc.alloc_sbuf_tensor_at("oall", [128, 16 * 1024], BF16, offset=a_xf)

    wb = A("wb", [128, 16 * D], BF16)
    a_wb = nc.lookup_mloc(wb).addr
    wf_sb = nc.alloc_sbuf_tensor_at("wf_sb", [128, 16 * D], FP8, offset=a_wb)
    xq_fm = nc.alloc_sbuf_tensor_at("xq_fm", [128, 16 * 1024], FP8,
                                    offset=a_wb + 32768)

    x1 = A("x1", [128, 8 * D], F32)
    a_x1 = nc.lookup_mloc(x1).addr
    qkv = nc.alloc_sbuf_tensor_at("qkv", [128, 3 * 4096], BF16, offset=a_x1)
    qbh = qkv[:, 0:4096]
    kbh = qkv[:, 4096:8192]
    vbh = qkv[:, 8192:12288]
    pt_sb = [nc.alloc_sbuf_tensor_at(f"pt{i}", [128, 1024], BF16,
                                     offset=a_x1 + 24576 + i * 2048)
             for i in range(2)]
    linv_bc = nc.alloc_sbuf_tensor_at("linv_bc", [128, 1024], F32,
                                      offset=a_x1 + 28672)
    osb = nc.alloc_sbuf_tensor_at("osb", [128, 1024], BF16,
                                  offset=a_x1 + 32768)
    qsq = nc.alloc_sbuf_tensor_at("qsq", [128, 512], BF16,
                                  offset=a_x1 + 34816)
    a_xts0 = nc.lookup_mloc(xts[0]).addr
    a_xts1 = nc.lookup_mloc(xts[1]).addr
    tq = nc.alloc_sbuf_tensor_at("tq", [128, D], F32, offset=a_xts0)
    xh2 = nc.alloc_sbuf_tensor_at("xh2", [128, D], F32, offset=a_xts1)
    a_xh0 = nc.lookup_mloc(xh_ts[0]).addr
    q8 = nc.alloc_sbuf_tensor_at("q8", [128, D], FP8, offset=a_xh0)
    a_stg0 = nc.lookup_mloc(stg[0]).addr
    assert nc.lookup_mloc(stg[3]).addr == a_stg0 + 3 * 1024
    sqts = nc.alloc_sbuf_tensor_at("sqts", [128, D], BF16, offset=a_stg0)
    a_xh1 = nc.lookup_mloc(xh_ts[1]).addr
    sqtsd = nc.alloc_sbuf_tensor_at("sqtsd", [128, D], BF16, offset=a_xh1)
    wb2 = nc.alloc_sbuf_tensor_at("wb2", [128, 16 * D], BF16, offset=a_x1)
    qkv2 = nc.alloc_sbuf_tensor_at("qkv2", [128, 3 * 4096], BF16,
                                   offset=a_x1 + 36864)
    qkvs = [qkv, qkv2]
    ptsum = nc.alloc_sbuf_tensor_at("ptsum", [128, 1024], F32,
                                    offset=a_x1 + 61440)
    ptsum2 = nc.alloc_sbuf_tensor_at("ptsum2", [128, 1024], F32,
                                     offset=a_xh1)
    ysb = nc.alloc_sbuf_tensor_at("ysb", [128, D], F32, offset=a_xf)
    ptsums = [ptsum, ptsum2]
    praw = nc.alloc_sbuf_tensor_at("praw", [128, 1024], BF16, offset=a_stg0)

    P0 = nc.alloc_psum_tensor("P0", [128, 1024], F32)
    P1 = nc.alloc_psum_tensor("P1", [128, 1024], F32)
    P2 = nc.alloc_psum_tensor("P2", [128, 1024], F32)
    L0 = nc.alloc_psum_tensor("L0", [128, 512], F32)
    L1 = nc.alloc_psum_tensor("L1", [128, 512], F32)
    banks4 = [P0[:, 0:512], P0[:, 512:1024], P1[:, 0:512], P1[:, 512:1024]]

    p = Prog(nc)
    sems = ["ld", "st", "wt", "pe", "ve", "se", "cc", "gp", "gl", "junk",
            "ob0", "ob1", "ob2", "ob3", "xr0", "xr1", "wfl"]
    for w_ in ("q", "k", "v", "y"):
        for k_ in range(4):
            sems.append(f"sg{w_}{k_}")
    for s_ in sems:
        p.mksem(s_)
    vals = {}

    for out_, in_ in ((idxt_sb, idxt), (id_sb, id_in), (onc, onc_in),
                      (onr, onr_in), (onrf, onrf_in), (maskd, mask_in), (bqs, bqi),
                      (bks, bki)):
        p.dma("sync", "ld", lambda e, o=out_, i=in_: e.dma_start(
            out=o[:, :], in_=i[:, :]))
    p.dma("sync", "ld", lambda e: e.dma_start(out=scl[0:1, 0:1], in_=wsc[:, :]))
    start_ld = p.val("ld")
    p.dma("gpsimd", "junk", lambda e: e.dma_start(out=bvr[:, :], in_=bvi[:, :]))
    p.dma("gpsimd", "junk", lambda e: e.dma_start(out=bor[:, :], in_=boi[:, :]))
    p.dma("gpsimd", "junk", lambda e: e.dma_start(out=iscb[:, :], in_=isc[:, :]))
    p.fence("f_sm", lambda e: e.dma_start(out=glt[:, :], in_=xin[0:128, 0:16]))
    for j in range(16):
        p.dma("gpsimd", "junk", lambda e, j=j: e.dma_start(
            out=wb[:, j * D:(j + 1) * D], in_=wqT[j * 128:(j + 1) * 128, :]))
    p.fence("f_wq", lambda e: e.dma_start(out=glt[:, :], in_=xin[0:128, 0:16]))

    p.wait("vector", "ld", start_ld)
    p.inc("vector", "ve", 1, lambda e: e.memset(epsv[:, :], EPS))
    p.inc("vector", "ve", 1, lambda e: e.memset(lnc[:, :], 24.0))
    p.inc("vector", "ve", 1, lambda e: e.memset(oncf[:, :], 1.0))
    p.inc("vector", "ve", 1, lambda e: e.tensor_scalar(
        bqs[:, :], bqs[:, :], SC, None, ALU.mult))
    p.inc("vector", "ve", 1, lambda e: e.tensor_scalar(
        bks[:, :], bks[:, :], SC, None, ALU.mult))
    p.wait("vector", "f_sm", 128)
    p.inc("vector", "ve", 1, lambda e: e.tensor_copy(scl[0:1, 3:4], iscb[:, :]))
    ve_scl = p.val("ve")
    p.wait("scalar", "ve", ve_scl)
    p.inc("scalar", "se", 1, lambda e: e.activation(
        scl[0:1, 4:5], scl[0:1, 3:4], AF.Ln))
    p.inc("scalar", "se", 1, lambda e: e.activation(
        scl[0:1, 1:2], scl[0:1, 4:5], AF.Exp, scale=-1.0))
    se_scl = p.val("se")
    p.wait("tensor", "ve", ve_scl)
    p.wait("tensor", "se", se_scl)
    p.inc("tensor", "pe", 1, lambda e: e.matmul(
        P2[:, 0:1], onrf[0:1, :], scl[0:1, 1:2], start=True, stop=True))
    p.inc("tensor", "pe", 1, lambda e: e.matmul(
        P2[:, 1:2], onrf[0:1, :], scl[0:1, 0:1], start=True, stop=True))
    p.inc("tensor", "pe", 1, lambda e: e.matmul(
        P2[:, 2:3], onrf[0:1, :], scl[0:1, 3:4], start=True, stop=True))
    pe_bc0 = p.val("pe")
    p.wait("scalar", "pe", pe_bc0)
    p.inc("scalar", "se", 1, lambda e: e.copy(rinv_bc[:, :], P2[:, 0:1]))
    p.inc("scalar", "se", 1, lambda e: e.copy(nb_sb[:, :], P2[:, 1:2]))
    se_bcast = p.val("se")
    p.wait("vector", "pe", pe_bc0)
    p.wait("vector", "se", se_bcast)
    p.inc("vector", "ve", 1, lambda e: e.tensor_tensor(
        wssc_bc[:, :], nb_sb[:, :], P2[:, 2:3], ALU.mult))
    ve_bcast = p.val("ve")

    def ln_tiles(src_getter, dst_q8=False):
        pass

    prev_se_r1 = 0
    prev_lnapply = {}
    prev_trev = {-1: 0}
    for t in range(8):
        xb, hb = xts[t % 2], xh_ts[t % 2]
        if t >= 2:
            p.wait("sync", "se", prev_lnapply[t - 2])
        p.dma("sync", "ld", lambda e, t=t, xb=xb: e.dma_start(
            out=xb[:, :], in_=xin[t * 128:(t + 1) * 128, :]))
        ld_here = p.val("ld")
        p.wait("scalar", "ld", ld_here)
        if t >= 1:
            p.wait("scalar", "ve", vals.get("ln1ve", 0))
        p.inc("scalar", "se", 1, lambda e, xb=xb: e.activation(
            sqts[:, :], xb[:, :], AF.Square))
        se_sq = p.val("se")
        p.wait("vector", "ld", ld_here)
        p.wait("vector", "se", se_sq)
        if t >= 1:
            p.wait("vector", "se", prev_se_r1)
        p.inc("vector", "ve", 1, lambda e, xb=xb: e.reduce_sum(
            mv[:, 0:1], xb[:, :], axis=AX.X))
        p.inc("vector", "ve", 1, lambda e: e.reduce_sum(
            mv[:, 1:2], sqts[:, :], axis=AX.X))
        p.inc("vector", "ve", 1, lambda e: e.tensor_scalar(
            stats[:, 0:1], mv[:, 0:1], 1.0 / D, None, ALU.mult))
        p.inc("vector", "ve", 1, lambda e: e.scalar_tensor_tensor(
            stats[:, 1:2], mv[:, 0:1], -1.0 / (D * D), mv[:, 0:1],
            ALU.mult, ALU.mult))
        ve_stats = p.val("ve")
        vals["ln1ve"] = ve_stats
        p.wait("scalar", "ve", ve_stats)
        p.inc("scalar", "se", 1, lambda e: e.activation(
            stats[:, 3:4], mv[:, 1:2], AF.Identity, bias=stats[:, 1:2],
            scale=1.0 / D))
        p.inc("scalar", "se", 1, lambda e: e.activation(
            r1a[:, :], stats[:, 3:4], AF.Ln, bias=epsv[:, :], scale=1.0))
        p.inc("scalar", "se", 1, lambda e: e.activation(
            r1[:, :], r1a[:, :], AF.Exp, bias=0.0, scale=-0.5))
        prev_se_r1 = p.val("se")
        p.wait("vector", "se", prev_se_r1)
        p.inc("vector", "ve", 1, lambda e: e.scalar_tensor_tensor(
            b2[:, :], stats[:, 0:1], -1.0, r1[:, :], ALU.mult, ALU.mult))
        ve_b2 = p.val("ve")
        p.wait("scalar", "ve", ve_b2)
        if t >= 2:
            p.wait("scalar", "pe", vals[f"tr{t-2}"])
        p.inc("scalar", "se", 1, lambda e, xb=xb, hb=hb: e.activation(
            hb[:, :], xb[:, :], AF.Identity, bias=b2[:, :], scale=r1[:, :]))
        prev_lnapply[t] = p.val("se")
        for rnd in range(2):
            ps = P2 if rnd == 0 else P1
            if t == 0 and rnd == 0:
                p.wait("tensor", "se", se_bcast)
                p.wait("tensor", "ve", ve_bcast)
            p.wait("tensor", "se", prev_lnapply[t])
            p.wait("tensor", "ve", prev_trev.get(t - 1, prev_trev[-1])
                   if rnd == 0 else vals.get(f"trevP1_{t-1}", 0))
            for jj in range(8):
                j = rnd * 8 + jj
                p.inc("tensor", "pe", 1,
                      lambda e, hb=hb, j=j, jj=jj, ps=ps: e.matmul(
                          ps[:, jj * 128:(jj + 1) * 128],
                          hb[:, j * 128:(j + 1) * 128], id_sb[:, :],
                          start=True, stop=True))
            pe_tr = p.val("pe")
            p.wait("vector", "pe", pe_tr)
            p.inc("vector", "ve", 1,
                  lambda e, t=t, rnd=rnd, ps=ps: e.tensor_copy(
                      xhat_fm[:, :].rearrange("p (j c) -> p j c", j=16)
                      [:, rnd * 8:(rnd + 1) * 8, t * 128:(t + 1) * 128],
                      ps[:, :].rearrange("p (j c) -> p j c", j=8)))
            if rnd == 0:
                prev_trev[t] = p.val("ve")
            else:
                vals[f"trevP1_{t}"] = p.val("ve")
        vals[f"tr{t}"] = p.val("pe")
    xhat_ve = p.val("ve")

    def proj_qk(which, wsem, bias_sb, base, wbuf, prevsec=None):
        for i, (oc, tb) in enumerate([(o, t2) for o in range(16)
                                      for t2 in range(2)]):
            bank, sb = banks4[i % 4], stg[i % 4]
            sem = f"sg{which}{i % 4}"
            p.wait("tensor", wsem, 128)
            if i == 0:
                p.wait("tensor", "ve", xhat_ve)
            if i >= 4:
                p.wait("tensor", sem, vals[f"{which}dma{i-4}"])
            elif prevsec:
                p.wait("tensor", f"sg{prevsec}{i % 4}",
                       vals[f"{prevsec}sgfull{i % 4}"])
            for j in range(16):
                p.inc("tensor", "pe", 1,
                      lambda e, j=j, oc=oc, tb=tb, bank=bank, wbuf=wbuf:
                      e.matmul(
                          bank, wbuf[:, j * D + oc * 128: j * D + (oc + 1) * 128],
                          xhat_fm[:, j * 1024 + tb * 512:
                                  j * 1024 + (tb + 1) * 512],
                          start=(j == 0), stop=(j == 15)))
            pe_here = p.val("pe")
            p.wait("scalar", "pe", pe_here)
            p.inc("scalar", "se", 1,
                  lambda e, bank=bank, sb=sb, oc=oc, bias_sb=bias_sb:
                  e.activation(sb[:, :], bank, AF.Identity,
                               bias=bias_sb[:, oc:oc + 1], scale=SC))
            p.dma("scalar", sem,
                  lambda e, sb=sb, oc=oc, tb=tb, base=base: e.dma_start(
                      out=cc1i[oc * 128 + base: oc * 128 + base + 128,
                               tb * 512:(tb + 1) * 512], in_=sb[:, :]))
            vals[f"{which}dma{i}"] = p.val(sem)
        for k_ in range(4):
            vals[f"{which}sgfull{k_}"] = p.val(f"sg{which}{k_}")

    for j in range(16):
        p.dma("gpsimd", "junk", lambda e, j=j: e.dma_start(
            out=wb2[:, j * D:(j + 1) * D], in_=wkT[j * 128:(j + 1) * 128, :]))
    p.fence("f_wk", lambda e: e.dma_start(out=glt[:, :], in_=xin[0:128, 0:16]))
    proj_qk("q", "f_wq", bqs, 0, wb)
    q_mms = p.val("pe")
    for k_ in range(4):
        p.wait("gpsimd", f"sgq{k_}", vals[f"qsgfull{k_}"])
    p.inc("gpsimd", "cc", 1, lambda e: e.collective_compute(
        "AllGather", ALU.bypass, ins=[cc1i[0:2048, :]], outs=[cc1oq[:, :]],
        replica_groups=GRPS))
    p.wait("gpsimd", "pe", q_mms)
    for j in range(16):
        p.dma("gpsimd", "junk", lambda e, j=j: e.dma_start(
            out=wb[:, j * D:(j + 1) * D], in_=wvT[j * 128:(j + 1) * 128, :]))
    p.fence("f_wv", lambda e: e.dma_start(out=glt[:, :], in_=xin[0:128, 0:16]))
    proj_qk("k", "f_wk", bks, 2048, wb2, prevsec="q")
    for k_ in range(4):
        p.wait("gpsimd", f"sgk{k_}", vals[f"ksgfull{k_}"])
    p.inc("gpsimd", "cc", 1, lambda e: e.collective_compute(
        "AllGather", ALU.bypass, ins=[cc1i[2048:4096, :]], outs=[cc1ok[:, :]],
        replica_groups=GRPS))

    for i, (tt, oc4) in enumerate([(t2, o) for t2 in range(8)
                                   for o in range(4)]):
        bank, sb = banks4[i % 4], stg[i % 4]
        sem = f"sgv{i % 4}"
        p.wait("tensor", "f_wv", 128)
        if i >= 4:
            p.wait("tensor", sem, vals[f"vdma{i-4}"])
        else:
            p.wait("tensor", f"sgk{i % 4}", vals[f"ksgfull{i % 4}"])
        for j in range(16):
            p.inc("tensor", "pe", 1,
                  lambda e, j=j, tt=tt, oc4=oc4, bank=bank: e.matmul(
                      bank,
                      xhat_fm[:, j * 1024 + tt * 128: j * 1024 + (tt + 1) * 128],
                      wb[:, j * D + oc4 * 512: j * D + (oc4 + 1) * 512],
                      start=(j == 0), stop=False))
        p.inc("tensor", "pe", 1, lambda e, oc4=oc4, bank=bank: e.matmul(
            bank, onr[:, :], bvr[0:1, oc4 * 512:(oc4 + 1) * 512],
            start=False, stop=True))
        pe_here = p.val("pe")
        p.wait("scalar", "pe", pe_here)
        p.inc("scalar", "se", 1, lambda e, bank=bank, sb=sb: e.activation(
            sb[:, :], bank, AF.Identity))
        p.dma("scalar", sem, lambda e, sb=sb, tt=tt, oc4=oc4: e.dma_start(
            out=cc1i[4096:6144, tt * 128:(tt + 1) * 128]
            .rearrange("(i q) c -> q i c", q=128)
            [:, 4 * oc4:4 * oc4 + 4, :],
            in_=sb[:, :].rearrange("p (i c) -> p i c", i=4)))
        vals[f"vdma{i}"] = p.val(sem)

    for k_ in range(4):
        p.wait("gpsimd", f"sgv{k_}", p.val(f"sgv{k_}"))
    p.inc("gpsimd", "cc", 1, lambda e: e.collective_compute(
        "AllGather", ALU.bypass, ins=[cc1i[4096:6144, :]], outs=[cc1ov[:, :]],
        replica_groups=GRPS))

    p.wait("gpsimd", "ld", start_ld)
    gstep = 0
    exp_hist = {}
    ov_hist = {}
    pts_hist = {}
    secs = (cc1oq, cc1ok, cc1ov)

    def emit_fin_l(fin):
        p.wait("tensor", "ve", pts_hist[fin["gend"] - 1])
        p.wait("tensor", "se", vals.get("se_lbank_free", 0))
        pts = fin["pts"]
        p.inc("tensor", "pe", 1, lambda e, pts=pts: e.matmul(
            L0[0:1, :], oncf[:, :], pts[:, 0:512], start=True, stop=True))
        p.inc("tensor", "pe", 1, lambda e, pts=pts: e.matmul(
            L1[0:1, :], oncf[:, :], pts[:, 512:1024], start=True, stop=True))
        vals[f"ptsum_free{fin['qc'] % 2}"] = p.val("pe")
        pe_qc = p.val("pe")
        p.wait("scalar", "pe", pe_qc)
        p.inc("scalar", "se", 1, lambda e: e.activation(
            lsb[0:1, 0:512], L0[0:1, :], AF.Ln, scale=C24))
        p.inc("scalar", "se", 1, lambda e: e.activation(
            lsb[0:1, 512:1024], L1[0:1, :], AF.Ln, scale=C24))
        p.inc("scalar", "se", 1, lambda e: e.activation(
            lsb2[0:1, :], lsb[0:1, :], AF.Exp, bias=lnc[0:1, :], scale=-1.0))
        vals["se_l_last"] = p.val("se")

    def emit_fin_osb(fin):
        u, qc = fin["u"], fin["qc"]
        p.wait("tensor", "se", vals["se_l_last"])
        p.inc("tensor", "pe", 1, lambda e: e.matmul(
            L0[:, :], onrf[0:1, :], lsb2[0:1, 0:512], start=True, stop=True))
        p.inc("tensor", "pe", 1, lambda e: e.matmul(
            L1[:, :], onrf[0:1, :], lsb2[0:1, 512:1024], start=True, stop=True))
        pe_l = p.val("pe")
        p.wait("scalar", "pe", pe_l)
        p.inc("scalar", "se", 1, lambda e: e.copy(linv_bc[:, 0:512], L0[:, :]))
        p.inc("scalar", "se", 1, lambda e: e.copy(
            linv_bc[:, 512:1024], L1[:, :]))
        se_linv = p.val("se")
        vals["se_lbank_free"] = se_linv
        p.wait("vector", "se", se_linv)
        p.wait("vector", f"ob{u}", p.val(f"ob{u}"))
        p.inc("vector", "ve", 1, lambda e: e.tensor_tensor(
            osb[:, :], praw[:, :], linv_bc[:, :], ALU.mult))
        ve_o = p.val("ve")
        p.wait("scalar", "ve", ve_o)
        p.dma("scalar", f"ob{u}", lambda e, u=u, qc=qc: e.dma_start(
            out=cc2i[u * 128:(u + 1) * 128, qc * 1024:(qc + 1) * 1024],
            in_=osb[:, :]))

    def emit_fin_praw(fin):
        p.wait("vector", "pe", ov_hist[fin["gend"] - 1])
        p.inc("vector", "ve", 1, lambda e: e.tensor_copy(
            praw[:, :], P2[:, :]))
        fin["ve_praw"] = p.val("ve")

    def emit_ag2(u):
        p.wait("gpsimd", f"ob{u}", 64)
        p.inc("gpsimd", "cc", 1, lambda e, u=u: e.collective_compute(
            "AllGather", ALU.bypass, ins=[cc2i[u * 128:(u + 1) * 128, :]],
            outs=[cc2o[u * 1024:(u + 1) * 1024, :]],
            replica_groups=GRPS))

    def emit_opiece(pu):
        p.wait("gpsimd", "cc", 4 + pu)
        for h in range(pu, 16, 4):
            p.dma("gpsimd", "junk", lambda e, h=h: e.indirect_dma_start(
                oall[:, h * 1024:(h + 1) * 1024], None,
                cc2o[:, :].rearrange("a (b c) -> (a b) c", c=1024),
                IndirectOffsetOnAxis(idxt_sb[:, 48 + h:49 + h], 0)))

    for u in range(4):
        qkvu = qkvs[u % 2]
        qb = qkvu[:, 0:4096]
        kb = qkvu[:, 4096:8192]
        vb = qkvu[:, 8192:12288]
        if u == 1:
            for j in range(16):
                p.dma("gpsimd", "junk", lambda e, j=j: e.dma_start(
                    out=wb[:, j * D:(j + 1) * D],
                    in_=woT[j * 128:(j + 1) * 128, :]))
            p.fence("f_wo", lambda e: e.dma_start(
                out=glt[:, :], in_=xin[0:128, 0:16]))
        if u >= 2:
            emit_ag2(u - 2)
            emit_opiece(u - 2)
            p.wait("gpsimd", "pe", vals[f"unitpe{u-2}"])
            p.wait("gpsimd", "se", vals[f"unitse{u-2}"])
        if u == 0:
            p.wait("gpsimd", "cc", 2)
        for s_ in range(4):
            for r_ in range(2):
                p.dma("gpsimd", "junk", lambda e, qkvu=qkvu, u=u, s_=s_, r_=r_:
                      e.indirect_dma_start(
                    qkvu[:, r_ * 4096 + s_ * 1024: r_ * 4096 + (s_ + 1) * 1024],
                    None, secs[r_][:, :],
                    IndirectOffsetOnAxis(
                        idxt_sb[:, u * 12 + s_ * 3 + r_:
                                u * 12 + s_ * 3 + r_ + 1], 0)))
        p.fence(f"f_uqk{u}", lambda e: e.dma_start(
            out=glt[:, :], in_=cc1oq[0:128, 0:16]))
        if u == 0:
            p.wait("gpsimd", "cc", 3)
        for s_ in range(4):
            p.dma("gpsimd", "junk", lambda e, qkvu=qkvu, u=u, s_=s_:
                  e.indirect_dma_start(
                qkvu[:, 8192 + s_ * 1024: 8192 + (s_ + 1) * 1024],
                None, secs[2][:, :],
                IndirectOffsetOnAxis(
                    idxt_sb[:, u * 12 + s_ * 3 + 2:
                            u * 12 + s_ * 3 + 3], 0)))
        p.fence(f"f_uv{u}", lambda e: e.dma_start(
            out=glt[:, :], in_=cc1oq[0:128, 0:16]))

        p.wait("scalar", f"f_uqk{u}", 128)
        if u > 0:
            p.wait("vector", "se", vals[f"unitse{u-1}"])
        for w, (src, slot) in enumerate(((qb, 0), (kb, 1))):
            for c8 in range(8):
                if c8 > 0 or w > 0:
                    p.wait("scalar", "pe", p.val("pe"))
                p.inc("scalar", "se", 1, lambda e, src=src, c8=c8: e.activation(
                    qsq[:, :], src[:, c8 * 512:(c8 + 1) * 512], AF.Square))
                se_h = p.val("se")
                p.wait("tensor", "se", se_h)
                p.wait("tensor", "ve", p.val("ve"))
                p.inc("tensor", "pe", 1, lambda e: e.matmul(
                    L0[0:1, :], onc[:, :], qsq[:, :], start=True, stop=True))
                pe_h = p.val("pe")
                p.wait("vector", "pe", pe_h)
                p.inc("vector", "ve", 1,
                      lambda e, w=w, c8=c8: e.reduce_max(
                          cmx[0:1, w * 8 + c8:w * 8 + c8 + 1],
                          L0[0:1, :], axis=AX.X))
        p.wait("scalar", "ve", p.val("ve"))
        p.inc("scalar", "se", 1, lambda e: e.copy(
            smax[0:1, 7:8], scl[0:1, 0:1]))
        p.wait("vector", "se", p.val("se"))
        p.inc("vector", "ve", 1, lambda e: e.reduce_max(
            smax[0:1, 0:1], cmx[0:1, 0:8], axis=AX.X))
        p.inc("vector", "ve", 1, lambda e: e.reduce_max(
            smax[0:1, 1:2], cmx[0:1, 8:16], axis=AX.X))
        p.wait("scalar", "ve", p.val("ve"))
        p.inc("scalar", "se", 1, lambda e: e.activation(
            smax[0:1, 2:3], smax[0:1, 0:1], AF.Ln))
        p.inc("scalar", "se", 1, lambda e: e.activation(
            smax[0:1, 5:6], smax[0:1, 1:2], AF.Ln))
        p.wait("vector", "se", p.val("se"))
        p.inc("vector", "ve", 1, lambda e: e.tensor_tensor(
            smax[0:1, 3:4], smax[0:1, 2:3], smax[0:1, 5:6], ALU.add))
        p.wait("scalar", "ve", p.val("ve"))
        p.inc("scalar", "se", 1, lambda e: e.activation(
            smax[0:1, 6:7], smax[0:1, 3:4], AF.Exp, scale=0.5))
        se_bnd = p.val("se")
        p.wait("vector", "se", se_bnd)
        p.inc("vector", "ve", 1, lambda e: e.tensor_scalar(
            smax[0:1, 4:5], smax[0:1, 6:7], -1.0, None, ALU.mult))
        ve_nb = p.val("ve")
        p.wait("tensor", "ve", ve_nb)
        p.inc("tensor", "pe", 1, lambda e: e.matmul(
            P2[:, 0:1], onrf[0:1, :], smax[0:1, 4:5], start=True, stop=True))
        pe_nb = p.val("pe")
        p.wait("scalar", "pe", pe_nb)
        p.inc("scalar", "se", 1, lambda e: e.copy(nb_sb[:, :], P2[:, 0:1]))
        se_nb = p.val("se")
        p.wait("tensor", f"f_uqk{u}", 128)
        p.wait("tensor", f"f_uv{u}", 128)

        fin = None
        for qc in range(4):
            qcb = qc * 1024
            nkt = 8 * qc + 8
            pend = None
            for kt in range(nkt):
                w0 = max(kt * 128 - qcb, 0)
                si = gstep % 2
                stp, ptb = (P0, pt_sb[0]) if si == 0 else (P1, pt_sb[1])
                if fin is not None:
                    if kt == 0:
                        emit_fin_praw(fin)
                    elif kt == 1:
                        emit_fin_l(fin)
                        p.wait("tensor", "ve", fin["ve_praw"])
                    elif kt == 2:
                        emit_fin_osb(fin)
                        fin = None
                if gstep - 2 in exp_hist:
                    p.wait("tensor", "se", exp_hist[gstep - 2])
                for a in ((w0, 512), (512, 1024)):
                    a0, b0 = max(a[0], w0), a[1]
                    if b0 <= w0 or b0 <= a0:
                        continue
                    p.inc("tensor", "pe", 1,
                          lambda e, kt=kt, a0=a0, b0=b0, qcb=qcb, stp=stp,
                          kb=kb, qb=qb:
                          e.matmul(stp[:, a0:b0],
                                   kb[:, kt * 128:(kt + 1) * 128],
                                   qb[:, qcb + a0:qcb + b0],
                                   start=True, stop=True))
                qk_here = p.val("pe")
                masked = kt * 128 >= qcb
                if masked:
                    p.wait("vector", "pe", qk_here)
                    p.inc("vector", "ve", 1,
                          lambda e, stp=stp, w0=w0: e.tensor_tensor(
                              stp[:, w0:w0 + 128], stp[:, w0:w0 + 128],
                              maskd[:, :], ALU.add))
                    mask_here = p.val("ve")
                p.wait("scalar", "pe", qk_here)
                if masked:
                    p.wait("scalar", "ve", mask_here)
                if gstep - 2 in ov_hist:
                    p.wait("scalar", "pe", ov_hist[gstep - 2])
                if gstep - 2 in pts_hist:
                    p.wait("scalar", "ve", pts_hist[gstep - 2])
                p.inc("scalar", "se", 1,
                      lambda e, stp=stp, ptb=ptb, w0=w0: e.activation(
                          ptb[:, w0:1024], stp[:, w0:1024], AF.Exp,
                          bias=nb_sb[:, :], scale=1.0))
                exp_hist[gstep] = p.val("se")
                p.wait("vector", "se", exp_hist[gstep])
                pts = ptsums[qc % 2]
                if kt == 0:
                    p.wait("vector", "pe", vals.get(f"ptsum_free{qc % 2}", 0))
                    p.inc("vector", "ve", 1,
                          lambda e, ptb=ptb, pts=pts: e.tensor_copy(
                              pts[:, :], ptb[:, :]))
                else:
                    p.inc("vector", "ve", 1,
                          lambda e, ptb=ptb, w0=w0, pts=pts: e.tensor_tensor(
                              pts[:, w0:1024], pts[:, w0:1024],
                              ptb[:, w0:1024], ALU.add))
                pts_hist[gstep] = p.val("ve")
                if pend is not None:
                    emit_ov(p, pend, P2, L0, L1, vb, onc, pt_sb,
                            exp_hist, ov_hist)
                pend = (kt, w0, si, gstep, kt == 0, False)
                gstep += 1
            pend = pend[:5] + (True,)
            emit_ov(p, pend, P2, L0, L1, vb, onc, pt_sb, exp_hist, ov_hist)
            pend = None
            fin = {"gend": gstep, "pts": ptsums[qc % 2], "u": u, "qc": qc}
        emit_fin_praw(fin)
        emit_fin_l(fin)
        p.wait("tensor", "ve", fin["ve_praw"])
        emit_fin_osb(fin)
        fin = None
        vals[f"unitpe{u}"] = p.val("pe")
        vals[f"unitse{u}"] = p.val("se")
        vals[f"odma{u}"] = p.val(f"ob{u}")

    emit_ag2(2)
    emit_opiece(2)
    emit_ag2(3)
    emit_opiece(3)
    p.fence("f_og", lambda e: e.dma_start(out=glt[:, :], in_=cc2o[0:128, 0:16]))

    p.wait("sync", "cc", 7)
    for i, (tt, oc4) in enumerate([(t2, o) for t2 in range(8)
                                   for o in range(4)]):
        bank = banks4[i % 4]
        if oc4 == 0:
            xb = xts[tt % 2]
            if tt >= 2:
                p.wait("sync", "ve", vals[f"resid{tt-2}_3"])
            p.dma("sync", f"xr{tt % 2}", lambda e, tt=tt, xb=xb: e.dma_start(
                out=xb[:, :], in_=xin[tt * 128:(tt + 1) * 128, :]))
            vals[f"xl{tt}"] = p.val(f"xr{tt % 2}")
        p.wait("tensor", "f_wo", 128)
        if i == 0:
            p.wait("tensor", "f_og", 128)
        if i >= 4:
            p.wait("tensor", "ve", vals[f"resid{(i-4)//4}_{(i-4)%4}"])
        for j in range(16):
            p.inc("tensor", "pe", 1,
                  lambda e, j=j, tt=tt, oc4=oc4, bank=bank: e.matmul(
                      bank,
                      oall[:, j * 1024 + tt * 128: j * 1024 + (tt + 1) * 128],
                      wb[:, j * D + oc4 * 512: j * D + (oc4 + 1) * 512],
                      start=(j == 0), stop=False))
        p.inc("tensor", "pe", 1, lambda e, oc4=oc4, bank=bank: e.matmul(
            bank, onr[:, :], bor[0:1, oc4 * 512:(oc4 + 1) * 512],
            start=False, stop=True))
        pe_here = p.val("pe")
        p.wait("vector", "pe", pe_here)
        p.wait("vector", f"xr{tt % 2}", vals[f"xl{tt}"])
        p.inc("vector", "ve", 1,
              lambda e, tt=tt, oc4=oc4, bank=bank, xb=xts[tt % 2]:
              e.tensor_tensor(
                  x1[:, tt * D + oc4 * 512: tt * D + (oc4 + 1) * 512],
                  bank, xb[:, oc4 * 512:(oc4 + 1) * 512], ALU.add))
        vals[f"resid{tt}_{oc4}"] = p.val("ve")
    op_mms = p.val("pe")

    p.wait("sync", "pe", op_mms)
    for j in range(16):
        p.dma("sync", "wfl", lambda e, j=j: e.dma_start(
            out=wf_sb[:, j * D:(j + 1) * D], in_=wfT[j * 128:(j + 1) * 128, :]))
    wf_done = p.val("wfl")

    def emit_ffn_pe(tt):
        p.wait("tensor", "se", vals[f"q8done{tt}"])
        for oc4 in range(4):
            i = tt * 4 + oc4
            bank = banks4[i % 4]
            if i >= 4:
                p.wait("tensor", "ve", vals[f"yev{i-4}"])
            for j in range(16):
                p.inc("tensor", "pe", 1,
                      lambda e, j=j, tt=tt, oc4=oc4, bank=bank: e.matmul(
                          bank,
                          xq_fm[:, j * 1024 + tt * 128: j * 1024 + (tt + 1) * 128],
                          wf_sb[:, j * D + oc4 * 512: j * D + (oc4 + 1) * 512],
                          start=(j == 0), stop=(j == 15)))
            vals[f"ympe{i}"] = p.val("pe")

    def emit_ffn_evac(tt):
        for oc4 in range(4):
            i = tt * 4 + oc4
            bank = banks4[i % 4]
            p.wait("vector", "pe", vals[f"ympe{i}"])
            if i >= 4:
                p.wait("vector", f"sgy{(i - 4) % 4}", vals[f"ydma{i-4}"])
            p.inc("vector", "ve", 1,
                  lambda e, tt=tt, oc4=oc4, bank=bank: e.scalar_tensor_tensor(
                      ysb[:, oc4 * 512:(oc4 + 1) * 512], bank, wssc_bc[:, 0:1],
                      x1[:, tt * D + oc4 * 512: tt * D + (oc4 + 1) * 512],
                      ALU.mult, ALU.add))
            vals[f"yev{i}"] = p.val("ve")
            p.wait("scalar", "ve", vals[f"yev{i}"])
            p.dma("scalar", f"sgy{i % 4}", lambda e, tt=tt, oc4=oc4: e.dma_start(
                out=yout[tt * 128:(tt + 1) * 128, oc4 * 512:(oc4 + 1) * 512],
                in_=ysb[:, oc4 * 512:(oc4 + 1) * 512]))
            vals[f"ydma{i}"] = p.val(f"sgy{i % 4}")

    prev_se_r1d = 0
    vals["trD-1"] = 0
    for t in range(8):
        p.wait("vector", "ve", vals[f"resid{t}_3"])
        if t >= 1:
            p.wait("vector", "se", prev_se_r1d)
        p.wait("vector", "pe", vals[f"trD{t-1}"])
        p.wait("scalar", "ve", vals[f"resid{t}_3"])
        if t >= 1:
            p.wait("scalar", "ve", vals[f"ln2ve{t-1}"])
        p.inc("scalar", "se", 1, lambda e, t=t: e.activation(
            sqtsd[:, :], x1[:, t * D:(t + 1) * D], AF.Square))
        se_sq = p.val("se")
        p.wait("vector", "se", se_sq)
        p.inc("vector", "ve", 1, lambda e, t=t: e.reduce_sum(
            mv[:, 0:1], x1[:, t * D:(t + 1) * D], axis=AX.X))
        p.inc("vector", "ve", 1, lambda e: e.reduce_sum(
            mv[:, 1:2], sqtsd[:, :], axis=AX.X))
        vals[f"ln2ve{t}"] = p.val("ve")
        p.inc("vector", "ve", 1, lambda e: e.tensor_scalar(
            stats[:, 0:1], mv[:, 0:1], 1.0 / D, None, ALU.mult))
        p.inc("vector", "ve", 1, lambda e: e.scalar_tensor_tensor(
            stats[:, 1:2], mv[:, 0:1], -1.0 / (D * D), mv[:, 0:1],
            ALU.mult, ALU.mult))
        ve_stats = p.val("ve")
        p.wait("scalar", "ve", ve_stats)
        p.inc("scalar", "se", 1, lambda e: e.activation(
            stats[:, 3:4], mv[:, 1:2], AF.Identity, bias=stats[:, 1:2],
            scale=1.0 / D))
        p.inc("scalar", "se", 1, lambda e: e.activation(
            r1a[:, :], stats[:, 3:4], AF.Ln, bias=epsv[:, :], scale=1.0))
        p.inc("scalar", "se", 1, lambda e: e.activation(
            r1[:, :], r1a[:, :], AF.Exp, bias=0.0, scale=-0.5))
        prev_se_r1d = p.val("se")
        p.wait("vector", "se", prev_se_r1d)
        p.inc("vector", "ve", 1, lambda e: e.scalar_tensor_tensor(
            b2[:, :], stats[:, 0:1], -1.0, r1[:, :], ALU.mult, ALU.mult))
        ve_b2 = p.val("ve")
        p.wait("scalar", "ve", ve_b2)
        p.inc("scalar", "se", 1, lambda e, t=t: e.activation(
            xh2[:, :], x1[:, t * D:(t + 1) * D], AF.Identity,
            bias=b2[:, :], scale=r1[:, :]))
        se_xh2 = p.val("se")
        p.wait("vector", "se", se_xh2)
        if t == 0:
            p.wait("vector", "se", se_bcast)
        p.inc("vector", "ve", 1, lambda e: e.tensor_scalar(
            tq[:, :], xh2[:, :], rinv_bc[:, 0:1], None, ALU.mult))
        p.inc("vector", "ve", 1, lambda e: e.tensor_scalar(
            q8[:, :], tq[:, :], 448.0, -448.0, ALU.min, ALU.max))
        ve_q8 = p.val("ve")
        p.wait("tensor", "ve", ve_q8)
        if t == 0:
            p.wait("tensor", "wfl", wf_done)
        p.wait("tensor", "se", vals.get(f"evP2_{t-1}", 0))
        for jj in range(8):
            p.inc("tensor", "pe", 1,
                  lambda e, jj=jj: e.matmul(
                      P2[:, jj * 128:(jj + 1) * 128],
                      q8[:, jj * 128:(jj + 1) * 128], id_sb[:, :],
                      start=True, stop=True))
        pe_tr0 = p.val("pe")
        p.wait("tensor", "se", vals.get(f"evL_{t-1}", 0))
        for jj in range(8):
            j = 8 + jj
            lt = L0 if jj < 4 else L1
            p.inc("tensor", "pe", 1,
                  lambda e, j=j, jj=jj, lt=lt: e.matmul(
                      lt[:, (jj % 4) * 128:(jj % 4 + 1) * 128],
                      q8[:, j * 128:(j + 1) * 128], id_sb[:, :],
                      start=True, stop=True))
        pe_tr1 = p.val("pe")
        p.wait("scalar", "pe", pe_tr0)
        p.inc("scalar", "se", 1,
              lambda e, t=t: e.copy(
                  xq_fm[:, :].rearrange("p (j c) -> p j c", j=16)
                  [:, 0:8, t * 128:(t + 1) * 128],
                  P2[:, :].rearrange("p (j c) -> p j c", j=8)))
        vals[f"evP2_{t}"] = p.val("se")
        p.wait("scalar", "pe", pe_tr1)
        p.inc("scalar", "se", 1,
              lambda e, t=t: e.copy(
                  xq_fm[:, :].rearrange("p (j c) -> p j c", j=16)
                  [:, 8:12, t * 128:(t + 1) * 128],
                  L0[:, :].rearrange("p (j c) -> p j c", j=4)))
        p.inc("scalar", "se", 1,
              lambda e, t=t: e.copy(
                  xq_fm[:, :].rearrange("p (j c) -> p j c", j=16)
                  [:, 12:16, t * 128:(t + 1) * 128],
                  L1[:, :].rearrange("p (j c) -> p j c", j=4)))
        vals[f"evL_{t}"] = p.val("se")
        vals[f"q8done{t}"] = p.val("se")
        vals[f"trD{t}"] = p.val("pe")
        if t >= 2:
            emit_ffn_evac(t - 2)
        if t >= 1:
            emit_ffn_pe(t - 1)
    emit_ffn_evac(6)
    emit_ffn_pe(7)
    emit_ffn_evac(7)

    for k_ in range(4):
        p.wait("scalar", f"sgy{k_}")
    p.wait("scalar", "st")

    with nc.Block() as block:
        @block.sync
        def _(e):
            for f in p.ops["sync"]:
                f(e)

        @block.gpsimd
        def _(e):
            for f in p.ops["gpsimd"]:
                f(e)

        @block.vector
        def _(e):
            for f in p.ops["vector"]:
                f(e)

        @block.scalar
        def _(e):
            for f in p.ops["scalar"]:
                f(e)

        @block.tensor
        def _(e):
            for f in p.ops["tensor"]:
                f(e)

    return nc


def emit_ov(p, pend, P2, L0, L1, vbh, onc, pt_sb, exp_hist, ov_hist):
    kt, w0, si, g, is_start, is_stop = pend
    ptb = pt_sb[si]
    p.wait("tensor", "se", exp_hist[g])
    for a in ((w0, 512), (512, 1024)):
        a0, b0 = max(a[0], w0), a[1]
        if b0 <= w0 or b0 <= a0:
            continue
        p.inc("tensor", "pe", 1,
              lambda e, kt=kt, a0=a0, b0=b0, ptb=ptb, st_=is_start,
              sp_=is_stop: e.matmul(
                  P2[:, a0:b0], vbh[:, kt * 128:(kt + 1) * 128],
                  ptb[:, a0:b0], start=st_, stop=sp_))
    ov_hist[g] = p.val("pe")


_NC_CACHE = []


def kernel(L0_attn_X, L0_attn_Wq, L0_attn_bq, L0_attn_Wk, L0_attn_bk,
           L0_attn_Wv, L0_attn_bv, L0_attn_Wo, L0_attn_bo, L0_ffn_weight,
           L0_ffn_weight_scale, L0_ffn_input_scale):
    X = np.ascontiguousarray(np.asarray(L0_attn_X, dtype=np.float32)
                             .reshape(B * S, D))
    wqTh = np.ascontiguousarray(np.asarray(L0_attn_Wq, dtype=np.float32).T)
    wkTh = np.ascontiguousarray(np.asarray(L0_attn_Wk, dtype=np.float32).T)
    wvTh = np.ascontiguousarray(np.asarray(L0_attn_Wv, dtype=np.float32).T)
    woTh = np.ascontiguousarray(np.asarray(L0_attn_Wo, dtype=np.float32).T)
    wf = np.asarray(L0_ffn_weight)
    wfTh = np.ascontiguousarray(wf.view(ml_dtypes.float8_e4m3).T)
    bqh = np.ascontiguousarray(
        np.asarray(L0_attn_bq, np.float32).reshape(16, 128).T)
    bkh = np.ascontiguousarray(
        np.asarray(L0_attn_bk, np.float32).reshape(16, 128).T)
    bvh = np.ascontiguousarray(np.asarray(L0_attn_bv, np.float32).reshape(1, D))
    boh = np.ascontiguousarray(np.asarray(L0_attn_bo, np.float32).reshape(1, D))
    wsch = np.asarray(L0_ffn_weight_scale, np.float32).reshape(1, 1)
    isch = np.asarray(L0_ffn_input_scale).reshape(1, 1)
    if isch.dtype != ml_dtypes.bfloat16:
        isch = isch.view(ml_dtypes.bfloat16) if isch.dtype.itemsize == 2 \
            else isch.astype(ml_dtypes.bfloat16)

    ident = np.eye(128, dtype=ml_dtypes.bfloat16)
    onc_h = np.ones((128, 1), dtype=ml_dtypes.bfloat16)
    onr_h = np.ones((1, 128), dtype=ml_dtypes.bfloat16)
    onrf_h = np.ones((1, 128), dtype=np.float32)
    kk, qq = np.meshgrid(np.arange(128), np.arange(128), indexing="ij")
    maskd_h = np.where(kk <= qq, 0.0, NEG).astype(np.float32)

    in_maps = []
    for c in range(NC):
        idxt_h = np.zeros((128, 64), dtype=np.int32)
        pp = np.arange(128, dtype=np.int32)
        for u in range(4):
            g = 4 * c + u
            h = g % 16
            for s_ in range(4):
                for r_ in range(3):
                    idxt_h[:, u * 12 + s_ * 3 + r_] = \
                        (4 * (c // 4) + s_) * 2048 + h * 128 + pp
        for h in range(16):
            orow = 1024 * (h % 4) + (4 * (c // 4) + h // 4) * 128 + pp
            idxt_h[:, 48 + h] = orow * 4 + (c % 4)
        in_maps.append({
            "xin": X[c * T:(c + 1) * T], "wqT": wqTh, "wkT": wkTh,
            "wvT": wvTh, "woT": woTh, "wfT": wfTh, "bqi": bqh, "bki": bkh,
            "bvi": bvh, "boi": boh, "wsc": wsch, "isc": isch,
            "id_in": ident, "onc_in": onc_h, "onr_in": onr_h, "onrf_in": onrf_h,
            "mask_in": maskd_h, "idxt": idxt_h,
        })

    if not _NC_CACHE:
        nc = bass.Bass("TRN2", target_bir_lowering=False)
        build(nc)
        _NC_CACHE.append(nc)
    nc = _NC_CACHE[0]
    res = run_bass_kernel_spmd(nc, in_maps, core_ids=list(range(NC)))
    out = np.concatenate([res.results[c]["yout"] for c in range(NC)], axis=0)
    return out.reshape(B, S, D)


# version 24
# speedup vs baseline: 1.1727x; 1.1067x over previous
import numpy as np
import ml_dtypes
import concourse.bass as bass
import concourse.mybir as mybir
from concourse.bass import IndirectOffsetOnAxis
from concourse.bass_utils import run_bass_kernel_spmd

F32 = mybir.dt.float32
BF16 = mybir.dt.bfloat16
FP8 = mybir.dt.float8e4
U32 = mybir.dt.uint32
I32 = mybir.dt.int32
ALU = mybir.AluOpType
AF = mybir.ActivationFunctionType
AX = mybir.AxisListType

D = 2048
S = 4096
B = 2
T = 1024
NC = 8
SC = float(128.0 ** -0.25)
EPS = 1e-5
NEG = -30.0
C24 = float(np.exp(24.0))
GRPS = [list(range(8))]
SH = 6144


class Prog:

    def __init__(self, nc):
        self.nc = nc
        self.ops = {"sync": [], "gpsimd": [], "vector": [], "scalar": [],
                    "tensor": []}
        self.sem = {}
        self.cnt = {}

    def mksem(self, name):
        self.sem[name] = self.nc.alloc_semaphore(name)
        self.cnt[name] = 0

    def emit(self, eng, fn):
        self.ops[eng].append(fn)

    def inc(self, eng, sem, n, fn):
        self.cnt[sem] += n
        h = self.sem[sem]
        self.ops[eng].append(lambda e, fn=fn, h=h, n=n: fn(e).then_inc(h, n))

    def dma(self, eng, sem, fn):
        self.inc(eng, sem, 16, fn)

    def wait(self, eng, sem, upto=None):
        v = self.cnt[sem] if upto is None else upto
        if v <= 0:
            return
        h = self.sem[sem]
        self.ops[eng].append(lambda e, h=h, v=v: e.wait_ge(h, v))

    def val(self, sem):
        return self.cnt[sem]

    def fence(self, name, fn):
        self.mksem(name)
        for _ in range(8):
            self.dma("gpsimd", name, fn)


def build(nc):
    xin = nc.dram_tensor("xin", [T, D], F32, kind="ExternalInput").ap()
    wqT = nc.dram_tensor("wqT", [D, D], F32, kind="ExternalInput").ap()
    wkT = nc.dram_tensor("wkT", [D, D], F32, kind="ExternalInput").ap()
    wvT = nc.dram_tensor("wvT", [D, D], F32, kind="ExternalInput").ap()
    woT = nc.dram_tensor("woT", [D, D], F32, kind="ExternalInput").ap()
    wfT = nc.dram_tensor("wfT", [D, D], FP8, kind="ExternalInput").ap()
    bqi = nc.dram_tensor("bqi", [128, 16], F32, kind="ExternalInput").ap()
    bki = nc.dram_tensor("bki", [128, 16], F32, kind="ExternalInput").ap()
    bvi = nc.dram_tensor("bvi", [1, D], F32, kind="ExternalInput").ap()
    boi = nc.dram_tensor("boi", [1, D], F32, kind="ExternalInput").ap()
    wsc = nc.dram_tensor("wsc", [1, 1], F32, kind="ExternalInput").ap()
    isc = nc.dram_tensor("isc", [1, 1], BF16, kind="ExternalInput").ap()
    id_in = nc.dram_tensor("id_in", [128, 128], BF16, kind="ExternalInput").ap()
    onc_in = nc.dram_tensor("onc_in", [128, 1], BF16, kind="ExternalInput").ap()
    onr_in = nc.dram_tensor("onr_in", [1, 128], BF16, kind="ExternalInput").ap()
    onrf_in = nc.dram_tensor("onrf_in", [1, 128], F32, kind="ExternalInput").ap()
    mask_in = nc.dram_tensor("mask_in", [128, 128], F32, kind="ExternalInput").ap()
    mexp_in = nc.dram_tensor("mexp_in", [128, 128], BF16, kind="ExternalInput").ap()
    idxt = nc.dram_tensor("idxt", [128, 64], I32, kind="ExternalInput").ap()
    yout = nc.dram_tensor("yout", [T, D], F32, kind="ExternalOutput").ap()

    cc1i = nc.dram_tensor("cc1i", [SH, 1024], BF16, kind="Internal").ap()
    cc1oq = nc.dram_tensor("cc1oq", [16384, 1024], BF16, kind="Internal",
                           addr_space="Shared").ap()
    cc1ok = nc.dram_tensor("cc1ok", [16384, 1024], BF16, kind="Internal",
                           addr_space="Shared").ap()
    cc1ov = nc.dram_tensor("cc1ov", [16384, 1024], BF16, kind="Internal",
                           addr_space="Shared").ap()
    cc2i = nc.dram_tensor("cc2i", [512, 4096], BF16, kind="Internal").ap()
    cc2o = nc.dram_tensor("cc2o", [4096, 4096], BF16, kind="Internal",
                          addr_space="Shared").ap()

    A = nc.alloc_sbuf_tensor
    idxt_sb = A("idxt_sb", [128, 64], I32)
    id_sb = A("id_sb", [128, 128], BF16)
    onc = A("onc", [128, 1], BF16)
    onr = A("onr", [1, 128], BF16)
    onrf = A("onrf", [1, 128], F32)
    maskd = A("maskd", [128, 128], F32)
    mexpd = A("mexpd", [128, 128], BF16)
    bqs = A("bqs", [128, 16], F32)
    bks = A("bks", [128, 16], F32)
    bvr = A("bvr", [1, D], BF16)
    bor = A("bor", [1, D], BF16)
    scl = A("scl", [1, 8], F32)
    iscb = A("iscb", [1, 1], BF16)
    rinv_bc = A("rinv_bc", [128, 1], F32)
    wssc_bc = A("wssc_bc", [128, 1], F32)
    nb_sb = A("nb_sb", [128, 1], F32)
    mv = A("mv", [128, 2], F32)
    stats = A("stats", [128, 24], F32)
    r1 = A("r1", [128, 1], F32)
    b2 = A("b2", [128, 1], F32)
    smax = A("smax", [1, 8], F32)
    cmx = A("cmx", [1, 16], F32)
    epsv = A("epsv", [128, 1], F32)
    oncf = A("oncf", [128, 1], F32)
    lsb = A("lsb", [1, 1024], F32)
    lnc = A("lnc", [1, 1], F32)
    lsb2 = A("lsb2", [1, 1024], F32)
    r1a = A("r1a", [128, 1], F32)
    glt = A("glt", [128, 16], BF16)

    xts = [A(f"xts{i}", [128, D], F32) for i in range(2)]
    xh_ts = [A(f"xh{i}", [128, D], BF16) for i in range(2)]
    stg = [A(f"stg{i}", [128, 512], BF16) for i in range(4)]

    xhat_fm = A("xhat_fm", [128, 16 * 1024], BF16)
    a_xf = nc.lookup_mloc(xhat_fm).addr
    oall = nc.alloc_sbuf_tensor_at("oall", [128, 16 * 1024], BF16, offset=a_xf)

    wb = A("wb", [128, 16 * D], BF16)
    a_wb = nc.lookup_mloc(wb).addr
    wf_sb = nc.alloc_sbuf_tensor_at("wf_sb", [128, 16 * D], FP8, offset=a_wb)
    xq_fm = nc.alloc_sbuf_tensor_at("xq_fm", [128, 16 * 1024], FP8,
                                    offset=a_wb + 32768)

    x1 = A("x1", [128, 8 * D], F32)
    a_x1 = nc.lookup_mloc(x1).addr
    qkv = nc.alloc_sbuf_tensor_at("qkv", [128, 3 * 4096], BF16, offset=a_x1)
    qbh = qkv[:, 0:4096]
    kbh = qkv[:, 4096:8192]
    vbh = qkv[:, 8192:12288]
    pt_sb = [nc.alloc_sbuf_tensor_at(f"pt{i}", [128, 1024], BF16,
                                     offset=a_x1 + 24576 + i * 2048)
             for i in range(2)]
    linv_bc = nc.alloc_sbuf_tensor_at("linv_bc", [128, 1024], F32,
                                      offset=a_x1 + 28672)
    osb = nc.alloc_sbuf_tensor_at("osb", [128, 1024], BF16,
                                  offset=a_x1 + 32768)
    qsq = nc.alloc_sbuf_tensor_at("qsq", [128, 512], BF16,
                                  offset=a_x1 + 34816)
    a_xts0 = nc.lookup_mloc(xts[0]).addr
    a_xts1 = nc.lookup_mloc(xts[1]).addr
    tq = nc.alloc_sbuf_tensor_at("tq", [128, D], F32, offset=a_xts0)
    xh2 = nc.alloc_sbuf_tensor_at("xh2", [128, D], F32, offset=a_xts1)
    a_xh0 = nc.lookup_mloc(xh_ts[0]).addr
    q8 = nc.alloc_sbuf_tensor_at("q8", [128, D], FP8, offset=a_xh0)
    a_stg0 = nc.lookup_mloc(stg[0]).addr
    assert nc.lookup_mloc(stg[3]).addr == a_stg0 + 3 * 1024
    sqts = nc.alloc_sbuf_tensor_at("sqts", [128, D], BF16, offset=a_stg0)
    a_xh1 = nc.lookup_mloc(xh_ts[1]).addr
    sqtsd = nc.alloc_sbuf_tensor_at("sqtsd", [128, D], BF16, offset=a_xh1)
    wb2 = nc.alloc_sbuf_tensor_at("wb2", [128, 16 * D], BF16, offset=a_x1)
    qkv2 = nc.alloc_sbuf_tensor_at("qkv2", [128, 3 * 4096], BF16,
                                   offset=a_x1 + 36864)
    qkvs = [qkv, qkv2]
    ptsum = nc.alloc_sbuf_tensor_at("ptsum", [128, 1024], F32,
                                    offset=a_x1 + 61440)
    ptsum2 = nc.alloc_sbuf_tensor_at("ptsum2", [128, 1024], F32,
                                     offset=a_xh1)
    ysb = nc.alloc_sbuf_tensor_at("ysb", [128, D], F32, offset=a_xf)
    ptsums = [ptsum, ptsum2]
    praw = nc.alloc_sbuf_tensor_at("praw", [128, 1024], BF16, offset=a_stg0)
    pt_sb3 = nc.alloc_sbuf_tensor_at("pt3", [128, 1024], BF16,
                                     offset=a_stg0 + 2048)

    P0 = nc.alloc_psum_tensor("P0", [128, 1024], F32)
    P1 = nc.alloc_psum_tensor("P1", [128, 1024], F32)
    P2 = nc.alloc_psum_tensor("P2", [128, 1024], F32)
    L0 = nc.alloc_psum_tensor("L0", [128, 512], F32)
    L1 = nc.alloc_psum_tensor("L1", [128, 512], F32)
    banks4 = [P0[:, 0:512], P0[:, 512:1024], P1[:, 0:512], P1[:, 512:1024]]

    p = Prog(nc)
    sems = ["ld", "st", "wt", "pe", "ve", "se", "cc", "gp", "gl", "junk",
            "ob0", "ob1", "ob2", "ob3", "xr0", "xr1", "wfl"]
    for w_ in ("q", "k", "v", "y"):
        for k_ in range(4):
            sems.append(f"sg{w_}{k_}")
    for s_ in sems:
        p.mksem(s_)
    vals = {}

    for out_, in_ in ((idxt_sb, idxt), (id_sb, id_in), (onc, onc_in),
                      (onr, onr_in), (onrf, onrf_in), (maskd, mask_in),
                      (mexpd, mexp_in), (bqs, bqi),
                      (bks, bki)):
        p.dma("sync", "ld", lambda e, o=out_, i=in_: e.dma_start(
            out=o[:, :], in_=i[:, :]))
    p.dma("sync", "ld", lambda e: e.dma_start(out=scl[0:1, 0:1], in_=wsc[:, :]))
    start_ld = p.val("ld")
    p.dma("gpsimd", "junk", lambda e: e.dma_start(out=bvr[:, :], in_=bvi[:, :]))
    p.dma("gpsimd", "junk", lambda e: e.dma_start(out=bor[:, :], in_=boi[:, :]))
    p.dma("gpsimd", "junk", lambda e: e.dma_start(out=iscb[:, :], in_=isc[:, :]))
    p.fence("f_sm", lambda e: e.dma_start(out=glt[:, :], in_=xin[0:128, 0:16]))
    for j in range(16):
        p.dma("gpsimd", "junk", lambda e, j=j: e.dma_start(
            out=wb[:, j * D:(j + 1) * D], in_=wqT[j * 128:(j + 1) * 128, :]))
    p.fence("f_wq", lambda e: e.dma_start(out=glt[:, :], in_=xin[0:128, 0:16]))

    p.wait("vector", "ld", start_ld)
    p.inc("vector", "ve", 1, lambda e: e.memset(epsv[:, :], EPS))
    p.inc("vector", "ve", 1, lambda e: e.memset(lnc[:, :], 24.0))
    p.inc("vector", "ve", 1, lambda e: e.memset(oncf[:, :], 1.0))
    p.inc("vector", "ve", 1, lambda e: e.tensor_scalar(
        bqs[:, :], bqs[:, :], SC, None, ALU.mult))
    p.inc("vector", "ve", 1, lambda e: e.tensor_scalar(
        bks[:, :], bks[:, :], SC, None, ALU.mult))
    p.wait("vector", "f_sm", 128)
    p.inc("vector", "ve", 1, lambda e: e.tensor_copy(scl[0:1, 3:4], iscb[:, :]))
    ve_scl = p.val("ve")
    p.wait("scalar", "ve", ve_scl)
    p.inc("scalar", "se", 1, lambda e: e.activation(
        scl[0:1, 4:5], scl[0:1, 3:4], AF.Ln))
    p.inc("scalar", "se", 1, lambda e: e.activation(
        scl[0:1, 1:2], scl[0:1, 4:5], AF.Exp, scale=-1.0))
    se_scl = p.val("se")
    p.wait("tensor", "ve", ve_scl)
    p.wait("tensor", "se", se_scl)
    p.inc("tensor", "pe", 1, lambda e: e.matmul(
        P2[:, 0:1], onrf[0:1, :], scl[0:1, 1:2], start=True, stop=True))
    p.inc("tensor", "pe", 1, lambda e: e.matmul(
        P2[:, 1:2], onrf[0:1, :], scl[0:1, 0:1], start=True, stop=True))
    p.inc("tensor", "pe", 1, lambda e: e.matmul(
        P2[:, 2:3], onrf[0:1, :], scl[0:1, 3:4], start=True, stop=True))
    pe_bc0 = p.val("pe")
    p.wait("scalar", "pe", pe_bc0)
    p.inc("scalar", "se", 1, lambda e: e.copy(rinv_bc[:, :], P2[:, 0:1]))
    p.inc("scalar", "se", 1, lambda e: e.copy(nb_sb[:, :], P2[:, 1:2]))
    se_bcast = p.val("se")
    p.wait("vector", "pe", pe_bc0)
    p.wait("vector", "se", se_bcast)
    p.inc("vector", "ve", 1, lambda e: e.tensor_tensor(
        wssc_bc[:, :], nb_sb[:, :], P2[:, 2:3], ALU.mult))
    ve_bcast = p.val("ve")

    def ln_tiles(src_getter, dst_q8=False):
        pass

    prev_se_r1 = 0
    prev_lnapply = {}
    prev_trev = {-1: 0}
    for t in range(8):
        xb, hb = xts[t % 2], xh_ts[t % 2]
        if t >= 2:
            p.wait("sync", "se", prev_lnapply[t - 2])
        p.dma("sync", "ld", lambda e, t=t, xb=xb: e.dma_start(
            out=xb[:, :], in_=xin[t * 128:(t + 1) * 128, :]))
        ld_here = p.val("ld")
        p.wait("scalar", "ld", ld_here)
        if t >= 1:
            p.wait("scalar", "ve", vals.get("ln1ve", 0))
        p.inc("scalar", "se", 1, lambda e, xb=xb: e.activation(
            sqts[:, :], xb[:, :], AF.Square))
        se_sq = p.val("se")
        p.wait("vector", "ld", ld_here)
        p.wait("vector", "se", se_sq)
        if t >= 1:
            p.wait("vector", "se", prev_se_r1)
        p.inc("vector", "ve", 1, lambda e, xb=xb: e.reduce_sum(
            mv[:, 0:1], xb[:, :], axis=AX.X))
        p.inc("vector", "ve", 1, lambda e: e.reduce_sum(
            mv[:, 1:2], sqts[:, :], axis=AX.X))
        p.inc("vector", "ve", 1, lambda e: e.tensor_scalar(
            stats[:, 0:1], mv[:, 0:1], 1.0 / D, None, ALU.mult))
        p.inc("vector", "ve", 1, lambda e: e.scalar_tensor_tensor(
            stats[:, 1:2], mv[:, 0:1], -1.0 / (D * D), mv[:, 0:1],
            ALU.mult, ALU.mult))
        ve_stats = p.val("ve")
        vals["ln1ve"] = ve_stats
        p.wait("scalar", "ve", ve_stats)
        p.inc("scalar", "se", 1, lambda e: e.activation(
            stats[:, 3:4], mv[:, 1:2], AF.Identity, bias=stats[:, 1:2],
            scale=1.0 / D))
        p.inc("scalar", "se", 1, lambda e: e.activation(
            r1a[:, :], stats[:, 3:4], AF.Ln, bias=epsv[:, :], scale=1.0))
        p.inc("scalar", "se", 1, lambda e: e.activation(
            r1[:, :], r1a[:, :], AF.Exp, bias=0.0, scale=-0.5))
        prev_se_r1 = p.val("se")
        p.wait("vector", "se", prev_se_r1)
        p.inc("vector", "ve", 1, lambda e: e.scalar_tensor_tensor(
            b2[:, :], stats[:, 0:1], -1.0, r1[:, :], ALU.mult, ALU.mult))
        ve_b2 = p.val("ve")
        p.wait("scalar", "ve", ve_b2)
        if t >= 2:
            p.wait("scalar", "pe", vals[f"tr{t-2}"])
        p.inc("scalar", "se", 1, lambda e, xb=xb, hb=hb: e.activation(
            hb[:, :], xb[:, :], AF.Identity, bias=b2[:, :], scale=r1[:, :]))
        prev_lnapply[t] = p.val("se")
        for rnd in range(2):
            ps = P2 if rnd == 0 else P1
            if t == 0 and rnd == 0:
                p.wait("tensor", "se", se_bcast)
                p.wait("tensor", "ve", ve_bcast)
            p.wait("tensor", "se", prev_lnapply[t])
            p.wait("tensor", "ve", prev_trev.get(t - 1, prev_trev[-1])
                   if rnd == 0 else vals.get(f"trevP1_{t-1}", 0))
            for jj in range(8):
                j = rnd * 8 + jj
                p.inc("tensor", "pe", 1,
                      lambda e, hb=hb, j=j, jj=jj, ps=ps: e.matmul(
                          ps[:, jj * 128:(jj + 1) * 128],
                          hb[:, j * 128:(j + 1) * 128], id_sb[:, :],
                          start=True, stop=True))
            pe_tr = p.val("pe")
            p.wait("vector", "pe", pe_tr)
            p.inc("vector", "ve", 1,
                  lambda e, t=t, rnd=rnd, ps=ps: e.tensor_copy(
                      xhat_fm[:, :].rearrange("p (j c) -> p j c", j=16)
                      [:, rnd * 8:(rnd + 1) * 8, t * 128:(t + 1) * 128],
                      ps[:, :].rearrange("p (j c) -> p j c", j=8)))
            if rnd == 0:
                prev_trev[t] = p.val("ve")
            else:
                vals[f"trevP1_{t}"] = p.val("ve")
        vals[f"tr{t}"] = p.val("pe")
    xhat_ve = p.val("ve")

    def proj_qk(which, wsem, bias_sb, base, wbuf, prevsec=None):
        for i, (oc, tb) in enumerate([(o, t2) for o in range(16)
                                      for t2 in range(2)]):
            bank, sb = banks4[i % 4], stg[i % 4]
            sem = f"sg{which}{i % 4}"
            p.wait("tensor", wsem, 128)
            if i == 0:
                p.wait("tensor", "ve", xhat_ve)
            if i >= 4:
                p.wait("tensor", sem, vals[f"{which}dma{i-4}"])
            elif prevsec:
                p.wait("tensor", f"sg{prevsec}{i % 4}",
                       vals[f"{prevsec}sgfull{i % 4}"])
            for j in range(16):
                p.inc("tensor", "pe", 1,
                      lambda e, j=j, oc=oc, tb=tb, bank=bank, wbuf=wbuf:
                      e.matmul(
                          bank, wbuf[:, j * D + oc * 128: j * D + (oc + 1) * 128],
                          xhat_fm[:, j * 1024 + tb * 512:
                                  j * 1024 + (tb + 1) * 512],
                          start=(j == 0), stop=(j == 15)))
            pe_here = p.val("pe")
            p.wait("scalar", "pe", pe_here)
            p.inc("scalar", "se", 1,
                  lambda e, bank=bank, sb=sb, oc=oc, bias_sb=bias_sb:
                  e.activation(sb[:, :], bank, AF.Identity,
                               bias=bias_sb[:, oc:oc + 1], scale=SC))
            p.dma("scalar", sem,
                  lambda e, sb=sb, oc=oc, tb=tb, base=base: e.dma_start(
                      out=cc1i[oc * 128 + base: oc * 128 + base + 128,
                               tb * 512:(tb + 1) * 512], in_=sb[:, :]))
            vals[f"{which}dma{i}"] = p.val(sem)
        for k_ in range(4):
            vals[f"{which}sgfull{k_}"] = p.val(f"sg{which}{k_}")

    for j in range(16):
        p.dma("gpsimd", "junk", lambda e, j=j: e.dma_start(
            out=wb2[:, j * D:(j + 1) * D], in_=wkT[j * 128:(j + 1) * 128, :]))
    p.fence("f_wk", lambda e: e.dma_start(out=glt[:, :], in_=xin[0:128, 0:16]))
    proj_qk("q", "f_wq", bqs, 0, wb)
    q_mms = p.val("pe")
    for k_ in range(4):
        p.wait("gpsimd", f"sgq{k_}", vals[f"qsgfull{k_}"])
    p.inc("gpsimd", "cc", 1, lambda e: e.collective_compute(
        "AllGather", ALU.bypass, ins=[cc1i[0:2048, :]], outs=[cc1oq[:, :]],
        replica_groups=GRPS))
    p.wait("gpsimd", "pe", q_mms)
    for j in range(16):
        p.dma("gpsimd", "junk", lambda e, j=j: e.dma_start(
            out=wb[:, j * D:(j + 1) * D], in_=wvT[j * 128:(j + 1) * 128, :]))
    p.fence("f_wv", lambda e: e.dma_start(out=glt[:, :], in_=xin[0:128, 0:16]))
    proj_qk("k", "f_wk", bks, 2048, wb2, prevsec="q")
    for k_ in range(4):
        p.wait("gpsimd", f"sgk{k_}", vals[f"ksgfull{k_}"])
    p.inc("gpsimd", "cc", 1, lambda e: e.collective_compute(
        "AllGather", ALU.bypass, ins=[cc1i[2048:4096, :]], outs=[cc1ok[:, :]],
        replica_groups=GRPS))

    for i, (tt, oc4) in enumerate([(t2, o) for t2 in range(8)
                                   for o in range(4)]):
        bank, sb = banks4[i % 4], stg[i % 4]
        sem = f"sgv{i % 4}"
        p.wait("tensor", "f_wv", 128)
        if i >= 4:
            p.wait("tensor", sem, vals[f"vdma{i-4}"])
        else:
            p.wait("tensor", f"sgk{i % 4}", vals[f"ksgfull{i % 4}"])
        for j in range(16):
            p.inc("tensor", "pe", 1,
                  lambda e, j=j, tt=tt, oc4=oc4, bank=bank: e.matmul(
                      bank,
                      xhat_fm[:, j * 1024 + tt * 128: j * 1024 + (tt + 1) * 128],
                      wb[:, j * D + oc4 * 512: j * D + (oc4 + 1) * 512],
                      start=(j == 0), stop=False))
        p.inc("tensor", "pe", 1, lambda e, oc4=oc4, bank=bank: e.matmul(
            bank, onr[:, :], bvr[0:1, oc4 * 512:(oc4 + 1) * 512],
            start=False, stop=True))
        pe_here = p.val("pe")
        p.wait("scalar", "pe", pe_here)
        p.inc("scalar", "se", 1, lambda e, bank=bank, sb=sb: e.activation(
            sb[:, :], bank, AF.Identity))
        p.dma("scalar", sem, lambda e, sb=sb, tt=tt, oc4=oc4: e.dma_start(
            out=cc1i[4096:6144, tt * 128:(tt + 1) * 128]
            .rearrange("(i q) c -> q i c", q=128)
            [:, 4 * oc4:4 * oc4 + 4, :],
            in_=sb[:, :].rearrange("p (i c) -> p i c", i=4)))
        vals[f"vdma{i}"] = p.val(sem)

    for k_ in range(4):
        p.wait("gpsimd", f"sgv{k_}", p.val(f"sgv{k_}"))
    p.inc("gpsimd", "cc", 1, lambda e: e.collective_compute(
        "AllGather", ALU.bypass, ins=[cc1i[4096:6144, :]], outs=[cc1ov[:, :]],
        replica_groups=GRPS))

    p.wait("gpsimd", "ld", start_ld)
    pt_sb.append(pt_sb3)
    gstep = 0
    exp_hist = {}
    ov_hist = {}
    pts_hist = {}
    secs = (cc1oq, cc1ok, cc1ov)

    def emit_fin_l(fin):
        p.wait("tensor", "ve", pts_hist[fin["gend"] - 1])
        p.wait("tensor", "se", vals.get("se_lbank_free", 0))
        pts = fin["pts"]
        p.inc("tensor", "pe", 1, lambda e, pts=pts: e.matmul(
            L0[0:1, :], oncf[:, :], pts[:, 0:512], start=True, stop=True))
        p.inc("tensor", "pe", 1, lambda e, pts=pts: e.matmul(
            L1[0:1, :], oncf[:, :], pts[:, 512:1024], start=True, stop=True))
        vals[f"ptsum_free{fin['qc'] % 2}"] = p.val("pe")
        pe_qc = p.val("pe")
        p.wait("scalar", "pe", pe_qc)
        p.inc("scalar", "se", 1, lambda e: e.activation(
            lsb[0:1, 0:512], L0[0:1, :], AF.Ln, scale=C24))
        p.inc("scalar", "se", 1, lambda e: e.activation(
            lsb[0:1, 512:1024], L1[0:1, :], AF.Ln, scale=C24))
        p.inc("scalar", "se", 1, lambda e: e.activation(
            lsb2[0:1, :], lsb[0:1, :], AF.Exp, bias=lnc[0:1, :], scale=-1.0))
        vals["se_l_last"] = p.val("se")

    def emit_fin_osb(fin):
        u, qc = fin["u"], fin["qc"]
        p.wait("tensor", "se", vals["se_l_last"])
        p.inc("tensor", "pe", 1, lambda e: e.matmul(
            L0[:, :], onrf[0:1, :], lsb2[0:1, 0:512], start=True, stop=True))
        p.inc("tensor", "pe", 1, lambda e: e.matmul(
            L1[:, :], onrf[0:1, :], lsb2[0:1, 512:1024], start=True, stop=True))
        pe_l = p.val("pe")
        p.wait("scalar", "pe", pe_l)
        p.inc("scalar", "se", 1, lambda e: e.copy(linv_bc[:, 0:512], L0[:, :]))
        p.inc("scalar", "se", 1, lambda e: e.copy(
            linv_bc[:, 512:1024], L1[:, :]))
        se_linv = p.val("se")
        vals["se_lbank_free"] = se_linv
        p.wait("vector", "se", se_linv)
        p.wait("vector", f"ob{u}", p.val(f"ob{u}"))
        p.inc("vector", "ve", 1, lambda e: e.tensor_tensor(
            osb[:, :], praw[:, :], linv_bc[:, :], ALU.mult))
        ve_o = p.val("ve")
        p.wait("scalar", "ve", ve_o)
        p.dma("scalar", f"ob{u}", lambda e, u=u, qc=qc: e.dma_start(
            out=cc2i[u * 128:(u + 1) * 128, qc * 1024:(qc + 1) * 1024],
            in_=osb[:, :]))

    def emit_fin_praw(fin):
        p.wait("vector", "pe", ov_hist[fin["gend"] - 1])
        p.inc("vector", "ve", 1, lambda e: e.tensor_copy(
            praw[:, :], P2[:, :]))
        fin["ve_praw"] = p.val("ve")

    def emit_ag2(u):
        p.wait("gpsimd", f"ob{u}", 64)
        p.inc("gpsimd", "cc", 1, lambda e, u=u: e.collective_compute(
            "AllGather", ALU.bypass, ins=[cc2i[u * 128:(u + 1) * 128, :]],
            outs=[cc2o[u * 1024:(u + 1) * 1024, :]],
            replica_groups=GRPS))

    def emit_opiece(pu):
        p.wait("gpsimd", "cc", 4 + pu)
        for h in range(pu, 16, 4):
            p.dma("gpsimd", "junk", lambda e, h=h: e.indirect_dma_start(
                oall[:, h * 1024:(h + 1) * 1024], None,
                cc2o[:, :].rearrange("a (b c) -> (a b) c", c=1024),
                IndirectOffsetOnAxis(idxt_sb[:, 48 + h:49 + h], 0)))

    for u in range(4):
        qkvu = qkvs[u % 2]
        qb = qkvu[:, 0:4096]
        kb = qkvu[:, 4096:8192]
        vb = qkvu[:, 8192:12288]
        if u == 1:
            for j in range(16):
                p.dma("gpsimd", "junk", lambda e, j=j: e.dma_start(
                    out=wb[:, j * D:(j + 1) * D],
                    in_=woT[j * 128:(j + 1) * 128, :]))
            p.fence("f_wo", lambda e: e.dma_start(
                out=glt[:, :], in_=xin[0:128, 0:16]))
        if u >= 2:
            emit_ag2(u - 2)
            emit_opiece(u - 2)
            p.wait("gpsimd", "pe", vals[f"unitpe{u-2}"])
            p.wait("gpsimd", "se", vals[f"unitse{u-2}"])
        if u == 0:
            p.wait("gpsimd", "cc", 2)
        for s_ in range(4):
            for r_ in range(2):
                p.dma("gpsimd", "junk", lambda e, qkvu=qkvu, u=u, s_=s_, r_=r_:
                      e.indirect_dma_start(
                    qkvu[:, r_ * 4096 + s_ * 1024: r_ * 4096 + (s_ + 1) * 1024],
                    None, secs[r_][:, :],
                    IndirectOffsetOnAxis(
                        idxt_sb[:, u * 12 + s_ * 3 + r_:
                                u * 12 + s_ * 3 + r_ + 1], 0)))
        p.fence(f"f_uqk{u}", lambda e: e.dma_start(
            out=glt[:, :], in_=cc1oq[0:128, 0:16]))
        if u == 0:
            p.wait("gpsimd", "cc", 3)
        for s_ in range(4):
            p.dma("gpsimd", "junk", lambda e, qkvu=qkvu, u=u, s_=s_:
                  e.indirect_dma_start(
                qkvu[:, 8192 + s_ * 1024: 8192 + (s_ + 1) * 1024],
                None, secs[2][:, :],
                IndirectOffsetOnAxis(
                    idxt_sb[:, u * 12 + s_ * 3 + 2:
                            u * 12 + s_ * 3 + 3], 0)))
        p.fence(f"f_uv{u}", lambda e: e.dma_start(
            out=glt[:, :], in_=cc1oq[0:128, 0:16]))

        p.wait("scalar", f"f_uqk{u}", 128)
        if u > 0:
            p.wait("vector", "se", vals[f"unitse{u-1}"])
        for w, (src, slot) in enumerate(((qb, 0), (kb, 1))):
            for c8 in range(8):
                if c8 > 0 or w > 0:
                    p.wait("scalar", "pe", p.val("pe"))
                p.inc("scalar", "se", 1, lambda e, src=src, c8=c8: e.activation(
                    qsq[:, :], src[:, c8 * 512:(c8 + 1) * 512], AF.Square))
                se_h = p.val("se")
                p.wait("tensor", "se", se_h)
                p.wait("tensor", "ve", p.val("ve"))
                p.inc("tensor", "pe", 1, lambda e: e.matmul(
                    L0[0:1, :], onc[:, :], qsq[:, :], start=True, stop=True))
                pe_h = p.val("pe")
                p.wait("vector", "pe", pe_h)
                p.inc("vector", "ve", 1,
                      lambda e, w=w, c8=c8: e.reduce_max(
                          cmx[0:1, w * 8 + c8:w * 8 + c8 + 1],
                          L0[0:1, :], axis=AX.X))
        p.wait("scalar", "ve", p.val("ve"))
        p.inc("scalar", "se", 1, lambda e: e.copy(
            smax[0:1, 7:8], scl[0:1, 0:1]))
        p.wait("vector", "se", p.val("se"))
        p.inc("vector", "ve", 1, lambda e: e.reduce_max(
            smax[0:1, 0:1], cmx[0:1, 0:8], axis=AX.X))
        p.inc("vector", "ve", 1, lambda e: e.reduce_max(
            smax[0:1, 1:2], cmx[0:1, 8:16], axis=AX.X))
        p.wait("scalar", "ve", p.val("ve"))
        p.inc("scalar", "se", 1, lambda e: e.activation(
            smax[0:1, 2:3], smax[0:1, 0:1], AF.Ln))
        p.inc("scalar", "se", 1, lambda e: e.activation(
            smax[0:1, 5:6], smax[0:1, 1:2], AF.Ln))
        p.wait("vector", "se", p.val("se"))
        p.inc("vector", "ve", 1, lambda e: e.tensor_tensor(
            smax[0:1, 3:4], smax[0:1, 2:3], smax[0:1, 5:6], ALU.add))
        p.wait("scalar", "ve", p.val("ve"))
        p.inc("scalar", "se", 1, lambda e: e.activation(
            smax[0:1, 6:7], smax[0:1, 3:4], AF.Exp, scale=0.5))
        se_bnd = p.val("se")
        p.wait("vector", "se", se_bnd)
        p.inc("vector", "ve", 1, lambda e: e.tensor_scalar(
            smax[0:1, 4:5], smax[0:1, 6:7], -1.0, None, ALU.mult))
        ve_nb = p.val("ve")
        p.wait("tensor", "ve", ve_nb)
        p.inc("tensor", "pe", 1, lambda e: e.matmul(
            P2[:, 0:1], onrf[0:1, :], smax[0:1, 4:5], start=True, stop=True))
        pe_nb = p.val("pe")
        p.wait("scalar", "pe", pe_nb)
        p.inc("scalar", "se", 1, lambda e: e.copy(nb_sb[:, :], P2[:, 0:1]))
        se_nb = p.val("se")
        p.wait("tensor", f"f_uqk{u}", 128)
        p.wait("tensor", f"f_uv{u}", 128)

        fin = None
        for qc in range(4):
            qcb = qc * 1024
            nkt = 8 * qc + 8
            pq = []
            for kt in range(nkt):
                w0 = max(kt * 128 - qcb, 0)
                stp = P0 if gstep % 2 == 0 else P1
                ptb = pt_sb[gstep % 3]
                if fin is not None:
                    if kt == 0:
                        emit_fin_praw(fin)
                    elif kt == 1:
                        emit_fin_l(fin)
                        p.wait("tensor", "ve", fin["ve_praw"])
                    elif kt == 2:
                        emit_fin_osb(fin)
                        fin = None
                if gstep - 2 in exp_hist:
                    p.wait("tensor", "se", exp_hist[gstep - 2])
                for a in ((w0, 512), (512, 1024)):
                    a0, b0 = max(a[0], w0), a[1]
                    if b0 <= w0 or b0 <= a0:
                        continue
                    p.inc("tensor", "pe", 1,
                          lambda e, kt=kt, a0=a0, b0=b0, qcb=qcb, stp=stp,
                          kb=kb, qb=qb:
                          e.matmul(stp[:, a0:b0],
                                   kb[:, kt * 128:(kt + 1) * 128],
                                   qb[:, qcb + a0:qcb + b0],
                                   start=True, stop=True))
                qk_here = p.val("pe")
                masked = kt * 128 >= qcb
                p.wait("scalar", "pe", qk_here)
                if gstep - 3 in ov_hist:
                    p.wait("scalar", "pe", ov_hist[gstep - 3])
                if gstep - 3 in pts_hist:
                    p.wait("scalar", "ve", pts_hist[gstep - 3])
                p.inc("scalar", "se", 1,
                      lambda e, stp=stp, ptb=ptb, w0=w0: e.activation(
                          ptb[:, w0:1024], stp[:, w0:1024], AF.Exp,
                          bias=nb_sb[:, :], scale=1.0))
                exp_hist[gstep] = p.val("se")
                p.wait("vector", "se", exp_hist[gstep])
                if masked:
                    p.inc("vector", "ve", 1,
                          lambda e, ptb=ptb, w0=w0: e.tensor_tensor(
                              ptb[:, w0:w0 + 128], ptb[:, w0:w0 + 128],
                              mexpd[:, :], ALU.mult))
                pts = ptsums[qc % 2]
                if kt == 0:
                    p.wait("vector", "pe", vals.get(f"ptsum_free{qc % 2}", 0))
                    p.inc("vector", "ve", 1,
                          lambda e, ptb=ptb, pts=pts: e.tensor_copy(
                              pts[:, :], ptb[:, :]))
                else:
                    p.inc("vector", "ve", 1,
                          lambda e, ptb=ptb, w0=w0, pts=pts: e.tensor_tensor(
                              pts[:, w0:1024], pts[:, w0:1024],
                              ptb[:, w0:1024], ALU.add))
                pts_hist[gstep] = p.val("ve")
                pq.append((kt, w0, gstep % 3, gstep, kt == 0, False,
                           pts_hist[gstep] if masked else 0))
                if len(pq) > 2:
                    emit_ov(p, pq.pop(0), P2, L0, L1, vb, onc, pt_sb,
                            exp_hist, ov_hist)
                gstep += 1
            pq[-1] = pq[-1][:5] + (True, pq[-1][6])
            for it in pq:
                emit_ov(p, it, P2, L0, L1, vb, onc, pt_sb,
                        exp_hist, ov_hist)
            pq = []
            fin = {"gend": gstep, "pts": ptsums[qc % 2], "u": u, "qc": qc}
        emit_fin_praw(fin)
        emit_fin_l(fin)
        p.wait("tensor", "ve", fin["ve_praw"])
        emit_fin_osb(fin)
        fin = None
        vals[f"unitpe{u}"] = p.val("pe")
        vals[f"unitse{u}"] = p.val("se")
        vals[f"odma{u}"] = p.val(f"ob{u}")

    emit_ag2(2)
    emit_opiece(2)
    emit_ag2(3)
    emit_opiece(3)
    p.fence("f_og", lambda e: e.dma_start(out=glt[:, :], in_=cc2o[0:128, 0:16]))

    p.wait("sync", "cc", 7)
    for i, (tt, oc4) in enumerate([(t2, o) for t2 in range(8)
                                   for o in range(4)]):
        bank = banks4[i % 4]
        if oc4 == 0:
            xb = xts[tt % 2]
            if tt >= 2:
                p.wait("sync", "ve", vals[f"resid{tt-2}_3"])
            p.dma("sync", f"xr{tt % 2}", lambda e, tt=tt, xb=xb: e.dma_start(
                out=xb[:, :], in_=xin[tt * 128:(tt + 1) * 128, :]))
            vals[f"xl{tt}"] = p.val(f"xr{tt % 2}")
        p.wait("tensor", "f_wo", 128)
        if i == 0:
            p.wait("tensor", "f_og", 128)
        if i >= 4:
            p.wait("tensor", "ve", vals[f"resid{(i-4)//4}_{(i-4)%4}"])
        for j in range(16):
            p.inc("tensor", "pe", 1,
                  lambda e, j=j, tt=tt, oc4=oc4, bank=bank: e.matmul(
                      bank,
                      oall[:, j * 1024 + tt * 128: j * 1024 + (tt + 1) * 128],
                      wb[:, j * D + oc4 * 512: j * D + (oc4 + 1) * 512],
                      start=(j == 0), stop=False))
        p.inc("tensor", "pe", 1, lambda e, oc4=oc4, bank=bank: e.matmul(
            bank, onr[:, :], bor[0:1, oc4 * 512:(oc4 + 1) * 512],
            start=False, stop=True))
        pe_here = p.val("pe")
        p.wait("vector", "pe", pe_here)
        p.wait("vector", f"xr{tt % 2}", vals[f"xl{tt}"])
        p.inc("vector", "ve", 1,
              lambda e, tt=tt, oc4=oc4, bank=bank, xb=xts[tt % 2]:
              e.tensor_tensor(
                  x1[:, tt * D + oc4 * 512: tt * D + (oc4 + 1) * 512],
                  bank, xb[:, oc4 * 512:(oc4 + 1) * 512], ALU.add))
        vals[f"resid{tt}_{oc4}"] = p.val("ve")
    op_mms = p.val("pe")

    p.wait("sync", "pe", op_mms)
    for j in range(16):
        p.dma("sync", "wfl", lambda e, j=j: e.dma_start(
            out=wf_sb[:, j * D:(j + 1) * D], in_=wfT[j * 128:(j + 1) * 128, :]))
    wf_done = p.val("wfl")

    def emit_ffn_pe(tt):
        p.wait("tensor", "se", vals[f"q8done{tt}"])
        for oc4 in range(4):
            i = tt * 4 + oc4
            bank = banks4[i % 4]
            if i >= 4:
                p.wait("tensor", "ve", vals[f"yev{i-4}"])
            for j in range(16):
                p.inc("tensor", "pe", 1,
                      lambda e, j=j, tt=tt, oc4=oc4, bank=bank: e.matmul(
                          bank,
                          xq_fm[:, j * 1024 + tt * 128: j * 1024 + (tt + 1) * 128],
                          wf_sb[:, j * D + oc4 * 512: j * D + (oc4 + 1) * 512],
                          start=(j == 0), stop=(j == 15)))
            vals[f"ympe{i}"] = p.val("pe")

    def emit_ffn_evac(tt):
        for oc4 in range(4):
            i = tt * 4 + oc4
            bank = banks4[i % 4]
            p.wait("vector", "pe", vals[f"ympe{i}"])
            if i >= 4:
                p.wait("vector", f"sgy{(i - 4) % 4}", vals[f"ydma{i-4}"])
            p.inc("vector", "ve", 1,
                  lambda e, tt=tt, oc4=oc4, bank=bank: e.scalar_tensor_tensor(
                      ysb[:, oc4 * 512:(oc4 + 1) * 512], bank, wssc_bc[:, 0:1],
                      x1[:, tt * D + oc4 * 512: tt * D + (oc4 + 1) * 512],
                      ALU.mult, ALU.add))
            vals[f"yev{i}"] = p.val("ve")
            p.wait("scalar", "ve", vals[f"yev{i}"])
            p.dma("scalar", f"sgy{i % 4}", lambda e, tt=tt, oc4=oc4: e.dma_start(
                out=yout[tt * 128:(tt + 1) * 128, oc4 * 512:(oc4 + 1) * 512],
                in_=ysb[:, oc4 * 512:(oc4 + 1) * 512]))
            vals[f"ydma{i}"] = p.val(f"sgy{i % 4}")

    prev_se_r1d = 0
    vals["trD-1"] = 0
    for t in range(8):
        p.wait("vector", "ve", vals[f"resid{t}_3"])
        if t >= 1:
            p.wait("vector", "se", prev_se_r1d)
        p.wait("vector", "pe", vals[f"trD{t-1}"])
        p.wait("scalar", "ve", vals[f"resid{t}_3"])
        if t >= 1:
            p.wait("scalar", "ve", vals[f"ln2ve{t-1}"])
        p.inc("scalar", "se", 1, lambda e, t=t: e.activation(
            sqtsd[:, :], x1[:, t * D:(t + 1) * D], AF.Square))
        se_sq = p.val("se")
        p.wait("vector", "se", se_sq)
        p.inc("vector", "ve", 1, lambda e, t=t: e.reduce_sum(
            mv[:, 0:1], x1[:, t * D:(t + 1) * D], axis=AX.X))
        p.inc("vector", "ve", 1, lambda e: e.reduce_sum(
            mv[:, 1:2], sqtsd[:, :], axis=AX.X))
        vals[f"ln2ve{t}"] = p.val("ve")
        p.inc("vector", "ve", 1, lambda e: e.tensor_scalar(
            stats[:, 0:1], mv[:, 0:1], 1.0 / D, None, ALU.mult))
        p.inc("vector", "ve", 1, lambda e: e.scalar_tensor_tensor(
            stats[:, 1:2], mv[:, 0:1], -1.0 / (D * D), mv[:, 0:1],
            ALU.mult, ALU.mult))
        ve_stats = p.val("ve")
        p.wait("scalar", "ve", ve_stats)
        p.inc("scalar", "se", 1, lambda e: e.activation(
            stats[:, 3:4], mv[:, 1:2], AF.Identity, bias=stats[:, 1:2],
            scale=1.0 / D))
        p.inc("scalar", "se", 1, lambda e: e.activation(
            r1a[:, :], stats[:, 3:4], AF.Ln, bias=epsv[:, :], scale=1.0))
        p.inc("scalar", "se", 1, lambda e: e.activation(
            r1[:, :], r1a[:, :], AF.Exp, bias=0.0, scale=-0.5))
        prev_se_r1d = p.val("se")
        p.wait("vector", "se", prev_se_r1d)
        p.inc("vector", "ve", 1, lambda e: e.scalar_tensor_tensor(
            b2[:, :], stats[:, 0:1], -1.0, r1[:, :], ALU.mult, ALU.mult))
        ve_b2 = p.val("ve")
        p.wait("scalar", "ve", ve_b2)
        p.inc("scalar", "se", 1, lambda e, t=t: e.activation(
            xh2[:, :], x1[:, t * D:(t + 1) * D], AF.Identity,
            bias=b2[:, :], scale=r1[:, :]))
        se_xh2 = p.val("se")
        p.wait("vector", "se", se_xh2)
        if t == 0:
            p.wait("vector", "se", se_bcast)
        p.inc("vector", "ve", 1, lambda e: e.tensor_scalar(
            tq[:, :], xh2[:, :], rinv_bc[:, 0:1], None, ALU.mult))
        p.inc("vector", "ve", 1, lambda e: e.tensor_scalar(
            q8[:, :], tq[:, :], 448.0, -448.0, ALU.min, ALU.max))
        ve_q8 = p.val("ve")
        p.wait("tensor", "ve", ve_q8)
        if t == 0:
            p.wait("tensor", "wfl", wf_done)
        p.wait("tensor", "se", vals.get(f"evP2_{t-1}", 0))
        for jj in range(8):
            p.inc("tensor", "pe", 1,
                  lambda e, jj=jj: e.matmul(
                      P2[:, jj * 128:(jj + 1) * 128],
                      q8[:, jj * 128:(jj + 1) * 128], id_sb[:, :],
                      start=True, stop=True))
        pe_tr0 = p.val("pe")
        p.wait("tensor", "se", vals.get(f"evL_{t-1}", 0))
        for jj in range(8):
            j = 8 + jj
            lt = L0 if jj < 4 else L1
            p.inc("tensor", "pe", 1,
                  lambda e, j=j, jj=jj, lt=lt: e.matmul(
                      lt[:, (jj % 4) * 128:(jj % 4 + 1) * 128],
                      q8[:, j * 128:(j + 1) * 128], id_sb[:, :],
                      start=True, stop=True))
        pe_tr1 = p.val("pe")
        p.wait("scalar", "pe", pe_tr0)
        p.inc("scalar", "se", 1,
              lambda e, t=t: e.copy(
                  xq_fm[:, :].rearrange("p (j c) -> p j c", j=16)
                  [:, 0:8, t * 128:(t + 1) * 128],
                  P2[:, :].rearrange("p (j c) -> p j c", j=8)))
        vals[f"evP2_{t}"] = p.val("se")
        p.wait("scalar", "pe", pe_tr1)
        p.inc("scalar", "se", 1,
              lambda e, t=t: e.copy(
                  xq_fm[:, :].rearrange("p (j c) -> p j c", j=16)
                  [:, 8:12, t * 128:(t + 1) * 128],
                  L0[:, :].rearrange("p (j c) -> p j c", j=4)))
        p.inc("scalar", "se", 1,
              lambda e, t=t: e.copy(
                  xq_fm[:, :].rearrange("p (j c) -> p j c", j=16)
                  [:, 12:16, t * 128:(t + 1) * 128],
                  L1[:, :].rearrange("p (j c) -> p j c", j=4)))
        vals[f"evL_{t}"] = p.val("se")
        vals[f"q8done{t}"] = p.val("se")
        vals[f"trD{t}"] = p.val("pe")
        if t >= 2:
            emit_ffn_evac(t - 2)
        if t >= 1:
            emit_ffn_pe(t - 1)
    emit_ffn_evac(6)
    emit_ffn_pe(7)
    emit_ffn_evac(7)

    for k_ in range(4):
        p.wait("scalar", f"sgy{k_}")
    p.wait("scalar", "st")

    with nc.Block() as block:
        @block.sync
        def _(e):
            for f in p.ops["sync"]:
                f(e)

        @block.gpsimd
        def _(e):
            for f in p.ops["gpsimd"]:
                f(e)

        @block.vector
        def _(e):
            for f in p.ops["vector"]:
                f(e)

        @block.scalar
        def _(e):
            for f in p.ops["scalar"]:
                f(e)

        @block.tensor
        def _(e):
            for f in p.ops["tensor"]:
                f(e)

    return nc


def emit_ov(p, pend, P2, L0, L1, vbh, onc, pt_sb, exp_hist, ov_hist):
    kt, w0, si, g, is_start, is_stop, ve_need = pend
    ptb = pt_sb[si]
    p.wait("tensor", "se", exp_hist[g])
    if ve_need:
        p.wait("tensor", "ve", ve_need)
    for a in ((w0, 512), (512, 1024)):
        a0, b0 = max(a[0], w0), a[1]
        if b0 <= w0 or b0 <= a0:
            continue
        p.inc("tensor", "pe", 1,
              lambda e, kt=kt, a0=a0, b0=b0, ptb=ptb, st_=is_start,
              sp_=is_stop: e.matmul(
                  P2[:, a0:b0], vbh[:, kt * 128:(kt + 1) * 128],
                  ptb[:, a0:b0], start=st_, stop=sp_))
    ov_hist[g] = p.val("pe")


_NC_CACHE = []


def kernel(L0_attn_X, L0_attn_Wq, L0_attn_bq, L0_attn_Wk, L0_attn_bk,
           L0_attn_Wv, L0_attn_bv, L0_attn_Wo, L0_attn_bo, L0_ffn_weight,
           L0_ffn_weight_scale, L0_ffn_input_scale):
    X = np.ascontiguousarray(np.asarray(L0_attn_X, dtype=np.float32)
                             .reshape(B * S, D))
    wqTh = np.ascontiguousarray(np.asarray(L0_attn_Wq, dtype=np.float32).T)
    wkTh = np.ascontiguousarray(np.asarray(L0_attn_Wk, dtype=np.float32).T)
    wvTh = np.ascontiguousarray(np.asarray(L0_attn_Wv, dtype=np.float32).T)
    woTh = np.ascontiguousarray(np.asarray(L0_attn_Wo, dtype=np.float32).T)
    wf = np.asarray(L0_ffn_weight)
    wfTh = np.ascontiguousarray(wf.view(ml_dtypes.float8_e4m3).T)
    bqh = np.ascontiguousarray(
        np.asarray(L0_attn_bq, np.float32).reshape(16, 128).T)
    bkh = np.ascontiguousarray(
        np.asarray(L0_attn_bk, np.float32).reshape(16, 128).T)
    bvh = np.ascontiguousarray(np.asarray(L0_attn_bv, np.float32).reshape(1, D))
    boh = np.ascontiguousarray(np.asarray(L0_attn_bo, np.float32).reshape(1, D))
    wsch = np.asarray(L0_ffn_weight_scale, np.float32).reshape(1, 1)
    isch = np.asarray(L0_ffn_input_scale).reshape(1, 1)
    if isch.dtype != ml_dtypes.bfloat16:
        isch = isch.view(ml_dtypes.bfloat16) if isch.dtype.itemsize == 2 \
            else isch.astype(ml_dtypes.bfloat16)

    ident = np.eye(128, dtype=ml_dtypes.bfloat16)
    onc_h = np.ones((128, 1), dtype=ml_dtypes.bfloat16)
    onr_h = np.ones((1, 128), dtype=ml_dtypes.bfloat16)
    onrf_h = np.ones((1, 128), dtype=np.float32)
    kk, qq = np.meshgrid(np.arange(128), np.arange(128), indexing="ij")
    maskd_h = np.where(kk <= qq, 0.0, NEG).astype(np.float32)
    mexp_h = np.exp(maskd_h).astype(ml_dtypes.bfloat16)

    in_maps = []
    for c in range(NC):
        idxt_h = np.zeros((128, 64), dtype=np.int32)
        pp = np.arange(128, dtype=np.int32)
        for u in range(4):
            g = 4 * c + u
            h = g % 16
            for s_ in range(4):
                for r_ in range(3):
                    idxt_h[:, u * 12 + s_ * 3 + r_] = \
                        (4 * (c // 4) + s_) * 2048 + h * 128 + pp
        for h in range(16):
            orow = 1024 * (h % 4) + (4 * (c // 4) + h // 4) * 128 + pp
            idxt_h[:, 48 + h] = orow * 4 + (c % 4)
        in_maps.append({
            "xin": X[c * T:(c + 1) * T], "wqT": wqTh, "wkT": wkTh,
            "wvT": wvTh, "woT": woTh, "wfT": wfTh, "bqi": bqh, "bki": bkh,
            "bvi": bvh, "boi": boh, "wsc": wsch, "isc": isch,
            "id_in": ident, "onc_in": onc_h, "onr_in": onr_h, "onrf_in": onrf_h,
            "mask_in": maskd_h, "mexp_in": mexp_h, "idxt": idxt_h,
        })

    if not _NC_CACHE:
        nc = bass.Bass("TRN2", target_bir_lowering=False)
        build(nc)
        _NC_CACHE.append(nc)
    nc = _NC_CACHE[0]
    res = run_bass_kernel_spmd(nc, in_maps, core_ids=list(range(NC)))
    out = np.concatenate([res.results[c]["yout"] for c in range(NC)], axis=0)
    return out.reshape(B, S, D)


# version 25
# speedup vs baseline: 1.1971x; 1.0208x over previous
import numpy as np
import ml_dtypes
import concourse.bass as bass
import concourse.mybir as mybir
from concourse.bass import IndirectOffsetOnAxis
from concourse.bass_utils import run_bass_kernel_spmd

F32 = mybir.dt.float32
BF16 = mybir.dt.bfloat16
FP8 = mybir.dt.float8e4
U32 = mybir.dt.uint32
I32 = mybir.dt.int32
ALU = mybir.AluOpType
AF = mybir.ActivationFunctionType
AX = mybir.AxisListType

D = 2048
S = 4096
B = 2
T = 1024
NC = 8
SC = float(128.0 ** -0.25)
EPS = 1e-5
NEG = -30.0
C24 = float(np.exp(24.0))
GRPS = [list(range(8))]
SH = 6144


class Prog:

    def __init__(self, nc):
        self.nc = nc
        self.ops = {"sync": [], "gpsimd": [], "vector": [], "scalar": [],
                    "tensor": []}
        self.sem = {}
        self.cnt = {}

    def mksem(self, name):
        self.sem[name] = self.nc.alloc_semaphore(name)
        self.cnt[name] = 0

    def emit(self, eng, fn):
        self.ops[eng].append(fn)

    def inc(self, eng, sem, n, fn):
        self.cnt[sem] += n
        h = self.sem[sem]
        self.ops[eng].append(lambda e, fn=fn, h=h, n=n: fn(e).then_inc(h, n))

    def dma(self, eng, sem, fn):
        self.inc(eng, sem, 16, fn)

    def wait(self, eng, sem, upto=None):
        v = self.cnt[sem] if upto is None else upto
        if v <= 0:
            return
        h = self.sem[sem]
        self.ops[eng].append(lambda e, h=h, v=v: e.wait_ge(h, v))

    def val(self, sem):
        return self.cnt[sem]

    def fence(self, name, fn):
        self.mksem(name)
        for _ in range(8):
            self.dma("gpsimd", name, fn)


def build(nc):
    xin = nc.dram_tensor("xin", [T, D], F32, kind="ExternalInput").ap()
    wqT = nc.dram_tensor("wqT", [D, D], F32, kind="ExternalInput").ap()
    wkT = nc.dram_tensor("wkT", [D, D], F32, kind="ExternalInput").ap()
    wvT = nc.dram_tensor("wvT", [D, D], F32, kind="ExternalInput").ap()
    woT = nc.dram_tensor("woT", [D, D], F32, kind="ExternalInput").ap()
    wfT = nc.dram_tensor("wfT", [D, D], FP8, kind="ExternalInput").ap()
    bqi = nc.dram_tensor("bqi", [128, 16], F32, kind="ExternalInput").ap()
    bki = nc.dram_tensor("bki", [128, 16], F32, kind="ExternalInput").ap()
    bvi = nc.dram_tensor("bvi", [1, D], F32, kind="ExternalInput").ap()
    boi = nc.dram_tensor("boi", [1, D], F32, kind="ExternalInput").ap()
    wsc = nc.dram_tensor("wsc", [1, 1], F32, kind="ExternalInput").ap()
    isc = nc.dram_tensor("isc", [1, 1], BF16, kind="ExternalInput").ap()
    id_in = nc.dram_tensor("id_in", [128, 128], BF16, kind="ExternalInput").ap()
    onc_in = nc.dram_tensor("onc_in", [128, 1], BF16, kind="ExternalInput").ap()
    onr_in = nc.dram_tensor("onr_in", [1, 128], BF16, kind="ExternalInput").ap()
    onrf_in = nc.dram_tensor("onrf_in", [1, 128], F32, kind="ExternalInput").ap()
    mask_in = nc.dram_tensor("mask_in", [128, 128], F32, kind="ExternalInput").ap()
    mexp_in = nc.dram_tensor("mexp_in", [128, 128], BF16, kind="ExternalInput").ap()
    idxt = nc.dram_tensor("idxt", [128, 64], I32, kind="ExternalInput").ap()
    yout = nc.dram_tensor("yout", [T, D], F32, kind="ExternalOutput").ap()

    cc1i = nc.dram_tensor("cc1i", [SH, 1024], BF16, kind="Internal").ap()
    cc1oq = nc.dram_tensor("cc1oq", [16384, 1024], BF16, kind="Internal",
                           addr_space="Shared").ap()
    cc1ok = nc.dram_tensor("cc1ok", [16384, 1024], BF16, kind="Internal",
                           addr_space="Shared").ap()
    cc1ov = nc.dram_tensor("cc1ov", [16384, 1024], BF16, kind="Internal",
                           addr_space="Shared").ap()
    cc2i = nc.dram_tensor("cc2i", [512, 4096], BF16, kind="Internal").ap()
    cc2o = nc.dram_tensor("cc2o", [4096, 4096], BF16, kind="Internal",
                          addr_space="Shared").ap()

    A = nc.alloc_sbuf_tensor
    idxt_sb = A("idxt_sb", [128, 64], I32)
    id_sb = A("id_sb", [128, 128], BF16)
    onc = A("onc", [128, 1], BF16)
    onr = A("onr", [1, 128], BF16)
    onrf = A("onrf", [1, 128], F32)
    maskd = A("maskd", [128, 128], F32)
    mexpd = A("mexpd", [128, 128], BF16)
    bqs = A("bqs", [128, 16], F32)
    bks = A("bks", [128, 16], F32)
    bvr = A("bvr", [1, D], BF16)
    bor = A("bor", [1, D], BF16)
    scl = A("scl", [1, 8], F32)
    iscb = A("iscb", [1, 1], BF16)
    rinv_bc = A("rinv_bc", [128, 1], F32)
    wssc_bc = A("wssc_bc", [128, 1], F32)
    nb_sb = A("nb_sb", [128, 1], F32)
    mv = A("mv", [128, 2], F32)
    stats = A("stats", [128, 24], F32)
    r1 = A("r1", [128, 1], F32)
    b2 = A("b2", [128, 1], F32)
    smax = A("smax", [1, 8], F32)
    cmx = A("cmx", [1, 16], F32)
    epsv = A("epsv", [128, 1], F32)
    oncf = A("oncf", [128, 1], F32)
    lsb = A("lsb", [1, 1024], F32)
    lnc = A("lnc", [1, 1], F32)
    lsb2 = A("lsb2", [1, 1024], F32)
    r1a = A("r1a", [128, 1], F32)
    glt = A("glt", [128, 16], BF16)

    xts = [A(f"xts{i}", [128, D], F32) for i in range(2)]
    xh_ts = [A(f"xh{i}", [128, D], BF16) for i in range(2)]
    stg = [A(f"stg{i}", [128, 512], BF16) for i in range(4)]

    xhat_fm = A("xhat_fm", [128, 16 * 1024], BF16)
    a_xf = nc.lookup_mloc(xhat_fm).addr
    oall = nc.alloc_sbuf_tensor_at("oall", [128, 16 * 1024], BF16, offset=a_xf)

    wb = A("wb", [128, 16 * D], BF16)
    a_wb = nc.lookup_mloc(wb).addr
    wf_sb = nc.alloc_sbuf_tensor_at("wf_sb", [128, 16 * D], FP8, offset=a_wb)
    xq_fm = nc.alloc_sbuf_tensor_at("xq_fm", [128, 16 * 1024], FP8,
                                    offset=a_wb + 32768)

    x1 = A("x1", [128, 8 * D], F32)
    a_x1 = nc.lookup_mloc(x1).addr
    qkv = nc.alloc_sbuf_tensor_at("qkv", [128, 3 * 4096], BF16, offset=a_x1)
    qbh = qkv[:, 0:4096]
    kbh = qkv[:, 4096:8192]
    vbh = qkv[:, 8192:12288]
    pt_sb = [nc.alloc_sbuf_tensor_at(f"pt{i}", [128, 1024], BF16,
                                     offset=a_x1 + 24576 + i * 2048)
             for i in range(2)]
    linv_bc = nc.alloc_sbuf_tensor_at("linv_bc", [128, 1024], F32,
                                      offset=a_x1 + 28672)
    osb = nc.alloc_sbuf_tensor_at("osb", [128, 1024], BF16,
                                  offset=a_x1 + 32768)
    qsq = nc.alloc_sbuf_tensor_at("qsq", [128, 512], BF16,
                                  offset=a_x1 + 34816)
    a_xts0 = nc.lookup_mloc(xts[0]).addr
    a_xts1 = nc.lookup_mloc(xts[1]).addr
    tq = nc.alloc_sbuf_tensor_at("tq", [128, D], F32, offset=a_xts0)
    xh2 = nc.alloc_sbuf_tensor_at("xh2", [128, D], F32, offset=a_xts1)
    a_xh0 = nc.lookup_mloc(xh_ts[0]).addr
    q8 = nc.alloc_sbuf_tensor_at("q8", [128, D], FP8, offset=a_xh0)
    a_stg0 = nc.lookup_mloc(stg[0]).addr
    assert nc.lookup_mloc(stg[3]).addr == a_stg0 + 3 * 1024
    sqts = nc.alloc_sbuf_tensor_at("sqts", [128, D], BF16, offset=a_stg0)
    a_xh1 = nc.lookup_mloc(xh_ts[1]).addr
    sqtsd = nc.alloc_sbuf_tensor_at("sqtsd", [128, D], BF16, offset=a_xh1)
    wb2 = nc.alloc_sbuf_tensor_at("wb2", [128, 16 * D], BF16, offset=a_x1)
    qkv2 = nc.alloc_sbuf_tensor_at("qkv2", [128, 3 * 4096], BF16,
                                   offset=a_x1 + 36864)
    qkvs = [qkv, qkv2]
    ptsum = nc.alloc_sbuf_tensor_at("ptsum", [128, 1024], F32,
                                    offset=a_x1 + 61440)
    ptsum2 = nc.alloc_sbuf_tensor_at("ptsum2", [128, 1024], F32,
                                     offset=a_xh1)
    ysb = nc.alloc_sbuf_tensor_at("ysb", [128, D], F32, offset=a_xf)
    ptsums = [ptsum, ptsum2]
    praw = nc.alloc_sbuf_tensor_at("praw", [128, 1024], BF16, offset=a_stg0)
    pt_sb3 = nc.alloc_sbuf_tensor_at("pt3", [128, 1024], BF16,
                                     offset=a_stg0 + 2048)

    P0 = nc.alloc_psum_tensor("P0", [128, 1024], F32)
    P1 = nc.alloc_psum_tensor("P1", [128, 1024], F32)
    P2 = nc.alloc_psum_tensor("P2", [128, 1024], F32)
    L0 = nc.alloc_psum_tensor("L0", [128, 512], F32)
    L1 = nc.alloc_psum_tensor("L1", [128, 512], F32)
    banks4 = [P0[:, 0:512], P0[:, 512:1024], P1[:, 0:512], P1[:, 512:1024]]

    p = Prog(nc)
    sems = ["ld", "st", "wt", "pe", "ve", "se", "cc", "gp", "gl", "junk",
            "ob0", "ob1", "ob2", "ob3", "xr0", "xr1", "wfl"]
    for w_ in ("q", "k", "v", "y"):
        for k_ in range(4):
            sems.append(f"sg{w_}{k_}")
    for s_ in sems:
        p.mksem(s_)
    vals = {}

    for out_, in_ in ((idxt_sb, idxt), (id_sb, id_in), (onc, onc_in),
                      (onr, onr_in), (onrf, onrf_in), (maskd, mask_in),
                      (mexpd, mexp_in), (bqs, bqi),
                      (bks, bki)):
        p.dma("sync", "ld", lambda e, o=out_, i=in_: e.dma_start(
            out=o[:, :], in_=i[:, :]))
    p.dma("sync", "ld", lambda e: e.dma_start(out=scl[0:1, 0:1], in_=wsc[:, :]))
    start_ld = p.val("ld")
    p.dma("gpsimd", "junk", lambda e: e.dma_start(out=bvr[:, :], in_=bvi[:, :]))
    p.dma("gpsimd", "junk", lambda e: e.dma_start(out=bor[:, :], in_=boi[:, :]))
    p.dma("gpsimd", "junk", lambda e: e.dma_start(out=iscb[:, :], in_=isc[:, :]))
    p.fence("f_sm", lambda e: e.dma_start(out=glt[:, :], in_=xin[0:128, 0:16]))
    for j in range(16):
        p.dma("gpsimd", "junk", lambda e, j=j: e.dma_start(
            out=wb[:, j * D:(j + 1) * D], in_=wqT[j * 128:(j + 1) * 128, :]))
    p.fence("f_wq", lambda e: e.dma_start(out=glt[:, :], in_=xin[0:128, 0:16]))

    p.wait("vector", "ld", start_ld)
    p.inc("vector", "ve", 1, lambda e: e.memset(epsv[:, :], EPS))
    p.inc("vector", "ve", 1, lambda e: e.memset(lnc[:, :], 24.0))
    p.inc("vector", "ve", 1, lambda e: e.memset(oncf[:, :], 1.0))
    p.inc("vector", "ve", 1, lambda e: e.tensor_scalar(
        bqs[:, :], bqs[:, :], SC, None, ALU.mult))
    p.inc("vector", "ve", 1, lambda e: e.tensor_scalar(
        bks[:, :], bks[:, :], SC, None, ALU.mult))
    p.wait("vector", "f_sm", 128)
    p.inc("vector", "ve", 1, lambda e: e.tensor_copy(scl[0:1, 3:4], iscb[:, :]))
    ve_scl = p.val("ve")
    p.wait("scalar", "ve", ve_scl)
    p.inc("scalar", "se", 1, lambda e: e.activation(
        scl[0:1, 4:5], scl[0:1, 3:4], AF.Ln))
    p.inc("scalar", "se", 1, lambda e: e.activation(
        scl[0:1, 1:2], scl[0:1, 4:5], AF.Exp, scale=-1.0))
    se_scl = p.val("se")
    p.wait("tensor", "ve", ve_scl)
    p.wait("tensor", "se", se_scl)
    p.inc("tensor", "pe", 1, lambda e: e.matmul(
        P2[:, 0:1], onrf[0:1, :], scl[0:1, 1:2], start=True, stop=True))
    p.inc("tensor", "pe", 1, lambda e: e.matmul(
        P2[:, 1:2], onrf[0:1, :], scl[0:1, 0:1], start=True, stop=True))
    p.inc("tensor", "pe", 1, lambda e: e.matmul(
        P2[:, 2:3], onrf[0:1, :], scl[0:1, 3:4], start=True, stop=True))
    pe_bc0 = p.val("pe")
    p.wait("scalar", "pe", pe_bc0)
    p.inc("scalar", "se", 1, lambda e: e.copy(rinv_bc[:, :], P2[:, 0:1]))
    p.inc("scalar", "se", 1, lambda e: e.copy(nb_sb[:, :], P2[:, 1:2]))
    se_bcast = p.val("se")
    p.wait("vector", "pe", pe_bc0)
    p.wait("vector", "se", se_bcast)
    p.inc("vector", "ve", 1, lambda e: e.tensor_tensor(
        wssc_bc[:, :], nb_sb[:, :], P2[:, 2:3], ALU.mult))
    ve_bcast = p.val("ve")

    def ln_tiles(src_getter, dst_q8=False):
        pass

    prev_se_r1 = 0
    prev_lnapply = {}
    prev_trev = {-1: 0}
    for t in range(8):
        xb, hb = xts[t % 2], xh_ts[t % 2]
        if t >= 2:
            p.wait("sync", "se", prev_lnapply[t - 2])
        p.dma("sync", "ld", lambda e, t=t, xb=xb: e.dma_start(
            out=xb[:, :], in_=xin[t * 128:(t + 1) * 128, :]))
        ld_here = p.val("ld")
        p.wait("scalar", "ld", ld_here)
        if t >= 1:
            p.wait("scalar", "ve", vals.get("ln1ve", 0))
        p.inc("scalar", "se", 1, lambda e, xb=xb: e.activation(
            sqts[:, :], xb[:, :], AF.Square))
        se_sq = p.val("se")
        p.wait("vector", "ld", ld_here)
        p.wait("vector", "se", se_sq)
        if t >= 1:
            p.wait("vector", "se", prev_se_r1)
        p.inc("vector", "ve", 1, lambda e, xb=xb: e.reduce_sum(
            mv[:, 0:1], xb[:, :], axis=AX.X))
        p.inc("vector", "ve", 1, lambda e: e.reduce_sum(
            mv[:, 1:2], sqts[:, :], axis=AX.X))
        p.inc("vector", "ve", 1, lambda e: e.tensor_scalar(
            stats[:, 0:1], mv[:, 0:1], 1.0 / D, None, ALU.mult))
        p.inc("vector", "ve", 1, lambda e: e.scalar_tensor_tensor(
            stats[:, 1:2], mv[:, 0:1], -1.0 / (D * D), mv[:, 0:1],
            ALU.mult, ALU.mult))
        ve_stats = p.val("ve")
        vals["ln1ve"] = ve_stats
        p.wait("scalar", "ve", ve_stats)
        p.inc("scalar", "se", 1, lambda e: e.activation(
            stats[:, 3:4], mv[:, 1:2], AF.Identity, bias=stats[:, 1:2],
            scale=1.0 / D))
        p.inc("scalar", "se", 1, lambda e: e.activation(
            r1a[:, :], stats[:, 3:4], AF.Ln, bias=epsv[:, :], scale=1.0))
        p.inc("scalar", "se", 1, lambda e: e.activation(
            r1[:, :], r1a[:, :], AF.Exp, bias=0.0, scale=-0.5))
        prev_se_r1 = p.val("se")
        p.wait("vector", "se", prev_se_r1)
        p.inc("vector", "ve", 1, lambda e: e.scalar_tensor_tensor(
            b2[:, :], stats[:, 0:1], -1.0, r1[:, :], ALU.mult, ALU.mult))
        ve_b2 = p.val("ve")
        p.wait("scalar", "ve", ve_b2)
        if t >= 2:
            p.wait("scalar", "pe", vals[f"tr{t-2}"])
        p.inc("scalar", "se", 1, lambda e, xb=xb, hb=hb: e.activation(
            hb[:, :], xb[:, :], AF.Identity, bias=b2[:, :], scale=r1[:, :]))
        prev_lnapply[t] = p.val("se")
        for rnd in range(2):
            ps = P2 if rnd == 0 else P1
            if t == 0 and rnd == 0:
                p.wait("tensor", "se", se_bcast)
                p.wait("tensor", "ve", ve_bcast)
            p.wait("tensor", "se", prev_lnapply[t])
            p.wait("tensor", "ve", prev_trev.get(t - 1, prev_trev[-1])
                   if rnd == 0 else vals.get(f"trevP1_{t-1}", 0))
            for jj in range(8):
                j = rnd * 8 + jj
                p.inc("tensor", "pe", 1,
                      lambda e, hb=hb, j=j, jj=jj, ps=ps: e.matmul(
                          ps[:, jj * 128:(jj + 1) * 128],
                          hb[:, j * 128:(j + 1) * 128], id_sb[:, :],
                          start=True, stop=True))
            pe_tr = p.val("pe")
            p.wait("vector", "pe", pe_tr)
            p.inc("vector", "ve", 1,
                  lambda e, t=t, rnd=rnd, ps=ps: e.tensor_copy(
                      xhat_fm[:, :].rearrange("p (j c) -> p j c", j=16)
                      [:, rnd * 8:(rnd + 1) * 8, t * 128:(t + 1) * 128],
                      ps[:, :].rearrange("p (j c) -> p j c", j=8)))
            if rnd == 0:
                prev_trev[t] = p.val("ve")
            else:
                vals[f"trevP1_{t}"] = p.val("ve")
        vals[f"tr{t}"] = p.val("pe")
        if t == 3:
            vals["xhat_half0"] = p.val("ve")
    xhat_ve = p.val("ve")

    def proj_qk(which, wsem, bias_sb, base, wbuf, prevsec=None):
        for i, (tb, oc) in enumerate([(t2, o) for t2 in range(2)
                                      for o in range(16)]):
            bank, sb = banks4[i % 4], stg[i % 4]
            sem = f"sg{which}{i % 4}"
            p.wait("tensor", wsem, 128)
            if i == 0:
                p.wait("tensor", "ve", vals["xhat_half0"])
            elif i == 16:
                p.wait("tensor", "ve", xhat_ve)
            if i >= 4:
                p.wait("tensor", sem, vals[f"{which}dma{i-4}"])
            elif prevsec:
                p.wait("tensor", f"sg{prevsec}{i % 4}",
                       vals[f"{prevsec}sgfull{i % 4}"])
            for j in range(16):
                p.inc("tensor", "pe", 1,
                      lambda e, j=j, oc=oc, tb=tb, bank=bank, wbuf=wbuf:
                      e.matmul(
                          bank, wbuf[:, j * D + oc * 128: j * D + (oc + 1) * 128],
                          xhat_fm[:, j * 1024 + tb * 512:
                                  j * 1024 + (tb + 1) * 512],
                          start=(j == 0), stop=(j == 15)))
            pe_here = p.val("pe")
            p.wait("scalar", "pe", pe_here)
            p.inc("scalar", "se", 1,
                  lambda e, bank=bank, sb=sb, oc=oc, bias_sb=bias_sb:
                  e.activation(sb[:, :], bank, AF.Identity,
                               bias=bias_sb[:, oc:oc + 1], scale=SC))
            p.dma("scalar", sem,
                  lambda e, sb=sb, oc=oc, tb=tb, base=base: e.dma_start(
                      out=cc1i[oc * 128 + base: oc * 128 + base + 128,
                               tb * 512:(tb + 1) * 512], in_=sb[:, :]))
            vals[f"{which}dma{i}"] = p.val(sem)
        for k_ in range(4):
            vals[f"{which}sgfull{k_}"] = p.val(f"sg{which}{k_}")

    for j in range(16):
        p.dma("gpsimd", "junk", lambda e, j=j: e.dma_start(
            out=wb2[:, j * D:(j + 1) * D], in_=wkT[j * 128:(j + 1) * 128, :]))
    p.fence("f_wk", lambda e: e.dma_start(out=glt[:, :], in_=xin[0:128, 0:16]))
    proj_qk("q", "f_wq", bqs, 0, wb)
    q_mms = p.val("pe")
    for k_ in range(4):
        p.wait("gpsimd", f"sgq{k_}", vals[f"qsgfull{k_}"])
    p.inc("gpsimd", "cc", 1, lambda e: e.collective_compute(
        "AllGather", ALU.bypass, ins=[cc1i[0:2048, :]], outs=[cc1oq[:, :]],
        replica_groups=GRPS))
    p.wait("gpsimd", "pe", q_mms)
    for j in range(16):
        p.dma("gpsimd", "junk", lambda e, j=j: e.dma_start(
            out=wb[:, j * D:(j + 1) * D], in_=wvT[j * 128:(j + 1) * 128, :]))
    p.fence("f_wv", lambda e: e.dma_start(out=glt[:, :], in_=xin[0:128, 0:16]))
    proj_qk("k", "f_wk", bks, 2048, wb2, prevsec="q")
    for k_ in range(4):
        p.wait("gpsimd", f"sgk{k_}", vals[f"ksgfull{k_}"])
    p.inc("gpsimd", "cc", 1, lambda e: e.collective_compute(
        "AllGather", ALU.bypass, ins=[cc1i[2048:4096, :]], outs=[cc1ok[:, :]],
        replica_groups=GRPS))

    for i, (tt, oc4) in enumerate([(t2, o) for t2 in range(8)
                                   for o in range(4)]):
        bank, sb = banks4[i % 4], stg[i % 4]
        sem = f"sgv{i % 4}"
        p.wait("tensor", "f_wv", 128)
        if i >= 4:
            p.wait("tensor", sem, vals[f"vdma{i-4}"])
        else:
            p.wait("tensor", f"sgk{i % 4}", vals[f"ksgfull{i % 4}"])
        for j in range(16):
            p.inc("tensor", "pe", 1,
                  lambda e, j=j, tt=tt, oc4=oc4, bank=bank: e.matmul(
                      bank,
                      xhat_fm[:, j * 1024 + tt * 128: j * 1024 + (tt + 1) * 128],
                      wb[:, j * D + oc4 * 512: j * D + (oc4 + 1) * 512],
                      start=(j == 0), stop=False))
        p.inc("tensor", "pe", 1, lambda e, oc4=oc4, bank=bank: e.matmul(
            bank, onr[:, :], bvr[0:1, oc4 * 512:(oc4 + 1) * 512],
            start=False, stop=True))
        pe_here = p.val("pe")
        p.wait("scalar", "pe", pe_here)
        p.inc("scalar", "se", 1, lambda e, bank=bank, sb=sb: e.activation(
            sb[:, :], bank, AF.Identity))
        p.dma("scalar", sem, lambda e, sb=sb, tt=tt, oc4=oc4: e.dma_start(
            out=cc1i[4096:6144, tt * 128:(tt + 1) * 128]
            .rearrange("(i q) c -> q i c", q=128)
            [:, 4 * oc4:4 * oc4 + 4, :],
            in_=sb[:, :].rearrange("p (i c) -> p i c", i=4)))
        vals[f"vdma{i}"] = p.val(sem)

    for k_ in range(4):
        p.wait("gpsimd", f"sgv{k_}", p.val(f"sgv{k_}"))
    p.inc("gpsimd", "cc", 1, lambda e: e.collective_compute(
        "AllGather", ALU.bypass, ins=[cc1i[4096:6144, :]], outs=[cc1ov[:, :]],
        replica_groups=GRPS))

    p.wait("gpsimd", "ld", start_ld)
    pt_sb.append(pt_sb3)
    gstep = 0
    exp_hist = {}
    ov_hist = {}
    pts_hist = {}
    secs = (cc1oq, cc1ok, cc1ov)

    def emit_fin_l(fin):
        p.wait("tensor", "ve", pts_hist[fin["gend"] - 1])
        p.wait("tensor", "se", vals.get("se_lbank_free", 0))
        pts = fin["pts"]
        p.inc("tensor", "pe", 1, lambda e, pts=pts: e.matmul(
            L0[0:1, :], oncf[:, :], pts[:, 0:512], start=True, stop=True))
        p.inc("tensor", "pe", 1, lambda e, pts=pts: e.matmul(
            L1[0:1, :], oncf[:, :], pts[:, 512:1024], start=True, stop=True))
        vals[f"ptsum_free{fin['qc'] % 2}"] = p.val("pe")
        pe_qc = p.val("pe")
        p.wait("scalar", "pe", pe_qc)
        p.inc("scalar", "se", 1, lambda e: e.activation(
            lsb[0:1, 0:512], L0[0:1, :], AF.Ln, scale=C24))
        p.inc("scalar", "se", 1, lambda e: e.activation(
            lsb[0:1, 512:1024], L1[0:1, :], AF.Ln, scale=C24))
        p.inc("scalar", "se", 1, lambda e: e.activation(
            lsb2[0:1, :], lsb[0:1, :], AF.Exp, bias=lnc[0:1, :], scale=-1.0))
        vals["se_l_last"] = p.val("se")

    def emit_fin_osb(fin):
        u, qc = fin["u"], fin["qc"]
        p.wait("tensor", "se", vals["se_l_last"])
        p.inc("tensor", "pe", 1, lambda e: e.matmul(
            L0[:, :], onrf[0:1, :], lsb2[0:1, 0:512], start=True, stop=True))
        p.inc("tensor", "pe", 1, lambda e: e.matmul(
            L1[:, :], onrf[0:1, :], lsb2[0:1, 512:1024], start=True, stop=True))
        pe_l = p.val("pe")
        p.wait("scalar", "pe", pe_l)
        p.inc("scalar", "se", 1, lambda e: e.copy(linv_bc[:, 0:512], L0[:, :]))
        p.inc("scalar", "se", 1, lambda e: e.copy(
            linv_bc[:, 512:1024], L1[:, :]))
        se_linv = p.val("se")
        vals["se_lbank_free"] = se_linv
        p.wait("vector", "se", se_linv)
        p.wait("vector", f"ob{u}", p.val(f"ob{u}"))
        p.inc("vector", "ve", 1, lambda e: e.tensor_tensor(
            osb[:, :], praw[:, :], linv_bc[:, :], ALU.mult))
        ve_o = p.val("ve")
        p.wait("scalar", "ve", ve_o)
        p.dma("scalar", f"ob{u}", lambda e, u=u, qc=qc: e.dma_start(
            out=cc2i[u * 128:(u + 1) * 128, qc * 1024:(qc + 1) * 1024],
            in_=osb[:, :]))

    def emit_fin_praw(fin):
        p.wait("vector", "pe", ov_hist[fin["gend"] - 1])
        p.inc("vector", "ve", 1, lambda e: e.tensor_copy(
            praw[:, :], P2[:, :]))
        fin["ve_praw"] = p.val("ve")

    def emit_ag2(u):
        p.wait("gpsimd", f"ob{u}", 64)
        p.inc("gpsimd", "cc", 1, lambda e, u=u: e.collective_compute(
            "AllGather", ALU.bypass, ins=[cc2i[u * 128:(u + 1) * 128, :]],
            outs=[cc2o[u * 1024:(u + 1) * 1024, :]],
            replica_groups=GRPS))

    def emit_opiece(pu):
        p.wait("gpsimd", "cc", 4 + pu)
        for h in range(pu, 16, 4):
            p.dma("gpsimd", "junk", lambda e, h=h: e.indirect_dma_start(
                oall[:, h * 1024:(h + 1) * 1024], None,
                cc2o[:, :].rearrange("a (b c) -> (a b) c", c=1024),
                IndirectOffsetOnAxis(idxt_sb[:, 48 + h:49 + h], 0)))

    for u in range(4):
        qkvu = qkvs[u % 2]
        qb = qkvu[:, 0:4096]
        kb = qkvu[:, 4096:8192]
        vb = qkvu[:, 8192:12288]
        if u == 1:
            for j in range(16):
                p.dma("gpsimd", "junk", lambda e, j=j: e.dma_start(
                    out=wb[:, j * D:(j + 1) * D],
                    in_=woT[j * 128:(j + 1) * 128, :]))
            p.fence("f_wo", lambda e: e.dma_start(
                out=glt[:, :], in_=xin[0:128, 0:16]))
        if u >= 2:
            emit_ag2(u - 2)
            emit_opiece(u - 2)
            p.wait("gpsimd", "pe", vals[f"unitpe{u-2}"])
            p.wait("gpsimd", "se", vals[f"unitse{u-2}"])
        if u == 0:
            p.wait("gpsimd", "cc", 2)
        for s_ in range(4):
            for r_ in range(2):
                p.dma("gpsimd", "junk", lambda e, qkvu=qkvu, u=u, s_=s_, r_=r_:
                      e.indirect_dma_start(
                    qkvu[:, r_ * 4096 + s_ * 1024: r_ * 4096 + (s_ + 1) * 1024],
                    None, secs[r_][:, :],
                    IndirectOffsetOnAxis(
                        idxt_sb[:, u * 12 + s_ * 3 + r_:
                                u * 12 + s_ * 3 + r_ + 1], 0)))
        p.fence(f"f_uqk{u}", lambda e: e.dma_start(
            out=glt[:, :], in_=cc1oq[0:128, 0:16]))
        if u == 0:
            p.wait("gpsimd", "cc", 3)
        for s_ in range(4):
            p.dma("gpsimd", "junk", lambda e, qkvu=qkvu, u=u, s_=s_:
                  e.indirect_dma_start(
                qkvu[:, 8192 + s_ * 1024: 8192 + (s_ + 1) * 1024],
                None, secs[2][:, :],
                IndirectOffsetOnAxis(
                    idxt_sb[:, u * 12 + s_ * 3 + 2:
                            u * 12 + s_ * 3 + 3], 0)))
        p.fence(f"f_uv{u}", lambda e: e.dma_start(
            out=glt[:, :], in_=cc1oq[0:128, 0:16]))

        p.wait("scalar", f"f_uqk{u}", 128)
        if u > 0:
            p.wait("vector", "se", vals[f"unitse{u-1}"])
        for w, (src, slot) in enumerate(((qb, 0), (kb, 1))):
            for c8 in range(8):
                if c8 > 0 or w > 0:
                    p.wait("scalar", "pe", p.val("pe"))
                p.inc("scalar", "se", 1, lambda e, src=src, c8=c8: e.activation(
                    qsq[:, :], src[:, c8 * 512:(c8 + 1) * 512], AF.Square))
                se_h = p.val("se")
                p.wait("tensor", "se", se_h)
                p.wait("tensor", "ve", p.val("ve"))
                p.inc("tensor", "pe", 1, lambda e: e.matmul(
                    L0[0:1, :], onc[:, :], qsq[:, :], start=True, stop=True))
                pe_h = p.val("pe")
                p.wait("vector", "pe", pe_h)
                p.inc("vector", "ve", 1,
                      lambda e, w=w, c8=c8: e.reduce_max(
                          cmx[0:1, w * 8 + c8:w * 8 + c8 + 1],
                          L0[0:1, :], axis=AX.X))
        p.wait("scalar", "ve", p.val("ve"))
        p.inc("scalar", "se", 1, lambda e: e.copy(
            smax[0:1, 7:8], scl[0:1, 0:1]))
        p.wait("vector", "se", p.val("se"))
        p.inc("vector", "ve", 1, lambda e: e.reduce_max(
            smax[0:1, 0:1], cmx[0:1, 0:8], axis=AX.X))
        p.inc("vector", "ve", 1, lambda e: e.reduce_max(
            smax[0:1, 1:2], cmx[0:1, 8:16], axis=AX.X))
        p.wait("scalar", "ve", p.val("ve"))
        p.inc("scalar", "se", 1, lambda e: e.activation(
            smax[0:1, 2:3], smax[0:1, 0:1], AF.Ln))
        p.inc("scalar", "se", 1, lambda e: e.activation(
            smax[0:1, 5:6], smax[0:1, 1:2], AF.Ln))
        p.wait("vector", "se", p.val("se"))
        p.inc("vector", "ve", 1, lambda e: e.tensor_tensor(
            smax[0:1, 3:4], smax[0:1, 2:3], smax[0:1, 5:6], ALU.add))
        p.wait("scalar", "ve", p.val("ve"))
        p.inc("scalar", "se", 1, lambda e: e.activation(
            smax[0:1, 6:7], smax[0:1, 3:4], AF.Exp, scale=0.5))
        se_bnd = p.val("se")
        p.wait("vector", "se", se_bnd)
        p.inc("vector", "ve", 1, lambda e: e.tensor_scalar(
            smax[0:1, 4:5], smax[0:1, 6:7], -1.0, None, ALU.mult))
        ve_nb = p.val("ve")
        p.wait("tensor", "ve", ve_nb)
        p.inc("tensor", "pe", 1, lambda e: e.matmul(
            P2[:, 0:1], onrf[0:1, :], smax[0:1, 4:5], start=True, stop=True))
        pe_nb = p.val("pe")
        p.wait("scalar", "pe", pe_nb)
        p.inc("scalar", "se", 1, lambda e: e.copy(nb_sb[:, :], P2[:, 0:1]))
        se_nb = p.val("se")
        p.wait("tensor", f"f_uqk{u}", 128)
        fuv_waited = False

        fin = None
        for qc in range(4):
            qcb = qc * 1024
            nkt = 8 * qc + 8
            pq = []
            for kt in range(nkt):
                w0 = max(kt * 128 - qcb, 0)
                stp = P0 if gstep % 2 == 0 else P1
                ptb = pt_sb[gstep % 3]
                if fin is not None:
                    if kt == 0:
                        emit_fin_praw(fin)
                    elif kt == 1:
                        emit_fin_l(fin)
                        p.wait("tensor", "ve", fin["ve_praw"])
                    elif kt == 2:
                        emit_fin_osb(fin)
                        fin = None
                if gstep - 2 in exp_hist:
                    p.wait("tensor", "se", exp_hist[gstep - 2])
                for a in ((w0, 512), (512, 1024)):
                    a0, b0 = max(a[0], w0), a[1]
                    if b0 <= w0 or b0 <= a0:
                        continue
                    p.inc("tensor", "pe", 1,
                          lambda e, kt=kt, a0=a0, b0=b0, qcb=qcb, stp=stp,
                          kb=kb, qb=qb:
                          e.matmul(stp[:, a0:b0],
                                   kb[:, kt * 128:(kt + 1) * 128],
                                   qb[:, qcb + a0:qcb + b0],
                                   start=True, stop=True))
                qk_here = p.val("pe")
                masked = kt * 128 >= qcb
                p.wait("scalar", "pe", qk_here)
                if gstep - 3 in ov_hist:
                    p.wait("scalar", "pe", ov_hist[gstep - 3])
                if gstep - 3 in pts_hist:
                    p.wait("scalar", "ve", pts_hist[gstep - 3])
                p.inc("scalar", "se", 1,
                      lambda e, stp=stp, ptb=ptb, w0=w0: e.activation(
                          ptb[:, w0:1024], stp[:, w0:1024], AF.Exp,
                          bias=nb_sb[:, :], scale=1.0))
                exp_hist[gstep] = p.val("se")
                p.wait("vector", "se", exp_hist[gstep])
                if masked:
                    p.inc("vector", "ve", 1,
                          lambda e, ptb=ptb, w0=w0: e.tensor_tensor(
                              ptb[:, w0:w0 + 128], ptb[:, w0:w0 + 128],
                              mexpd[:, :], ALU.mult))
                pts = ptsums[qc % 2]
                if kt == 0:
                    p.wait("vector", "pe", vals.get(f"ptsum_free{qc % 2}", 0))
                    p.inc("vector", "ve", 1,
                          lambda e, ptb=ptb, pts=pts: e.tensor_copy(
                              pts[:, :], ptb[:, :]))
                else:
                    p.inc("vector", "ve", 1,
                          lambda e, ptb=ptb, w0=w0, pts=pts: e.tensor_tensor(
                              pts[:, w0:1024], pts[:, w0:1024],
                              ptb[:, w0:1024], ALU.add))
                pts_hist[gstep] = p.val("ve")
                pq.append((kt, w0, gstep % 3, gstep, kt == 0, False,
                           pts_hist[gstep] if masked else 0))
                if len(pq) > 2:
                    if not fuv_waited:
                        p.wait("tensor", f"f_uv{u}", 128)
                        fuv_waited = True
                    emit_ov(p, pq.pop(0), P2, L0, L1, vb, onc, pt_sb,
                            exp_hist, ov_hist)
                gstep += 1
            pq[-1] = pq[-1][:5] + (True, pq[-1][6])
            for it in pq:
                emit_ov(p, it, P2, L0, L1, vb, onc, pt_sb,
                        exp_hist, ov_hist)
            pq = []
            fin = {"gend": gstep, "pts": ptsums[qc % 2], "u": u, "qc": qc}
        emit_fin_praw(fin)
        emit_fin_l(fin)
        p.wait("tensor", "ve", fin["ve_praw"])
        emit_fin_osb(fin)
        fin = None
        vals[f"unitpe{u}"] = p.val("pe")
        vals[f"unitse{u}"] = p.val("se")
        vals[f"odma{u}"] = p.val(f"ob{u}")

    emit_ag2(2)
    emit_opiece(2)
    emit_ag2(3)
    emit_opiece(3)
    p.fence("f_og", lambda e: e.dma_start(out=glt[:, :], in_=cc2o[0:128, 0:16]))

    p.wait("sync", "cc", 7)
    for i, (tt, oc4) in enumerate([(t2, o) for t2 in range(8)
                                   for o in range(4)]):
        bank = banks4[i % 4]
        if oc4 == 0:
            xb = xts[tt % 2]
            if tt >= 2:
                p.wait("sync", "ve", vals[f"resid{tt-2}_3"])
            p.dma("sync", f"xr{tt % 2}", lambda e, tt=tt, xb=xb: e.dma_start(
                out=xb[:, :], in_=xin[tt * 128:(tt + 1) * 128, :]))
            vals[f"xl{tt}"] = p.val(f"xr{tt % 2}")
        p.wait("tensor", "f_wo", 128)
        if i == 0:
            p.wait("tensor", "f_og", 128)
        if i >= 4:
            p.wait("tensor", "ve", vals[f"resid{(i-4)//4}_{(i-4)%4}"])
        for j in range(16):
            p.inc("tensor", "pe", 1,
                  lambda e, j=j, tt=tt, oc4=oc4, bank=bank: e.matmul(
                      bank,
                      oall[:, j * 1024 + tt * 128: j * 1024 + (tt + 1) * 128],
                      wb[:, j * D + oc4 * 512: j * D + (oc4 + 1) * 512],
                      start=(j == 0), stop=False))
        p.inc("tensor", "pe", 1, lambda e, oc4=oc4, bank=bank: e.matmul(
            bank, onr[:, :], bor[0:1, oc4 * 512:(oc4 + 1) * 512],
            start=False, stop=True))
        pe_here = p.val("pe")
        p.wait("vector", "pe", pe_here)
        p.wait("vector", f"xr{tt % 2}", vals[f"xl{tt}"])
        p.inc("vector", "ve", 1,
              lambda e, tt=tt, oc4=oc4, bank=bank, xb=xts[tt % 2]:
              e.tensor_tensor(
                  x1[:, tt * D + oc4 * 512: tt * D + (oc4 + 1) * 512],
                  bank, xb[:, oc4 * 512:(oc4 + 1) * 512], ALU.add))
        vals[f"resid{tt}_{oc4}"] = p.val("ve")
    op_mms = p.val("pe")

    p.wait("sync", "pe", op_mms)
    for j in range(16):
        p.dma("sync", "wfl", lambda e, j=j: e.dma_start(
            out=wf_sb[:, j * D:(j + 1) * D], in_=wfT[j * 128:(j + 1) * 128, :]))
    wf_done = p.val("wfl")

    def emit_ffn_pe(tt):
        p.wait("tensor", "se", vals[f"q8done{tt}"])
        for oc4 in range(4):
            i = tt * 4 + oc4
            bank = banks4[i % 4]
            if i >= 4:
                p.wait("tensor", "ve", vals[f"yev{i-4}"])
            for j in range(16):
                p.inc("tensor", "pe", 1,
                      lambda e, j=j, tt=tt, oc4=oc4, bank=bank: e.matmul(
                          bank,
                          xq_fm[:, j * 1024 + tt * 128: j * 1024 + (tt + 1) * 128],
                          wf_sb[:, j * D + oc4 * 512: j * D + (oc4 + 1) * 512],
                          start=(j == 0), stop=(j == 15)))
            vals[f"ympe{i}"] = p.val("pe")

    def emit_ffn_evac(tt):
        for oc4 in range(4):
            i = tt * 4 + oc4
            bank = banks4[i % 4]
            p.wait("vector", "pe", vals[f"ympe{i}"])
            if i >= 4:
                p.wait("vector", f"sgy{(i - 4) % 4}", vals[f"ydma{i-4}"])
            p.inc("vector", "ve", 1,
                  lambda e, tt=tt, oc4=oc4, bank=bank: e.scalar_tensor_tensor(
                      ysb[:, oc4 * 512:(oc4 + 1) * 512], bank, wssc_bc[:, 0:1],
                      x1[:, tt * D + oc4 * 512: tt * D + (oc4 + 1) * 512],
                      ALU.mult, ALU.add))
            vals[f"yev{i}"] = p.val("ve")
            p.wait("scalar", "ve", vals[f"yev{i}"])
            p.dma("scalar", f"sgy{i % 4}", lambda e, tt=tt, oc4=oc4: e.dma_start(
                out=yout[tt * 128:(tt + 1) * 128, oc4 * 512:(oc4 + 1) * 512],
                in_=ysb[:, oc4 * 512:(oc4 + 1) * 512]))
            vals[f"ydma{i}"] = p.val(f"sgy{i % 4}")

    prev_se_r1d = 0
    vals["trD-1"] = 0
    for t in range(8):
        p.wait("vector", "ve", vals[f"resid{t}_3"])
        if t >= 1:
            p.wait("vector", "se", prev_se_r1d)
        p.wait("vector", "pe", vals[f"trD{t-1}"])
        p.wait("scalar", "ve", vals[f"resid{t}_3"])
        if t >= 1:
            p.wait("scalar", "ve", vals[f"ln2ve{t-1}"])
        p.inc("scalar", "se", 1, lambda e, t=t: e.activation(
            sqtsd[:, :], x1[:, t * D:(t + 1) * D], AF.Square))
        se_sq = p.val("se")
        p.wait("vector", "se", se_sq)
        p.inc("vector", "ve", 1, lambda e, t=t: e.reduce_sum(
            mv[:, 0:1], x1[:, t * D:(t + 1) * D], axis=AX.X))
        p.inc("vector", "ve", 1, lambda e: e.reduce_sum(
            mv[:, 1:2], sqtsd[:, :], axis=AX.X))
        vals[f"ln2ve{t}"] = p.val("ve")
        p.inc("vector", "ve", 1, lambda e: e.tensor_scalar(
            stats[:, 0:1], mv[:, 0:1], 1.0 / D, None, ALU.mult))
        p.inc("vector", "ve", 1, lambda e: e.scalar_tensor_tensor(
            stats[:, 1:2], mv[:, 0:1], -1.0 / (D * D), mv[:, 0:1],
            ALU.mult, ALU.mult))
        ve_stats = p.val("ve")
        p.wait("scalar", "ve", ve_stats)
        p.inc("scalar", "se", 1, lambda e: e.activation(
            stats[:, 3:4], mv[:, 1:2], AF.Identity, bias=stats[:, 1:2],
            scale=1.0 / D))
        p.inc("scalar", "se", 1, lambda e: e.activation(
            r1a[:, :], stats[:, 3:4], AF.Ln, bias=epsv[:, :], scale=1.0))
        p.inc("scalar", "se", 1, lambda e: e.activation(
            r1[:, :], r1a[:, :], AF.Exp, bias=0.0, scale=-0.5))
        prev_se_r1d = p.val("se")
        p.wait("vector", "se", prev_se_r1d)
        p.inc("vector", "ve", 1, lambda e: e.scalar_tensor_tensor(
            b2[:, :], stats[:, 0:1], -1.0, r1[:, :], ALU.mult, ALU.mult))
        ve_b2 = p.val("ve")
        p.wait("scalar", "ve", ve_b2)
        p.inc("scalar", "se", 1, lambda e, t=t: e.activation(
            xh2[:, :], x1[:, t * D:(t + 1) * D], AF.Identity,
            bias=b2[:, :], scale=r1[:, :]))
        se_xh2 = p.val("se")
        p.wait("vector", "se", se_xh2)
        if t == 0:
            p.wait("vector", "se", se_bcast)
        p.inc("vector", "ve", 1, lambda e: e.tensor_scalar(
            tq[:, :], xh2[:, :], rinv_bc[:, 0:1], None, ALU.mult))
        p.inc("vector", "ve", 1, lambda e: e.tensor_scalar(
            q8[:, :], tq[:, :], 448.0, -448.0, ALU.min, ALU.max))
        ve_q8 = p.val("ve")
        p.wait("tensor", "ve", ve_q8)
        if t == 0:
            p.wait("tensor", "wfl", wf_done)
        p.wait("tensor", "se", vals.get(f"evP2_{t-1}", 0))
        for jj in range(8):
            p.inc("tensor", "pe", 1,
                  lambda e, jj=jj: e.matmul(
                      P2[:, jj * 128:(jj + 1) * 128],
                      q8[:, jj * 128:(jj + 1) * 128], id_sb[:, :],
                      start=True, stop=True))
        pe_tr0 = p.val("pe")
        p.wait("tensor", "se", vals.get(f"evL_{t-1}", 0))
        for jj in range(8):
            j = 8 + jj
            lt = L0 if jj < 4 else L1
            p.inc("tensor", "pe", 1,
                  lambda e, j=j, jj=jj, lt=lt: e.matmul(
                      lt[:, (jj % 4) * 128:(jj % 4 + 1) * 128],
                      q8[:, j * 128:(j + 1) * 128], id_sb[:, :],
                      start=True, stop=True))
        pe_tr1 = p.val("pe")
        p.wait("scalar", "pe", pe_tr0)
        p.inc("scalar", "se", 1,
              lambda e, t=t: e.copy(
                  xq_fm[:, :].rearrange("p (j c) -> p j c", j=16)
                  [:, 0:8, t * 128:(t + 1) * 128],
                  P2[:, :].rearrange("p (j c) -> p j c", j=8)))
        vals[f"evP2_{t}"] = p.val("se")
        p.wait("scalar", "pe", pe_tr1)
        p.inc("scalar", "se", 1,
              lambda e, t=t: e.copy(
                  xq_fm[:, :].rearrange("p (j c) -> p j c", j=16)
                  [:, 8:12, t * 128:(t + 1) * 128],
                  L0[:, :].rearrange("p (j c) -> p j c", j=4)))
        p.inc("scalar", "se", 1,
              lambda e, t=t: e.copy(
                  xq_fm[:, :].rearrange("p (j c) -> p j c", j=16)
                  [:, 12:16, t * 128:(t + 1) * 128],
                  L1[:, :].rearrange("p (j c) -> p j c", j=4)))
        vals[f"evL_{t}"] = p.val("se")
        vals[f"q8done{t}"] = p.val("se")
        vals[f"trD{t}"] = p.val("pe")
        if t >= 2:
            emit_ffn_evac(t - 2)
        if t >= 1:
            emit_ffn_pe(t - 1)
    emit_ffn_evac(6)
    emit_ffn_pe(7)
    emit_ffn_evac(7)

    for k_ in range(4):
        p.wait("scalar", f"sgy{k_}")
    p.wait("scalar", "st")

    with nc.Block() as block:
        @block.sync
        def _(e):
            for f in p.ops["sync"]:
                f(e)

        @block.gpsimd
        def _(e):
            for f in p.ops["gpsimd"]:
                f(e)

        @block.vector
        def _(e):
            for f in p.ops["vector"]:
                f(e)

        @block.scalar
        def _(e):
            for f in p.ops["scalar"]:
                f(e)

        @block.tensor
        def _(e):
            for f in p.ops["tensor"]:
                f(e)

    return nc


def emit_ov(p, pend, P2, L0, L1, vbh, onc, pt_sb, exp_hist, ov_hist):
    kt, w0, si, g, is_start, is_stop, ve_need = pend
    ptb = pt_sb[si]
    p.wait("tensor", "se", exp_hist[g])
    if ve_need:
        p.wait("tensor", "ve", ve_need)
    for a in ((w0, 512), (512, 1024)):
        a0, b0 = max(a[0], w0), a[1]
        if b0 <= w0 or b0 <= a0:
            continue
        p.inc("tensor", "pe", 1,
              lambda e, kt=kt, a0=a0, b0=b0, ptb=ptb, st_=is_start,
              sp_=is_stop: e.matmul(
                  P2[:, a0:b0], vbh[:, kt * 128:(kt + 1) * 128],
                  ptb[:, a0:b0], start=st_, stop=sp_))
    ov_hist[g] = p.val("pe")


_NC_CACHE = []


def kernel(L0_attn_X, L0_attn_Wq, L0_attn_bq, L0_attn_Wk, L0_attn_bk,
           L0_attn_Wv, L0_attn_bv, L0_attn_Wo, L0_attn_bo, L0_ffn_weight,
           L0_ffn_weight_scale, L0_ffn_input_scale):
    X = np.ascontiguousarray(np.asarray(L0_attn_X, dtype=np.float32)
                             .reshape(B * S, D))
    wqTh = np.ascontiguousarray(np.asarray(L0_attn_Wq, dtype=np.float32).T)
    wkTh = np.ascontiguousarray(np.asarray(L0_attn_Wk, dtype=np.float32).T)
    wvTh = np.ascontiguousarray(np.asarray(L0_attn_Wv, dtype=np.float32).T)
    woTh = np.ascontiguousarray(np.asarray(L0_attn_Wo, dtype=np.float32).T)
    wf = np.asarray(L0_ffn_weight)
    wfTh = np.ascontiguousarray(wf.view(ml_dtypes.float8_e4m3).T)
    bqh = np.ascontiguousarray(
        np.asarray(L0_attn_bq, np.float32).reshape(16, 128).T)
    bkh = np.ascontiguousarray(
        np.asarray(L0_attn_bk, np.float32).reshape(16, 128).T)
    bvh = np.ascontiguousarray(np.asarray(L0_attn_bv, np.float32).reshape(1, D))
    boh = np.ascontiguousarray(np.asarray(L0_attn_bo, np.float32).reshape(1, D))
    wsch = np.asarray(L0_ffn_weight_scale, np.float32).reshape(1, 1)
    isch = np.asarray(L0_ffn_input_scale).reshape(1, 1)
    if isch.dtype != ml_dtypes.bfloat16:
        isch = isch.view(ml_dtypes.bfloat16) if isch.dtype.itemsize == 2 \
            else isch.astype(ml_dtypes.bfloat16)

    ident = np.eye(128, dtype=ml_dtypes.bfloat16)
    onc_h = np.ones((128, 1), dtype=ml_dtypes.bfloat16)
    onr_h = np.ones((1, 128), dtype=ml_dtypes.bfloat16)
    onrf_h = np.ones((1, 128), dtype=np.float32)
    kk, qq = np.meshgrid(np.arange(128), np.arange(128), indexing="ij")
    maskd_h = np.where(kk <= qq, 0.0, NEG).astype(np.float32)
    mexp_h = np.exp(maskd_h).astype(ml_dtypes.bfloat16)

    in_maps = []
    for c in range(NC):
        idxt_h = np.zeros((128, 64), dtype=np.int32)
        pp = np.arange(128, dtype=np.int32)
        for u in range(4):
            g = 4 * c + u
            h = g % 16
            for s_ in range(4):
                for r_ in range(3):
                    idxt_h[:, u * 12 + s_ * 3 + r_] = \
                        (4 * (c // 4) + s_) * 2048 + h * 128 + pp
        for h in range(16):
            orow = 1024 * (h % 4) + (4 * (c // 4) + h // 4) * 128 + pp
            idxt_h[:, 48 + h] = orow * 4 + (c % 4)
        in_maps.append({
            "xin": X[c * T:(c + 1) * T], "wqT": wqTh, "wkT": wkTh,
            "wvT": wvTh, "woT": woTh, "wfT": wfTh, "bqi": bqh, "bki": bkh,
            "bvi": bvh, "boi": boh, "wsc": wsch, "isc": isch,
            "id_in": ident, "onc_in": onc_h, "onr_in": onr_h, "onrf_in": onrf_h,
            "mask_in": maskd_h, "mexp_in": mexp_h, "idxt": idxt_h,
        })

    if not _NC_CACHE:
        nc = bass.Bass("TRN2", target_bir_lowering=False)
        build(nc)
        _NC_CACHE.append(nc)
    nc = _NC_CACHE[0]
    res = run_bass_kernel_spmd(nc, in_maps, core_ids=list(range(NC)))
    out = np.concatenate([res.results[c]["yout"] for c in range(NC)], axis=0)
    return out.reshape(B, S, D)


# version 26
# speedup vs baseline: 1.2225x; 1.0212x over previous
import numpy as np
import ml_dtypes
import concourse.bass as bass
import concourse.mybir as mybir
from concourse.bass import IndirectOffsetOnAxis
from concourse.bass_utils import run_bass_kernel_spmd

F32 = mybir.dt.float32
BF16 = mybir.dt.bfloat16
FP8 = mybir.dt.float8e4
U32 = mybir.dt.uint32
I32 = mybir.dt.int32
ALU = mybir.AluOpType
AF = mybir.ActivationFunctionType
AX = mybir.AxisListType

D = 2048
S = 4096
B = 2
T = 1024
NC = 8
SC = float(128.0 ** -0.25)
EPS = 1e-5
NEG = -30.0
C24 = float(np.exp(24.0))
GRPS = [list(range(8))]
SH = 6144


class Prog:

    def __init__(self, nc):
        self.nc = nc
        self.ops = {"sync": [], "gpsimd": [], "vector": [], "scalar": [],
                    "tensor": []}
        self.sem = {}
        self.cnt = {}

    def mksem(self, name):
        self.sem[name] = self.nc.alloc_semaphore(name)
        self.cnt[name] = 0

    def emit(self, eng, fn):
        self.ops[eng].append(fn)

    def inc(self, eng, sem, n, fn):
        self.cnt[sem] += n
        h = self.sem[sem]
        self.ops[eng].append(lambda e, fn=fn, h=h, n=n: fn(e).then_inc(h, n))

    def dma(self, eng, sem, fn):
        self.inc(eng, sem, 16, fn)

    def wait(self, eng, sem, upto=None):
        v = self.cnt[sem] if upto is None else upto
        if v <= 0:
            return
        h = self.sem[sem]
        self.ops[eng].append(lambda e, h=h, v=v: e.wait_ge(h, v))

    def val(self, sem):
        return self.cnt[sem]

    def fence(self, name, fn):
        self.mksem(name)
        for _ in range(8):
            self.dma("gpsimd", name, fn)


def build(nc):
    xin = nc.dram_tensor("xin", [T, D], F32, kind="ExternalInput").ap()
    wqT = nc.dram_tensor("wqT", [D, D], F32, kind="ExternalInput").ap()
    wkT = nc.dram_tensor("wkT", [D, D], F32, kind="ExternalInput").ap()
    wvT = nc.dram_tensor("wvT", [D, D], F32, kind="ExternalInput").ap()
    woT = nc.dram_tensor("woT", [D, D], F32, kind="ExternalInput").ap()
    wfT = nc.dram_tensor("wfT", [D, D], FP8, kind="ExternalInput").ap()
    bqi = nc.dram_tensor("bqi", [128, 16], F32, kind="ExternalInput").ap()
    bki = nc.dram_tensor("bki", [128, 16], F32, kind="ExternalInput").ap()
    bvi = nc.dram_tensor("bvi", [1, D], F32, kind="ExternalInput").ap()
    boi = nc.dram_tensor("boi", [1, D], F32, kind="ExternalInput").ap()
    wsc = nc.dram_tensor("wsc", [1, 1], F32, kind="ExternalInput").ap()
    isc = nc.dram_tensor("isc", [1, 1], BF16, kind="ExternalInput").ap()
    id_in = nc.dram_tensor("id_in", [128, 128], BF16, kind="ExternalInput").ap()
    onc_in = nc.dram_tensor("onc_in", [128, 1], BF16, kind="ExternalInput").ap()
    onr_in = nc.dram_tensor("onr_in", [1, 128], BF16, kind="ExternalInput").ap()
    onrf_in = nc.dram_tensor("onrf_in", [1, 128], F32, kind="ExternalInput").ap()
    mask_in = nc.dram_tensor("mask_in", [128, 128], F32, kind="ExternalInput").ap()
    mexp_in = nc.dram_tensor("mexp_in", [128, 128], BF16, kind="ExternalInput").ap()
    idxt = nc.dram_tensor("idxt", [128, 64], I32, kind="ExternalInput").ap()
    yout = nc.dram_tensor("yout", [T, D], F32, kind="ExternalOutput").ap()

    cc1i = nc.dram_tensor("cc1i", [SH, 1024], BF16, kind="Internal").ap()
    cc1oq = nc.dram_tensor("cc1oq", [16384, 1024], BF16, kind="Internal",
                           addr_space="Shared").ap()
    cc1ok = nc.dram_tensor("cc1ok", [16384, 1024], BF16, kind="Internal",
                           addr_space="Shared").ap()
    cc1ov = nc.dram_tensor("cc1ov", [16384, 1024], BF16, kind="Internal",
                           addr_space="Shared").ap()
    cc2i = nc.dram_tensor("cc2i", [512, 4096], BF16, kind="Internal").ap()
    cc2o = nc.dram_tensor("cc2o", [4096, 4096], BF16, kind="Internal",
                          addr_space="Shared").ap()

    A = nc.alloc_sbuf_tensor
    idxt_sb = A("idxt_sb", [128, 64], I32)
    id_sb = A("id_sb", [128, 128], BF16)
    onc = A("onc", [128, 1], BF16)
    onr = A("onr", [1, 128], BF16)
    onrf = A("onrf", [1, 128], F32)
    maskd = A("maskd", [128, 128], F32)
    mexpd = A("mexpd", [128, 128], BF16)
    bqs = A("bqs", [128, 16], F32)
    bks = A("bks", [128, 16], F32)
    bvr = A("bvr", [1, D], BF16)
    bor = A("bor", [1, D], BF16)
    scl = A("scl", [1, 8], F32)
    iscb = A("iscb", [1, 1], BF16)
    rinv_bc = A("rinv_bc", [128, 1], F32)
    wssc_bc = A("wssc_bc", [128, 1], F32)
    nb_sb = A("nb_sb", [128, 1], F32)
    mv = A("mv", [128, 2], F32)
    stats = A("stats", [128, 24], F32)
    r1 = A("r1", [128, 1], F32)
    b2 = A("b2", [128, 1], F32)
    smax = A("smax", [1, 8], F32)
    cmx = A("cmx", [1, 16], F32)
    epsv = A("epsv", [128, 1], F32)
    oncf = A("oncf", [128, 1], F32)
    lsb = A("lsb", [1, 1024], F32)
    lnc = A("lnc", [1, 1], F32)
    lsb2 = A("lsb2", [1, 1024], F32)
    r1a = A("r1a", [128, 1], F32)
    glt = A("glt", [128, 16], BF16)

    xts = [A(f"xts{i}", [128, D], F32) for i in range(2)]
    xh_ts = [A(f"xh{i}", [128, D], BF16) for i in range(2)]
    stg = [A(f"stg{i}", [128, 512], BF16) for i in range(4)]

    xhat_fm = A("xhat_fm", [128, 16 * 1024], BF16)
    a_xf = nc.lookup_mloc(xhat_fm).addr
    oall = nc.alloc_sbuf_tensor_at("oall", [128, 16 * 1024], BF16, offset=a_xf)

    wb = A("wb", [128, 16 * D], BF16)
    a_wb = nc.lookup_mloc(wb).addr
    wf_sb = nc.alloc_sbuf_tensor_at("wf_sb", [128, 16 * D], FP8, offset=a_wb)
    xq_fm = nc.alloc_sbuf_tensor_at("xq_fm", [128, 16 * 1024], FP8,
                                    offset=a_wb + 32768)

    x1 = A("x1", [128, 8 * D], F32)
    a_x1 = nc.lookup_mloc(x1).addr
    qkv = nc.alloc_sbuf_tensor_at("qkv", [128, 3 * 4096], BF16, offset=a_x1)
    qbh = qkv[:, 0:4096]
    kbh = qkv[:, 4096:8192]
    vbh = qkv[:, 8192:12288]
    pt_sb = [nc.alloc_sbuf_tensor_at(f"pt{i}", [128, 1024], BF16,
                                     offset=a_x1 + 24576 + i * 2048)
             for i in range(2)]
    linv_bc = nc.alloc_sbuf_tensor_at("linv_bc", [128, 1024], F32,
                                      offset=a_x1 + 28672)
    osb = nc.alloc_sbuf_tensor_at("osb", [128, 1024], BF16,
                                  offset=a_x1 + 32768)
    qsq = nc.alloc_sbuf_tensor_at("qsq", [128, 512], BF16,
                                  offset=a_x1 + 34816)
    a_xts0 = nc.lookup_mloc(xts[0]).addr
    a_xts1 = nc.lookup_mloc(xts[1]).addr
    tq = nc.alloc_sbuf_tensor_at("tq", [128, D], F32, offset=a_xts0)
    xh2 = nc.alloc_sbuf_tensor_at("xh2", [128, D], F32, offset=a_xts1)
    a_xh0 = nc.lookup_mloc(xh_ts[0]).addr
    q8 = nc.alloc_sbuf_tensor_at("q8", [128, D], FP8, offset=a_xh0)
    a_stg0 = nc.lookup_mloc(stg[0]).addr
    assert nc.lookup_mloc(stg[3]).addr == a_stg0 + 3 * 1024
    sqts = nc.alloc_sbuf_tensor_at("sqts", [128, D], BF16, offset=a_stg0)
    a_xh1 = nc.lookup_mloc(xh_ts[1]).addr
    sqtsd = nc.alloc_sbuf_tensor_at("sqtsd", [128, D], BF16, offset=a_xh1)
    wb2 = nc.alloc_sbuf_tensor_at("wb2", [128, 16 * D], BF16, offset=a_x1)
    qkv2 = nc.alloc_sbuf_tensor_at("qkv2", [128, 3 * 4096], BF16,
                                   offset=a_x1 + 36864)
    qkvs = [qkv, qkv2]
    ptsum = nc.alloc_sbuf_tensor_at("ptsum", [128, 1024], F32,
                                    offset=a_x1 + 61440)
    ptsum2 = nc.alloc_sbuf_tensor_at("ptsum2", [128, 1024], F32,
                                     offset=a_xh1)
    ysb = nc.alloc_sbuf_tensor_at("ysb", [128, D], F32, offset=a_xf)
    ptsums = [ptsum, ptsum2]
    praw = nc.alloc_sbuf_tensor_at("praw", [128, 1024], BF16, offset=a_stg0)
    pt_sb3 = nc.alloc_sbuf_tensor_at("pt3", [128, 1024], BF16,
                                     offset=a_stg0 + 2048)

    P0 = nc.alloc_psum_tensor("P0", [128, 1024], F32)
    P1 = nc.alloc_psum_tensor("P1", [128, 1024], F32)
    P2 = nc.alloc_psum_tensor("P2", [128, 1024], F32)
    L0 = nc.alloc_psum_tensor("L0", [128, 512], F32)
    L1 = nc.alloc_psum_tensor("L1", [128, 512], F32)
    banks4 = [P0[:, 0:512], P0[:, 512:1024], P1[:, 0:512], P1[:, 512:1024]]

    p = Prog(nc)
    sems = ["ld", "st", "wt", "pe", "ve", "se", "cc", "gp", "gl", "junk",
            "ob0", "ob1", "ob2", "ob3", "xr0", "xr1", "wfl"]
    for w_ in ("q", "k", "v", "y"):
        for k_ in range(4):
            sems.append(f"sg{w_}{k_}")
    for s_ in sems:
        p.mksem(s_)
    vals = {}

    for out_, in_ in ((idxt_sb, idxt), (id_sb, id_in), (onc, onc_in),
                      (onr, onr_in), (onrf, onrf_in), (maskd, mask_in),
                      (mexpd, mexp_in), (bqs, bqi),
                      (bks, bki)):
        p.dma("sync", "ld", lambda e, o=out_, i=in_: e.dma_start(
            out=o[:, :], in_=i[:, :]))
    p.dma("sync", "ld", lambda e: e.dma_start(out=scl[0:1, 0:1], in_=wsc[:, :]))
    start_ld = p.val("ld")
    p.dma("gpsimd", "junk", lambda e: e.dma_start(out=bvr[:, :], in_=bvi[:, :]))
    p.dma("gpsimd", "junk", lambda e: e.dma_start(out=bor[:, :], in_=boi[:, :]))
    p.dma("gpsimd", "junk", lambda e: e.dma_start(out=iscb[:, :], in_=isc[:, :]))
    p.fence("f_sm", lambda e: e.dma_start(out=glt[:, :], in_=xin[0:128, 0:16]))
    for j in range(16):
        p.dma("gpsimd", "junk", lambda e, j=j: e.dma_start(
            out=wb[:, j * D:(j + 1) * D], in_=wqT[j * 128:(j + 1) * 128, :]))
    p.fence("f_wq", lambda e: e.dma_start(out=glt[:, :], in_=xin[0:128, 0:16]))

    p.wait("vector", "ld", start_ld)
    p.inc("vector", "ve", 1, lambda e: e.memset(epsv[:, :], EPS))
    p.inc("vector", "ve", 1, lambda e: e.memset(lnc[:, :], 24.0))
    p.inc("vector", "ve", 1, lambda e: e.memset(oncf[:, :], 1.0))
    p.inc("vector", "ve", 1, lambda e: e.tensor_scalar(
        bqs[:, :], bqs[:, :], SC, None, ALU.mult))
    p.inc("vector", "ve", 1, lambda e: e.tensor_scalar(
        bks[:, :], bks[:, :], SC, None, ALU.mult))
    p.wait("vector", "f_sm", 128)
    p.inc("vector", "ve", 1, lambda e: e.tensor_copy(scl[0:1, 3:4], iscb[:, :]))
    ve_scl = p.val("ve")
    p.wait("scalar", "ve", ve_scl)
    p.inc("scalar", "se", 1, lambda e: e.activation(
        scl[0:1, 4:5], scl[0:1, 3:4], AF.Ln))
    p.inc("scalar", "se", 1, lambda e: e.activation(
        scl[0:1, 1:2], scl[0:1, 4:5], AF.Exp, scale=-1.0))
    se_scl = p.val("se")
    p.wait("tensor", "ve", ve_scl)
    p.wait("tensor", "se", se_scl)
    p.inc("tensor", "pe", 1, lambda e: e.matmul(
        P2[:, 0:1], onrf[0:1, :], scl[0:1, 1:2], start=True, stop=True))
    p.inc("tensor", "pe", 1, lambda e: e.matmul(
        P2[:, 1:2], onrf[0:1, :], scl[0:1, 0:1], start=True, stop=True))
    p.inc("tensor", "pe", 1, lambda e: e.matmul(
        P2[:, 2:3], onrf[0:1, :], scl[0:1, 3:4], start=True, stop=True))
    pe_bc0 = p.val("pe")
    p.wait("scalar", "pe", pe_bc0)
    p.inc("scalar", "se", 1, lambda e: e.copy(rinv_bc[:, :], P2[:, 0:1]))
    p.inc("scalar", "se", 1, lambda e: e.copy(nb_sb[:, :], P2[:, 1:2]))
    se_bcast = p.val("se")
    p.wait("vector", "pe", pe_bc0)
    p.wait("vector", "se", se_bcast)
    p.inc("vector", "ve", 1, lambda e: e.tensor_tensor(
        wssc_bc[:, :], nb_sb[:, :], P2[:, 2:3], ALU.mult))
    ve_bcast = p.val("ve")

    def ln_tiles(src_getter, dst_q8=False):
        pass

    prev_se_r1 = 0
    prev_lnapply = {}
    prev_trev = {-1: 0}
    for t in range(8):
        xb, hb = xts[t % 2], xh_ts[t % 2]
        if t >= 2:
            p.wait("sync", "se", prev_lnapply[t - 2])
        p.dma("sync", "ld", lambda e, t=t, xb=xb: e.dma_start(
            out=xb[:, :], in_=xin[t * 128:(t + 1) * 128, :]))
        ld_here = p.val("ld")
        p.wait("scalar", "ld", ld_here)
        if t >= 1:
            p.wait("scalar", "ve", vals.get("ln1ve", 0))
        p.inc("scalar", "se", 1, lambda e, xb=xb: e.activation(
            sqts[:, :], xb[:, :], AF.Square))
        se_sq = p.val("se")
        p.wait("vector", "ld", ld_here)
        p.wait("vector", "se", se_sq)
        if t >= 1:
            p.wait("vector", "se", prev_se_r1)
        p.inc("vector", "ve", 1, lambda e, xb=xb: e.reduce_sum(
            mv[:, 0:1], xb[:, :], axis=AX.X))
        p.inc("vector", "ve", 1, lambda e: e.reduce_sum(
            mv[:, 1:2], sqts[:, :], axis=AX.X))
        p.inc("vector", "ve", 1, lambda e: e.tensor_scalar(
            stats[:, 0:1], mv[:, 0:1], 1.0 / D, None, ALU.mult))
        p.inc("vector", "ve", 1, lambda e: e.scalar_tensor_tensor(
            stats[:, 1:2], mv[:, 0:1], -1.0 / (D * D), mv[:, 0:1],
            ALU.mult, ALU.mult))
        ve_stats = p.val("ve")
        vals["ln1ve"] = ve_stats
        p.wait("scalar", "ve", ve_stats)
        p.inc("scalar", "se", 1, lambda e: e.activation(
            stats[:, 3:4], mv[:, 1:2], AF.Identity, bias=stats[:, 1:2],
            scale=1.0 / D))
        p.inc("scalar", "se", 1, lambda e: e.activation(
            r1a[:, :], stats[:, 3:4], AF.Ln, bias=epsv[:, :], scale=1.0))
        p.inc("scalar", "se", 1, lambda e: e.activation(
            r1[:, :], r1a[:, :], AF.Exp, bias=0.0, scale=-0.5))
        prev_se_r1 = p.val("se")
        p.wait("vector", "se", prev_se_r1)
        p.inc("vector", "ve", 1, lambda e: e.scalar_tensor_tensor(
            b2[:, :], stats[:, 0:1], -1.0, r1[:, :], ALU.mult, ALU.mult))
        ve_b2 = p.val("ve")
        p.wait("scalar", "ve", ve_b2)
        if t >= 2:
            p.wait("scalar", "pe", vals[f"tr{t-2}"])
        p.inc("scalar", "se", 1, lambda e, xb=xb, hb=hb: e.activation(
            hb[:, :], xb[:, :], AF.Identity, bias=b2[:, :], scale=r1[:, :]))
        prev_lnapply[t] = p.val("se")
        for rnd in range(2):
            ps = P2 if rnd == 0 else P1
            if t == 0 and rnd == 0:
                p.wait("tensor", "se", se_bcast)
                p.wait("tensor", "ve", ve_bcast)
            p.wait("tensor", "se", prev_lnapply[t])
            p.wait("tensor", "ve", prev_trev.get(t - 1, prev_trev[-1])
                   if rnd == 0 else vals.get(f"trevP1_{t-1}", 0))
            for jj in range(8):
                j = rnd * 8 + jj
                p.inc("tensor", "pe", 1,
                      lambda e, hb=hb, j=j, jj=jj, ps=ps: e.matmul(
                          ps[:, jj * 128:(jj + 1) * 128],
                          hb[:, j * 128:(j + 1) * 128], id_sb[:, :],
                          start=True, stop=True))
            pe_tr = p.val("pe")
            p.wait("vector", "pe", pe_tr)
            p.inc("vector", "ve", 1,
                  lambda e, t=t, rnd=rnd, ps=ps: e.tensor_copy(
                      xhat_fm[:, :].rearrange("p (j c) -> p j c", j=16)
                      [:, rnd * 8:(rnd + 1) * 8, t * 128:(t + 1) * 128],
                      ps[:, :].rearrange("p (j c) -> p j c", j=8)))
            if rnd == 0:
                prev_trev[t] = p.val("ve")
            else:
                vals[f"trevP1_{t}"] = p.val("ve")
        vals[f"tr{t}"] = p.val("pe")
        if t == 3:
            vals["xhat_half0"] = p.val("ve")
    xhat_ve = p.val("ve")

    def proj_qk(which, wsem, bias_sb, base, wbuf, prevsec=None):
        for i, (tb, oc) in enumerate([(t2, o) for t2 in range(2)
                                      for o in range(16)]):
            bank, sb = banks4[i % 4], stg[i % 4]
            sem = f"sg{which}{i % 4}"
            p.wait("tensor", wsem, 128)
            if i == 0:
                p.wait("tensor", "ve", vals["xhat_half0"])
            elif i == 16:
                p.wait("tensor", "ve", xhat_ve)
            if i >= 4:
                p.wait("tensor", sem, vals[f"{which}dma{i-4}"])
            elif prevsec:
                p.wait("tensor", f"sg{prevsec}{i % 4}",
                       vals[f"{prevsec}sgfull{i % 4}"])
            for j in range(16):
                p.inc("tensor", "pe", 1,
                      lambda e, j=j, oc=oc, tb=tb, bank=bank, wbuf=wbuf:
                      e.matmul(
                          bank, wbuf[:, j * D + oc * 128: j * D + (oc + 1) * 128],
                          xhat_fm[:, j * 1024 + tb * 512:
                                  j * 1024 + (tb + 1) * 512],
                          start=(j == 0), stop=(j == 15)))
            pe_here = p.val("pe")
            p.wait("scalar", "pe", pe_here)
            p.inc("scalar", "se", 1,
                  lambda e, bank=bank, sb=sb, oc=oc, bias_sb=bias_sb:
                  e.activation(sb[:, :], bank, AF.Identity,
                               bias=bias_sb[:, oc:oc + 1], scale=SC))
            p.dma("scalar", sem,
                  lambda e, sb=sb, oc=oc, tb=tb, base=base: e.dma_start(
                      out=cc1i[oc * 128 + base: oc * 128 + base + 128,
                               tb * 512:(tb + 1) * 512], in_=sb[:, :]))
            vals[f"{which}dma{i}"] = p.val(sem)
        for k_ in range(4):
            vals[f"{which}sgfull{k_}"] = p.val(f"sg{which}{k_}")

    for j in range(16):
        p.dma("gpsimd", "junk", lambda e, j=j: e.dma_start(
            out=wb2[:, j * D:(j + 1) * D], in_=wkT[j * 128:(j + 1) * 128, :]))
    p.fence("f_wk", lambda e: e.dma_start(out=glt[:, :], in_=xin[0:128, 0:16]))
    proj_qk("q", "f_wq", bqs, 0, wb)
    q_mms = p.val("pe")
    for k_ in range(4):
        p.wait("gpsimd", f"sgq{k_}", vals[f"qsgfull{k_}"])
    p.inc("gpsimd", "cc", 1, lambda e: e.collective_compute(
        "AllGather", ALU.bypass, ins=[cc1i[0:2048, :]], outs=[cc1oq[:, :]],
        replica_groups=GRPS))
    p.wait("gpsimd", "pe", q_mms)
    for j in range(16):
        p.dma("gpsimd", "junk", lambda e, j=j: e.dma_start(
            out=wb[:, j * D:(j + 1) * D], in_=wvT[j * 128:(j + 1) * 128, :]))
    p.fence("f_wv", lambda e: e.dma_start(out=glt[:, :], in_=xin[0:128, 0:16]))
    proj_qk("k", "f_wk", bks, 2048, wb2, prevsec="q")
    for k_ in range(4):
        p.wait("gpsimd", f"sgk{k_}", vals[f"ksgfull{k_}"])
    p.inc("gpsimd", "cc", 1, lambda e: e.collective_compute(
        "AllGather", ALU.bypass, ins=[cc1i[2048:4096, :]], outs=[cc1ok[:, :]],
        replica_groups=GRPS))

    for i, (tt, oc4) in enumerate([(t2, o) for t2 in range(8)
                                   for o in range(4)]):
        bank, sb = banks4[i % 4], stg[i % 4]
        sem = f"sgv{i % 4}"
        p.wait("tensor", "f_wv", 128)
        if i >= 4:
            p.wait("tensor", sem, vals[f"vdma{i-4}"])
        else:
            p.wait("tensor", f"sgk{i % 4}", vals[f"ksgfull{i % 4}"])
        for j in range(16):
            p.inc("tensor", "pe", 1,
                  lambda e, j=j, tt=tt, oc4=oc4, bank=bank: e.matmul(
                      bank,
                      xhat_fm[:, j * 1024 + tt * 128: j * 1024 + (tt + 1) * 128],
                      wb[:, j * D + oc4 * 512: j * D + (oc4 + 1) * 512],
                      start=(j == 0), stop=False))
        p.inc("tensor", "pe", 1, lambda e, oc4=oc4, bank=bank: e.matmul(
            bank, onr[:, :], bvr[0:1, oc4 * 512:(oc4 + 1) * 512],
            start=False, stop=True))
        pe_here = p.val("pe")
        p.wait("scalar", "pe", pe_here)
        p.inc("scalar", "se", 1, lambda e, bank=bank, sb=sb: e.activation(
            sb[:, :], bank, AF.Identity))
        p.dma("scalar", sem, lambda e, sb=sb, tt=tt, oc4=oc4: e.dma_start(
            out=cc1i[4096:6144, tt * 128:(tt + 1) * 128]
            .rearrange("(i q) c -> q i c", q=128)
            [:, 4 * oc4:4 * oc4 + 4, :],
            in_=sb[:, :].rearrange("p (i c) -> p i c", i=4)))
        vals[f"vdma{i}"] = p.val(sem)

    for k_ in range(4):
        p.wait("gpsimd", f"sgv{k_}", p.val(f"sgv{k_}"))
    p.inc("gpsimd", "cc", 1, lambda e: e.collective_compute(
        "AllGather", ALU.bypass, ins=[cc1i[4096:6144, :]], outs=[cc1ov[:, :]],
        replica_groups=GRPS))

    p.wait("gpsimd", "ld", start_ld)
    pt_sb.append(pt_sb3)
    gstep = 0
    exp_hist = {}
    ov_hist = {}
    pts_hist = {}
    secs = (cc1oq, cc1ok, cc1ov)

    def emit_fin_l(fin):
        p.wait("tensor", "ve", pts_hist[fin["gend"] - 1])
        p.wait("tensor", "se", vals.get("se_lbank_free", 0))
        pts = fin["pts"]
        p.inc("tensor", "pe", 1, lambda e, pts=pts: e.matmul(
            L0[0:1, :], oncf[:, :], pts[:, 0:512], start=True, stop=True))
        p.inc("tensor", "pe", 1, lambda e, pts=pts: e.matmul(
            L1[0:1, :], oncf[:, :], pts[:, 512:1024], start=True, stop=True))
        vals[f"ptsum_free{fin['qc'] % 2}"] = p.val("pe")
        pe_qc = p.val("pe")
        p.wait("scalar", "pe", pe_qc)
        p.inc("scalar", "se", 1, lambda e: e.activation(
            lsb[0:1, 0:512], L0[0:1, :], AF.Ln, scale=C24))
        p.inc("scalar", "se", 1, lambda e: e.activation(
            lsb[0:1, 512:1024], L1[0:1, :], AF.Ln, scale=C24))
        p.inc("scalar", "se", 1, lambda e: e.activation(
            lsb2[0:1, :], lsb[0:1, :], AF.Exp, bias=lnc[0:1, :], scale=-1.0))
        vals["se_l_last"] = p.val("se")

    def emit_fin_osb(fin):
        u, qc = fin["u"], fin["qc"]
        p.wait("tensor", "se", vals["se_l_last"])
        p.inc("tensor", "pe", 1, lambda e: e.matmul(
            L0[:, :], onrf[0:1, :], lsb2[0:1, 0:512], start=True, stop=True))
        p.inc("tensor", "pe", 1, lambda e: e.matmul(
            L1[:, :], onrf[0:1, :], lsb2[0:1, 512:1024], start=True, stop=True))
        pe_l = p.val("pe")
        p.wait("scalar", "pe", pe_l)
        p.inc("scalar", "se", 1, lambda e: e.copy(linv_bc[:, 0:512], L0[:, :]))
        p.inc("scalar", "se", 1, lambda e: e.copy(
            linv_bc[:, 512:1024], L1[:, :]))
        se_linv = p.val("se")
        vals["se_lbank_free"] = se_linv
        p.wait("vector", "se", se_linv)
        p.wait("vector", f"ob{u}", p.val(f"ob{u}"))
        p.inc("vector", "ve", 1, lambda e: e.tensor_tensor(
            osb[:, :], praw[:, :], linv_bc[:, :], ALU.mult))
        ve_o = p.val("ve")
        p.wait("scalar", "ve", ve_o)
        p.dma("scalar", f"ob{u}", lambda e, u=u, qc=qc: e.dma_start(
            out=cc2i[u * 128:(u + 1) * 128, qc * 1024:(qc + 1) * 1024],
            in_=osb[:, :]))

    def emit_fin_praw(fin):
        p.wait("vector", "pe", ov_hist[fin["gend"] - 1])
        p.inc("vector", "ve", 1, lambda e: e.tensor_copy(
            praw[:, :], P2[:, :]))
        fin["ve_praw"] = p.val("ve")

    def emit_ag2(u):
        p.wait("gpsimd", f"ob{u}", 64)
        p.inc("gpsimd", "cc", 1, lambda e, u=u: e.collective_compute(
            "AllGather", ALU.bypass, ins=[cc2i[u * 128:(u + 1) * 128, :]],
            outs=[cc2o[u * 1024:(u + 1) * 1024, :]],
            replica_groups=GRPS))

    def emit_opiece(pu):
        p.wait("gpsimd", "cc", 4 + pu)
        for h in range(pu, 16, 4):
            p.dma("gpsimd", "junk", lambda e, h=h: e.indirect_dma_start(
                oall[:, h * 1024:(h + 1) * 1024], None,
                cc2o[:, :].rearrange("a (b c) -> (a b) c", c=1024),
                IndirectOffsetOnAxis(idxt_sb[:, 48 + h:49 + h], 0)))
        p.fence(f"f_og{pu}", lambda e: e.dma_start(
            out=glt[:, :], in_=cc2o[0:128, 0:16]))

    for u in range(4):
        qkvu = qkvs[u % 2]
        qb = qkvu[:, 0:4096]
        kb = qkvu[:, 4096:8192]
        vb = qkvu[:, 8192:12288]
        if u == 1:
            for j in range(16):
                p.dma("gpsimd", "junk", lambda e, j=j: e.dma_start(
                    out=wb[:, j * D:(j + 1) * D],
                    in_=woT[j * 128:(j + 1) * 128, :]))
            p.fence("f_wo", lambda e: e.dma_start(
                out=glt[:, :], in_=xin[0:128, 0:16]))
        if u >= 2:
            emit_ag2(u - 2)
            emit_opiece(u - 2)
            p.wait("gpsimd", "pe", vals[f"unitpe{u-2}"])
            p.wait("gpsimd", "se", vals[f"unitse{u-2}"])
        if u == 0:
            p.wait("gpsimd", "cc", 2)
        for s_ in range(4):
            for r_ in range(2):
                p.dma("gpsimd", "junk", lambda e, qkvu=qkvu, u=u, s_=s_, r_=r_:
                      e.indirect_dma_start(
                    qkvu[:, r_ * 4096 + s_ * 1024: r_ * 4096 + (s_ + 1) * 1024],
                    None, secs[r_][:, :],
                    IndirectOffsetOnAxis(
                        idxt_sb[:, u * 12 + s_ * 3 + r_:
                                u * 12 + s_ * 3 + r_ + 1], 0)))
        p.fence(f"f_uqk{u}", lambda e: e.dma_start(
            out=glt[:, :], in_=cc1oq[0:128, 0:16]))
        if u == 0:
            p.wait("gpsimd", "cc", 3)
        for s_ in range(4):
            p.dma("gpsimd", "junk", lambda e, qkvu=qkvu, u=u, s_=s_:
                  e.indirect_dma_start(
                qkvu[:, 8192 + s_ * 1024: 8192 + (s_ + 1) * 1024],
                None, secs[2][:, :],
                IndirectOffsetOnAxis(
                    idxt_sb[:, u * 12 + s_ * 3 + 2:
                            u * 12 + s_ * 3 + 3], 0)))
        p.fence(f"f_uv{u}", lambda e: e.dma_start(
            out=glt[:, :], in_=cc1oq[0:128, 0:16]))

        p.wait("scalar", f"f_uqk{u}", 128)
        if u > 0:
            p.wait("vector", "se", vals[f"unitse{u-1}"])
        for w, (src, slot) in enumerate(((qb, 0), (kb, 1))):
            for c8 in range(8):
                if c8 > 0 or w > 0:
                    p.wait("scalar", "pe", p.val("pe"))
                p.inc("scalar", "se", 1, lambda e, src=src, c8=c8: e.activation(
                    qsq[:, :], src[:, c8 * 512:(c8 + 1) * 512], AF.Square))
                se_h = p.val("se")
                p.wait("tensor", "se", se_h)
                p.wait("tensor", "ve", p.val("ve"))
                p.inc("tensor", "pe", 1, lambda e: e.matmul(
                    L0[0:1, :], onc[:, :], qsq[:, :], start=True, stop=True))
                pe_h = p.val("pe")
                p.wait("vector", "pe", pe_h)
                p.inc("vector", "ve", 1,
                      lambda e, w=w, c8=c8: e.reduce_max(
                          cmx[0:1, w * 8 + c8:w * 8 + c8 + 1],
                          L0[0:1, :], axis=AX.X))
        p.wait("scalar", "ve", p.val("ve"))
        p.inc("scalar", "se", 1, lambda e: e.copy(
            smax[0:1, 7:8], scl[0:1, 0:1]))
        p.wait("vector", "se", p.val("se"))
        p.inc("vector", "ve", 1, lambda e: e.reduce_max(
            smax[0:1, 0:1], cmx[0:1, 0:8], axis=AX.X))
        p.inc("vector", "ve", 1, lambda e: e.reduce_max(
            smax[0:1, 1:2], cmx[0:1, 8:16], axis=AX.X))
        p.wait("scalar", "ve", p.val("ve"))
        p.inc("scalar", "se", 1, lambda e: e.activation(
            smax[0:1, 2:3], smax[0:1, 0:1], AF.Ln))
        p.inc("scalar", "se", 1, lambda e: e.activation(
            smax[0:1, 5:6], smax[0:1, 1:2], AF.Ln))
        p.wait("vector", "se", p.val("se"))
        p.inc("vector", "ve", 1, lambda e: e.tensor_tensor(
            smax[0:1, 3:4], smax[0:1, 2:3], smax[0:1, 5:6], ALU.add))
        p.wait("scalar", "ve", p.val("ve"))
        p.inc("scalar", "se", 1, lambda e: e.activation(
            smax[0:1, 6:7], smax[0:1, 3:4], AF.Exp, scale=0.5))
        se_bnd = p.val("se")
        p.wait("vector", "se", se_bnd)
        p.inc("vector", "ve", 1, lambda e: e.tensor_scalar(
            smax[0:1, 4:5], smax[0:1, 6:7], -1.0, None, ALU.mult))
        ve_nb = p.val("ve")
        p.wait("tensor", "ve", ve_nb)
        p.inc("tensor", "pe", 1, lambda e: e.matmul(
            P2[:, 0:1], onrf[0:1, :], smax[0:1, 4:5], start=True, stop=True))
        pe_nb = p.val("pe")
        p.wait("scalar", "pe", pe_nb)
        p.inc("scalar", "se", 1, lambda e: e.copy(nb_sb[:, :], P2[:, 0:1]))
        se_nb = p.val("se")
        p.wait("tensor", f"f_uqk{u}", 128)
        fuv_waited = False

        fin = None
        for qc in range(4):
            qcb = qc * 1024
            nkt = 8 * qc + 8
            pq = []
            for kt in range(nkt):
                w0 = max(kt * 128 - qcb, 0)
                stp = P0 if gstep % 2 == 0 else P1
                ptb = pt_sb[gstep % 3]
                if fin is not None:
                    if kt == 0:
                        emit_fin_praw(fin)
                    elif kt == 1:
                        emit_fin_l(fin)
                        p.wait("tensor", "ve", fin["ve_praw"])
                    elif kt == 2:
                        emit_fin_osb(fin)
                        fin = None
                if gstep - 2 in exp_hist:
                    p.wait("tensor", "se", exp_hist[gstep - 2])
                for a in ((w0, 512), (512, 1024)):
                    a0, b0 = max(a[0], w0), a[1]
                    if b0 <= w0 or b0 <= a0:
                        continue
                    p.inc("tensor", "pe", 1,
                          lambda e, kt=kt, a0=a0, b0=b0, qcb=qcb, stp=stp,
                          kb=kb, qb=qb:
                          e.matmul(stp[:, a0:b0],
                                   kb[:, kt * 128:(kt + 1) * 128],
                                   qb[:, qcb + a0:qcb + b0],
                                   start=True, stop=True))
                qk_here = p.val("pe")
                masked = kt * 128 >= qcb
                p.wait("scalar", "pe", qk_here)
                if gstep - 3 in ov_hist:
                    p.wait("scalar", "pe", ov_hist[gstep - 3])
                if gstep - 3 in pts_hist:
                    p.wait("scalar", "ve", pts_hist[gstep - 3])
                p.inc("scalar", "se", 1,
                      lambda e, stp=stp, ptb=ptb, w0=w0: e.activation(
                          ptb[:, w0:1024], stp[:, w0:1024], AF.Exp,
                          bias=nb_sb[:, :], scale=1.0))
                exp_hist[gstep] = p.val("se")
                p.wait("vector", "se", exp_hist[gstep])
                if masked:
                    p.inc("vector", "ve", 1,
                          lambda e, ptb=ptb, w0=w0: e.tensor_tensor(
                              ptb[:, w0:w0 + 128], ptb[:, w0:w0 + 128],
                              mexpd[:, :], ALU.mult))
                pts = ptsums[qc % 2]
                if kt == 0:
                    p.wait("vector", "pe", vals.get(f"ptsum_free{qc % 2}", 0))
                    p.inc("vector", "ve", 1,
                          lambda e, ptb=ptb, pts=pts: e.tensor_copy(
                              pts[:, :], ptb[:, :]))
                else:
                    p.inc("vector", "ve", 1,
                          lambda e, ptb=ptb, w0=w0, pts=pts: e.tensor_tensor(
                              pts[:, w0:1024], pts[:, w0:1024],
                              ptb[:, w0:1024], ALU.add))
                pts_hist[gstep] = p.val("ve")
                pq.append((kt, w0, gstep % 3, gstep, kt == 0, False,
                           pts_hist[gstep] if masked else 0))
                if len(pq) > 2:
                    if not fuv_waited:
                        p.wait("tensor", f"f_uv{u}", 128)
                        fuv_waited = True
                    emit_ov(p, pq.pop(0), P2, L0, L1, vb, onc, pt_sb,
                            exp_hist, ov_hist)
                gstep += 1
            pq[-1] = pq[-1][:5] + (True, pq[-1][6])
            for it in pq:
                emit_ov(p, it, P2, L0, L1, vb, onc, pt_sb,
                        exp_hist, ov_hist)
            pq = []
            fin = {"gend": gstep, "pts": ptsums[qc % 2], "u": u, "qc": qc}
        emit_fin_praw(fin)
        emit_fin_l(fin)
        p.wait("tensor", "ve", fin["ve_praw"])
        emit_fin_osb(fin)
        fin = None
        vals[f"unitpe{u}"] = p.val("pe")
        vals[f"unitse{u}"] = p.val("se")
        vals[f"odma{u}"] = p.val(f"ob{u}")

    emit_ag2(2)
    emit_opiece(2)
    emit_ag2(3)
    emit_opiece(3)

    p.wait("sync", "cc", 7)
    for i, (tt, oc4) in enumerate([(t2, o) for t2 in range(8)
                                   for o in range(4)]):
        bank = banks4[i % 4]
        if oc4 == 0:
            xb = xts[tt % 2]
            if tt >= 2:
                p.wait("sync", "ve", vals[f"resid{tt-2}_3"])
            p.dma("sync", f"xr{tt % 2}", lambda e, tt=tt, xb=xb: e.dma_start(
                out=xb[:, :], in_=xin[tt * 128:(tt + 1) * 128, :]))
            vals[f"xl{tt}"] = p.val(f"xr{tt % 2}")
        p.wait("tensor", "f_wo", 128)
        if i == 0:
            for pu_ in range(3):
                p.wait("tensor", f"f_og{pu_}", 128)
        if i >= 4:
            p.wait("tensor", "ve", vals[f"resid{(i-4)//4}_{(i-4)%4}"])
        jorder = [j for j in range(16) if j % 4 != 3] + [3, 7, 11, 15]
        for jn, j in enumerate(jorder):
            if i == 0 and jn == 12:
                p.wait("tensor", "f_og3", 128)
            p.inc("tensor", "pe", 1,
                  lambda e, j=j, tt=tt, oc4=oc4, bank=bank, jn=jn: e.matmul(
                      bank,
                      oall[:, j * 1024 + tt * 128: j * 1024 + (tt + 1) * 128],
                      wb[:, j * D + oc4 * 512: j * D + (oc4 + 1) * 512],
                      start=(jn == 0), stop=False))
        p.inc("tensor", "pe", 1, lambda e, oc4=oc4, bank=bank: e.matmul(
            bank, onr[:, :], bor[0:1, oc4 * 512:(oc4 + 1) * 512],
            start=False, stop=True))
        pe_here = p.val("pe")
        p.wait("vector", "pe", pe_here)
        p.wait("vector", f"xr{tt % 2}", vals[f"xl{tt}"])
        p.inc("vector", "ve", 1,
              lambda e, tt=tt, oc4=oc4, bank=bank, xb=xts[tt % 2]:
              e.tensor_tensor(
                  x1[:, tt * D + oc4 * 512: tt * D + (oc4 + 1) * 512],
                  bank, xb[:, oc4 * 512:(oc4 + 1) * 512], ALU.add))
        vals[f"resid{tt}_{oc4}"] = p.val("ve")
    op_mms = p.val("pe")

    p.wait("sync", "pe", op_mms)
    for j in range(16):
        p.dma("sync", "wfl", lambda e, j=j: e.dma_start(
            out=wf_sb[:, j * D:(j + 1) * D], in_=wfT[j * 128:(j + 1) * 128, :]))
    wf_done = p.val("wfl")

    def emit_ffn_pe(tt):
        p.wait("tensor", "se", vals[f"q8done{tt}"])
        for oc4 in range(4):
            i = tt * 4 + oc4
            bank = banks4[i % 4]
            if i >= 4:
                p.wait("tensor", "ve", vals[f"yev{i-4}"])
            for j in range(16):
                p.inc("tensor", "pe", 1,
                      lambda e, j=j, tt=tt, oc4=oc4, bank=bank: e.matmul(
                          bank,
                          xq_fm[:, j * 1024 + tt * 128: j * 1024 + (tt + 1) * 128],
                          wf_sb[:, j * D + oc4 * 512: j * D + (oc4 + 1) * 512],
                          start=(j == 0), stop=(j == 15)))
            vals[f"ympe{i}"] = p.val("pe")

    def emit_ffn_evac(tt):
        for oc4 in range(4):
            i = tt * 4 + oc4
            bank = banks4[i % 4]
            p.wait("vector", "pe", vals[f"ympe{i}"])
            if i >= 4:
                p.wait("vector", f"sgy{(i - 4) % 4}", vals[f"ydma{i-4}"])
            p.inc("vector", "ve", 1,
                  lambda e, tt=tt, oc4=oc4, bank=bank: e.scalar_tensor_tensor(
                      ysb[:, oc4 * 512:(oc4 + 1) * 512], bank, wssc_bc[:, 0:1],
                      x1[:, tt * D + oc4 * 512: tt * D + (oc4 + 1) * 512],
                      ALU.mult, ALU.add))
            vals[f"yev{i}"] = p.val("ve")
            p.wait("scalar", "ve", vals[f"yev{i}"])
            p.dma("scalar", f"sgy{i % 4}", lambda e, tt=tt, oc4=oc4: e.dma_start(
                out=yout[tt * 128:(tt + 1) * 128, oc4 * 512:(oc4 + 1) * 512],
                in_=ysb[:, oc4 * 512:(oc4 + 1) * 512]))
            vals[f"ydma{i}"] = p.val(f"sgy{i % 4}")

    prev_se_r1d = 0
    vals["trD-1"] = 0
    for t in range(8):
        p.wait("vector", "ve", vals[f"resid{t}_3"])
        if t >= 1:
            p.wait("vector", "se", prev_se_r1d)
        p.wait("vector", "pe", vals[f"trD{t-1}"])
        p.wait("scalar", "ve", vals[f"resid{t}_3"])
        if t >= 1:
            p.wait("scalar", "ve", vals[f"ln2ve{t-1}"])
        p.inc("scalar", "se", 1, lambda e, t=t: e.activation(
            sqtsd[:, :], x1[:, t * D:(t + 1) * D], AF.Square))
        se_sq = p.val("se")
        p.wait("vector", "se", se_sq)
        p.inc("vector", "ve", 1, lambda e, t=t: e.reduce_sum(
            mv[:, 0:1], x1[:, t * D:(t + 1) * D], axis=AX.X))
        p.inc("vector", "ve", 1, lambda e: e.reduce_sum(
            mv[:, 1:2], sqtsd[:, :], axis=AX.X))
        vals[f"ln2ve{t}"] = p.val("ve")
        p.inc("vector", "ve", 1, lambda e: e.tensor_scalar(
            stats[:, 0:1], mv[:, 0:1], 1.0 / D, None, ALU.mult))
        p.inc("vector", "ve", 1, lambda e: e.scalar_tensor_tensor(
            stats[:, 1:2], mv[:, 0:1], -1.0 / (D * D), mv[:, 0:1],
            ALU.mult, ALU.mult))
        ve_stats = p.val("ve")
        p.wait("scalar", "ve", ve_stats)
        p.inc("scalar", "se", 1, lambda e: e.activation(
            stats[:, 3:4], mv[:, 1:2], AF.Identity, bias=stats[:, 1:2],
            scale=1.0 / D))
        p.inc("scalar", "se", 1, lambda e: e.activation(
            r1a[:, :], stats[:, 3:4], AF.Ln, bias=epsv[:, :], scale=1.0))
        p.inc("scalar", "se", 1, lambda e: e.activation(
            r1[:, :], r1a[:, :], AF.Exp, bias=0.0, scale=-0.5))
        prev_se_r1d = p.val("se")
        p.wait("vector", "se", prev_se_r1d)
        p.inc("vector", "ve", 1, lambda e: e.scalar_tensor_tensor(
            b2[:, :], stats[:, 0:1], -1.0, r1[:, :], ALU.mult, ALU.mult))
        ve_b2 = p.val("ve")
        p.wait("scalar", "ve", ve_b2)
        p.inc("scalar", "se", 1, lambda e, t=t: e.activation(
            xh2[:, :], x1[:, t * D:(t + 1) * D], AF.Identity,
            bias=b2[:, :], scale=r1[:, :]))
        se_xh2 = p.val("se")
        p.wait("vector", "se", se_xh2)
        if t == 0:
            p.wait("vector", "se", se_bcast)
        p.inc("vector", "ve", 1, lambda e: e.tensor_scalar(
            tq[:, :], xh2[:, :], rinv_bc[:, 0:1], None, ALU.mult))
        p.inc("vector", "ve", 1, lambda e: e.tensor_scalar(
            q8[:, :], tq[:, :], 448.0, -448.0, ALU.min, ALU.max))
        ve_q8 = p.val("ve")
        p.wait("tensor", "ve", ve_q8)
        if t == 0:
            p.wait("tensor", "wfl", wf_done)
        p.wait("tensor", "se", vals.get(f"evP2_{t-1}", 0))
        for jj in range(8):
            p.inc("tensor", "pe", 1,
                  lambda e, jj=jj: e.matmul(
                      P2[:, jj * 128:(jj + 1) * 128],
                      q8[:, jj * 128:(jj + 1) * 128], id_sb[:, :],
                      start=True, stop=True))
        pe_tr0 = p.val("pe")
        p.wait("tensor", "se", vals.get(f"evL_{t-1}", 0))
        for jj in range(8):
            j = 8 + jj
            lt = L0 if jj < 4 else L1
            p.inc("tensor", "pe", 1,
                  lambda e, j=j, jj=jj, lt=lt: e.matmul(
                      lt[:, (jj % 4) * 128:(jj % 4 + 1) * 128],
                      q8[:, j * 128:(j + 1) * 128], id_sb[:, :],
                      start=True, stop=True))
        pe_tr1 = p.val("pe")
        p.wait("scalar", "pe", pe_tr0)
        p.inc("scalar", "se", 1,
              lambda e, t=t: e.copy(
                  xq_fm[:, :].rearrange("p (j c) -> p j c", j=16)
                  [:, 0:8, t * 128:(t + 1) * 128],
                  P2[:, :].rearrange("p (j c) -> p j c", j=8)))
        vals[f"evP2_{t}"] = p.val("se")
        p.wait("scalar", "pe", pe_tr1)
        p.inc("scalar", "se", 1,
              lambda e, t=t: e.copy(
                  xq_fm[:, :].rearrange("p (j c) -> p j c", j=16)
                  [:, 8:12, t * 128:(t + 1) * 128],
                  L0[:, :].rearrange("p (j c) -> p j c", j=4)))
        p.inc("scalar", "se", 1,
              lambda e, t=t: e.copy(
                  xq_fm[:, :].rearrange("p (j c) -> p j c", j=16)
                  [:, 12:16, t * 128:(t + 1) * 128],
                  L1[:, :].rearrange("p (j c) -> p j c", j=4)))
        vals[f"evL_{t}"] = p.val("se")
        vals[f"q8done{t}"] = p.val("se")
        vals[f"trD{t}"] = p.val("pe")
        if t >= 2:
            emit_ffn_evac(t - 2)
        if t >= 1:
            emit_ffn_pe(t - 1)
    emit_ffn_evac(6)
    emit_ffn_pe(7)
    emit_ffn_evac(7)

    for k_ in range(4):
        p.wait("scalar", f"sgy{k_}")
    p.wait("scalar", "st")

    with nc.Block() as block:
        @block.sync
        def _(e):
            for f in p.ops["sync"]:
                f(e)

        @block.gpsimd
        def _(e):
            for f in p.ops["gpsimd"]:
                f(e)

        @block.vector
        def _(e):
            for f in p.ops["vector"]:
                f(e)

        @block.scalar
        def _(e):
            for f in p.ops["scalar"]:
                f(e)

        @block.tensor
        def _(e):
            for f in p.ops["tensor"]:
                f(e)

    return nc


def emit_ov(p, pend, P2, L0, L1, vbh, onc, pt_sb, exp_hist, ov_hist):
    kt, w0, si, g, is_start, is_stop, ve_need = pend
    ptb = pt_sb[si]
    p.wait("tensor", "se", exp_hist[g])
    if ve_need:
        p.wait("tensor", "ve", ve_need)
    for a in ((w0, 512), (512, 1024)):
        a0, b0 = max(a[0], w0), a[1]
        if b0 <= w0 or b0 <= a0:
            continue
        p.inc("tensor", "pe", 1,
              lambda e, kt=kt, a0=a0, b0=b0, ptb=ptb, st_=is_start,
              sp_=is_stop: e.matmul(
                  P2[:, a0:b0], vbh[:, kt * 128:(kt + 1) * 128],
                  ptb[:, a0:b0], start=st_, stop=sp_))
    ov_hist[g] = p.val("pe")


_NC_CACHE = []


def kernel(L0_attn_X, L0_attn_Wq, L0_attn_bq, L0_attn_Wk, L0_attn_bk,
           L0_attn_Wv, L0_attn_bv, L0_attn_Wo, L0_attn_bo, L0_ffn_weight,
           L0_ffn_weight_scale, L0_ffn_input_scale):
    X = np.ascontiguousarray(np.asarray(L0_attn_X, dtype=np.float32)
                             .reshape(B * S, D))
    wqTh = np.ascontiguousarray(np.asarray(L0_attn_Wq, dtype=np.float32).T)
    wkTh = np.ascontiguousarray(np.asarray(L0_attn_Wk, dtype=np.float32).T)
    wvTh = np.ascontiguousarray(np.asarray(L0_attn_Wv, dtype=np.float32).T)
    woTh = np.ascontiguousarray(np.asarray(L0_attn_Wo, dtype=np.float32).T)
    wf = np.asarray(L0_ffn_weight)
    wfTh = np.ascontiguousarray(wf.view(ml_dtypes.float8_e4m3).T)
    bqh = np.ascontiguousarray(
        np.asarray(L0_attn_bq, np.float32).reshape(16, 128).T)
    bkh = np.ascontiguousarray(
        np.asarray(L0_attn_bk, np.float32).reshape(16, 128).T)
    bvh = np.ascontiguousarray(np.asarray(L0_attn_bv, np.float32).reshape(1, D))
    boh = np.ascontiguousarray(np.asarray(L0_attn_bo, np.float32).reshape(1, D))
    wsch = np.asarray(L0_ffn_weight_scale, np.float32).reshape(1, 1)
    isch = np.asarray(L0_ffn_input_scale).reshape(1, 1)
    if isch.dtype != ml_dtypes.bfloat16:
        isch = isch.view(ml_dtypes.bfloat16) if isch.dtype.itemsize == 2 \
            else isch.astype(ml_dtypes.bfloat16)

    ident = np.eye(128, dtype=ml_dtypes.bfloat16)
    onc_h = np.ones((128, 1), dtype=ml_dtypes.bfloat16)
    onr_h = np.ones((1, 128), dtype=ml_dtypes.bfloat16)
    onrf_h = np.ones((1, 128), dtype=np.float32)
    kk, qq = np.meshgrid(np.arange(128), np.arange(128), indexing="ij")
    maskd_h = np.where(kk <= qq, 0.0, NEG).astype(np.float32)
    mexp_h = np.exp(maskd_h).astype(ml_dtypes.bfloat16)

    in_maps = []
    for c in range(NC):
        idxt_h = np.zeros((128, 64), dtype=np.int32)
        pp = np.arange(128, dtype=np.int32)
        for u in range(4):
            g = 4 * c + u
            h = g % 16
            for s_ in range(4):
                for r_ in range(3):
                    idxt_h[:, u * 12 + s_ * 3 + r_] = \
                        (4 * (c // 4) + s_) * 2048 + h * 128 + pp
        for h in range(16):
            orow = 1024 * (h % 4) + (4 * (c // 4) + h // 4) * 128 + pp
            idxt_h[:, 48 + h] = orow * 4 + (c % 4)
        in_maps.append({
            "xin": X[c * T:(c + 1) * T], "wqT": wqTh, "wkT": wkTh,
            "wvT": wvTh, "woT": woTh, "wfT": wfTh, "bqi": bqh, "bki": bkh,
            "bvi": bvh, "boi": boh, "wsc": wsch, "isc": isch,
            "id_in": ident, "onc_in": onc_h, "onr_in": onr_h, "onrf_in": onrf_h,
            "mask_in": maskd_h, "mexp_in": mexp_h, "idxt": idxt_h,
        })

    if not _NC_CACHE:
        nc = bass.Bass("TRN2", target_bir_lowering=False)
        build(nc)
        _NC_CACHE.append(nc)
    nc = _NC_CACHE[0]
    res = run_bass_kernel_spmd(nc, in_maps, core_ids=list(range(NC)))
    out = np.concatenate([res.results[c]["yout"] for c in range(NC)], axis=0)
    return out.reshape(B, S, D)
